# version 11
# baseline (speedup 1.0000x reference)
import contextlib
import struct
import numpy as np
import ml_dtypes

import concourse.bass as bass
import concourse.mybir as mybir
from concourse.bass_utils import run_bass_kernel_spmd

F32 = mybir.dt.float32
F16 = mybir.dt.float16
I32 = mybir.dt.int32
AF = mybir.ActivationFunctionType
OP = mybir.AluOpType

N_CORES = 8
B, TT, C = 8, 8192, 1024
TOK = B * TT // N_CORES
T = 512
NT = TOK // T
NC = C // 128
FF = 4096
NF = FF // 128
EPS = 1e-5
BIAS = 8192.0
GC = 768

_YBITS = [0xb904c000, 0xb9adbb00, 0xba556b00, 0xbaf5ce80, 0xbb84b340, 0xbc064180,
          0xbc7e5920, 0xbce15244, 0xbd3a5e80, 0xbd8f926e, 0xbdcd3b48, 0xbe073b54,
          0xbe227686, 0xbe2e0cbe, 0xbe1df8a2, 0xbdcd765d, 0x00000000, 0x3e1944d2,
          0x3eb103af, 0x3f147cd0, 0x3f57625e, 0x3f8f1895, 0x3fb32c4c, 0x3fd706d9,
          0x3ffa2d0c, 0x400e3d5c, 0x401f01a7, 0x402f79be, 0x403fbda6, 0x404fe146,
          0x405ff2aa, 0x406ffa93, 0x407ffded]
_Y = np.array([struct.unpack('<f', struct.pack('<I', b))[0] for b in _YBITS],
              dtype=np.float32)
_X = np.linspace(-4, 4, 33).astype(np.float32)


def _gelu_consts():
    E = _Y.astype(np.float64)[16:] - _X.astype(np.float64)[16:] / 2.0
    K1 = np.diff(E)
    lam0 = np.float32(K1[0])
    e = np.diff(np.concatenate([K1, [0.125]])).astype(np.float32)
    const_off = np.float32(np.sum(e.astype(np.float64) * np.arange(1, 17)))
    return lam0, e, const_off


LAM0, ECOEF, CONST_OFF = _gelu_consts()
PAIR_RHO = [np.float32(np.float64(ECOEF[2 * j]) / np.float64(ECOEF[2 * j + 1]))
            for j in range(8)]
PAIR_EB = [ECOEF[2 * j + 1] for j in range(8)]


def _verify_gelu_chain():
    def ref(xq):
        y = np.interp(xq, _X.astype(np.float64), _Y.astype(np.float64))
        return np.where(xq > 4.0, xq, np.where(xq < -4.0, 0.0, y))

    A0 = np.arange(-16384, 16385).astype(np.float64) + BIAS
    s = np.abs(A0 * np.float64(1.0 / 64.0) + np.float64(-128.0))
    acc = (A0 - BIAS) * np.float64(1.0 / 512.0)
    acc = np.float64(LAM0) * s + acc
    acc = acc - np.float64(CONST_OFF)
    for j in range(8):
        rho, eb = PAIR_RHO[j], PAIR_EB[j]
        u = np.maximum(s, np.float64(2 * j + 1)) * np.float64(rho)
        u = np.maximum(s, np.float64(2 * j + 2)) + u
        acc = u * np.float64(eb) + acc
    w2 = acc * 256.0 + BIAS
    fr = w2 - np.floor(w2)
    assert np.abs(fr - 0.5).min() > 1e-4, "site-2 rounding tie on lattice"
    for conv in (np.trunc, np.round):
        for jit in (0.0, 0.9, -0.9):
            z2 = conv(w2 + jit)
            A2 = z2.copy()
            for _ in range(2):
                d2 = w2 - A2
                A2 = (A2 + (d2 > 0.5)) - (d2 < -0.5)
            q2 = A2 - BIAS
            y2_ref = np.round(256.0 * ref((A0 - BIAS) / 256.0))
            assert np.array_equal(q2, y2_ref), "gelu chain verification failed"


_verify_gelu_chain()



def build_nc():
    nc = bass.Bass("TRN2", target_bir_lowering=False)

    xin = nc.dram_tensor("x", [C, TOK], F32, kind="ExternalInput").ap()
    wkt = nc.dram_tensor("wkt", [256, FF], F32, kind="ExternalInput").ap()
    wv = nc.dram_tensor("wv", [FF, 256], F32, kind="ExternalInput").ap()
    w1 = nc.dram_tensor("w1", [C, FF], F32, kind="ExternalInput").ap()
    w2 = nc.dram_tensor("w2", [FF, C], F32, kind="ExternalInput").ap()
    lng_in = nc.dram_tensor("lng", [128, NC], F32, kind="ExternalInput").ap()
    lnb_in = nc.dram_tensor("lnb", [128, NC], F32, kind="ExternalInput").ap()
    b1_in = nc.dram_tensor("b1t", [128, NF], F32, kind="ExternalInput").ap()
    b2_in = nc.dram_tensor("b2t", [128, NC], F32, kind="ExternalInput").ap()
    yout = nc.dram_tensor("y", [C, TOK], F32, kind="ExternalOutput").ap()

    w1h = nc.dram_tensor("w1h", [C, FF], F16, kind="Internal").ap()
    w2h = nc.dram_tensor("w2h", [FF, C], F16, kind="Internal").ap()

    es = contextlib.ExitStack()
    sb = lambda name, shape, dt: es.enter_context(nc.sbuf_tensor(name, shape, dt))
    ps = lambda name, shape: es.enter_context(nc.psum_tensor(name, shape, F32))
    sem = lambda name: es.enter_context(nc.semaphore(name))

    with es:
        X0 = sb("X0", [128, NC * T], F32)
        X1 = sb("X1", [128, NC * T], F32)
        TA = sb("TA", [128, NC * T], F32)
        U = sb("U", [128, NC * T], F16)
        KTA = sb("KTA", [128, FF], F16)
        KTB = sb("KTB", [128, FF], F16)
        VV = sb("VV", [128, FF * 2], F16)
        W1B = sb("W1B", [128, 8192], F16)
        W2B = sb("W2B", [128, 8192], F16)
        RB = sb("RB", [128, 4096], F16)
        PB = sb("PB", [128, 4096], F16)
        HB = sb("HB", [128, NF * T], F16)
        WQ = sb("WQ", [128, 1024], F32)
        WQ2 = sb("WQ2", [128, 1024], F32)
        GZ = sb("GZ", [128, 1024], I32)
        GD = sb("GD", [128, 1024], F32)
        GS = sb("GS", [128, 1024], F32)
        GA = sb("GA", [128, 1024], F32)
        GB = sb("GB", [128, 1024], F32)
        CB1 = sb("CB1", [128, T], F32)
        VB = sb("VB", [128, T], F32)
        SQ = sb("SQ", [128, T], F32)
        AB = sb("AB", [128, T], F32)
        BB = sb("BB", [128, T], F32)
        LNG = sb("LNG", [128, NC], F32)
        LNB = sb("LNB", [128, NC], F32)
        B1T = sb("B1T", [128, NF], F32)
        B2T = sb("B2T", [128, NC], F32)
        B1Q = sb("B1Q", [128, NF], F32)
        B2Q = sb("B2Q", [128, NC], F32)
        ONES = sb("ONES", [128, 128], F16)
        CEPS = sb("CEPS", [128, 1], F32)

        psS1 = ps("psS1", [128, T])
        psS2 = ps("psS2", [128, T])
        psC0 = ps("psC0", [128, T])
        psC1 = ps("psC1", [128, T])
        psO0 = ps("psO0", [128, T])
        psO1 = ps("psO1", [128, T])
        psH0 = ps("psH0", [128, T])
        psH1 = ps("psH1", [128, T])

        s_x = sem("s_x")
        s_w1 = sem("s_w1")
        s_w2 = sem("s_w2")
        s_gp = sem("s_gp")
        s_y = sem("s_y")
        s_mm = sem("s_mm")
        s_sc = sem("s_sc")
        s_vd = sem("s_vd")

        cnt = {"s_x": 0, "s_y": 0, "s_mm": 0, "s_sc": 0, "s_vd": 0,
               "s_w1": 0, "s_w2": 0, "s_gp": 0}
        hmap = {"s_x": s_x, "s_y": s_y, "s_mm": s_mm, "s_sc": s_sc, "s_vd": s_vd,
                "s_w1": s_w1, "s_w2": s_w2, "s_gp": s_gp}

        blk = es.enter_context(nc.Block())

        def sig(inst, name, k=1):
            cnt[name] += k
            inst.then_inc(hmap[name], k)

        def waits(eng, names):
            for n in names:
                eng.wait_ge(hmap[n], cnt[n])

        X = [X0, X1]
        xlast = {"w1_user": 0, "w2_user": 0}

        def vc(vector, emit):
            vector.wait_ge(s_vd, cnt["s_vd"])
            sig(emit(), "s_vd")

        def sc_op(scalar, emit):
            scalar.wait_ge(s_sc, cnt["s_sc"])
            sig(emit(), "s_sc")

        def gc(gp, emit):
            gp.wait_ge(s_gp, cnt["s_gp"])
            sig(emit(), "s_gp")

        def SY(f):
            blk.sync(f)

        def TE(f):
            blk.tensor(f)

        def VE(f):
            def g(vector):
                vector.wait_ge(s_vd, cnt["s_vd"])
                f(vector)
            blk.vector(g)

        def GP(f):
            def g(gp):
                gp.wait_ge(s_gp, cnt["s_gp"])
                f(gp)
            blk.gpsimd(g)

        def SC(f):
            def g(scalar):
                scalar.wait_ge(s_sc, cnt["s_sc"])
                f(scalar)
            blk.scalar(g)

        def startup_sync(sync):
            sig(sync.dma_start(out=LNG[:], in_=lng_in[:, :]), "s_x", 16)
            sig(sync.dma_start(out=LNB[:], in_=lnb_in[:, :]), "s_x", 16)
            sig(sync.dma_start(out=B1T[:], in_=b1_in[:, :]), "s_x", 16)
            sig(sync.dma_start(out=B2T[:], in_=b2_in[:, :]), "s_x", 16)
        SY(startup_sync)

        def startup_vec(vector):
            waits(vector, ["s_x"])
            sig(vector.memset(ONES[:], 1.0 / 1024.0), "s_vd")
            sig(vector.memset(CEPS[:], EPS), "s_vd")
            sig(vector.tensor_scalar(B1Q[:], B1T[:], 256.0, BIAS, OP.mult, OP.add), "s_vd")
            sig(vector.tensor_scalar(B2Q[:], B2T[:], 256.0, BIAS, OP.mult, OP.add), "s_vd")
        VE(startup_vec)

        def conv_round(src_ap, dst_sbuf_ap=None, dst_dram_ap=None):
            def ld(sync):
                waits(sync, ["s_vd"])
                sig(sync.dma_start(out=X0[:, 0:src_cols], in_=src_ap), "s_x", 16)
            def cv(vector):
                waits(vector, ["s_x"])
                if dst_sbuf_ap is not None:
                    sig(vector.tensor_copy(dst_sbuf_ap, X0[:, 0:src_cols]), "s_vd")
                else:
                    sig(vector.tensor_copy(RB[:, 0:src_cols], X0[:, 0:src_cols]), "s_vd")
            SY(ld)
            VE(cv)
            if dst_dram_ap is not None:
                def st(sync):
                    waits(sync, ["s_vd"])
                    sig(sync.dma_start(out=dst_dram_ap, in_=RB[:, 0:src_cols]), "s_x", 16)
                SY(st)

        src_cols = 4096
        conv_round(wkt[0:128, :], dst_sbuf_ap=KTA[:])
        conv_round(wkt[128:256, :], dst_sbuf_ap=KTB[:])
        wv_r = wv.rearrange("(j p) d -> p j d", p=128)
        conv_round(wv_r[:, 0:16, :], dst_sbuf_ap=VV[:, 0:4096])
        conv_round(wv_r[:, 16:32, :], dst_sbuf_ap=VV[:, 4096:8192])
        w1h_r = w1h
        for c in range(NC):
            conv_round(w1[128 * c:128 * (c + 1), :],
                       dst_dram_ap=w1h_r[128 * c:128 * (c + 1), :])
        w2_r = w2.rearrange("(j p) d -> p j d", p=128)
        w2h_r = w2h.rearrange("(j p) d -> p j d", p=128)
        for r in range(8):
            conv_round(w2_r[:, 4 * r:4 * (r + 1), :],
                       dst_dram_ap=w2h_r[:, 4 * r:4 * (r + 1), :])


        def g_round2(g, W, Z, D, S, A, sl):
            gc(g, lambda: g.tensor_copy(Z[:, sl], W[:, sl]))
            gc(g, lambda: g.tensor_copy(A[:, sl], Z[:, sl]))
            gc(g, lambda: g.tensor_tensor(D[:, sl], W[:, sl], A[:, sl], OP.subtract))
            gc(g, lambda: g.tensor_scalar(S[:, sl], D[:, sl], 0.5, None, OP.is_gt))
            gc(g, lambda: g.tensor_tensor(S[:, sl], S[:, sl], A[:, sl], OP.add))
            gc(g, lambda: g.tensor_scalar(A[:, sl], D[:, sl], -0.5, None, OP.is_lt))
            gc(g, lambda: g.tensor_tensor(D[:, sl], S[:, sl], A[:, sl], OP.subtract))
            gc(g, lambda: g.tensor_tensor(S[:, sl], W[:, sl], D[:, sl], OP.subtract))
            gc(g, lambda: g.tensor_scalar(A[:, sl], S[:, sl], 0.5, None, OP.is_gt))
            gc(g, lambda: g.tensor_tensor(A[:, sl], A[:, sl], D[:, sl], OP.add))
            gc(g, lambda: g.tensor_scalar(S[:, sl], S[:, sl], -0.5, None, OP.is_lt))
            gc(g, lambda: g.tensor_tensor(A[:, sl], A[:, sl], S[:, sl], OP.subtract))

        def g_floor2(g, Bv, Z, D, S, F, sl):
            gc(g, lambda: g.tensor_copy(Z[:, sl], Bv[:, sl]))
            gc(g, lambda: g.tensor_copy(F[:, sl], Z[:, sl]))
            for it in range(2):
                gc(g, lambda: g.tensor_tensor(D[:, sl], Bv[:, sl], F[:, sl], OP.subtract))
                gc(g, lambda: g.tensor_scalar(S[:, sl], D[:, sl], 1.0, None, OP.is_ge))
                gc(g, lambda: g.tensor_tensor(S[:, sl], S[:, sl], F[:, sl], OP.add))
                gc(g, lambda: g.tensor_scalar(D[:, sl], D[:, sl], 0.0, None, OP.is_lt))
                gc(g, lambda: g.tensor_tensor(F[:, sl], S[:, sl], D[:, sl], OP.subtract))

        def g_rhe(g, W, Z, D, S, A, Bv, sl):
            gc(g, lambda: g.tensor_scalar(Bv[:, sl], W[:, sl], 0.5, None, OP.mult))
            gc(g, lambda: g.tensor_scalar(Bv[:, sl], Bv[:, sl], 0.25, None, OP.add))
            g_floor2(g, Bv, Z, D, S, A, sl)
            gc(g, lambda: g.tensor_scalar(Bv[:, sl], W[:, sl], -0.5, None, OP.mult))
            gc(g, lambda: g.tensor_scalar(Bv[:, sl], Bv[:, sl], 0.25, None, OP.add))
            g_floor2(g, Bv, Z, D, S, W, sl)
            gc(g, lambda: g.tensor_tensor(A[:, sl], A[:, sl], W[:, sl], OP.subtract))

        def g_gelu(g, Wb, sl, cols_hb):
            g_round2(g, Wb, GZ, GD, GS, GA, sl)
            gc(g, lambda: g.tensor_scalar(GS[:, sl], GA[:, sl], BIAS, None, OP.subtract))
            gc(g, lambda: g.tensor_scalar(GS[:, sl], GS[:, sl], 1.0 / 64.0, None, OP.mult))
            gc(g, lambda: g.tensor_scalar(GD[:, sl], GS[:, sl], -1.0, None, OP.mult))
            gc(g, lambda: g.tensor_scalar(GD[:, sl], GD[:, sl], 0.0, None, OP.max))
            gc(g, lambda: g.tensor_scalar(GS[:, sl], GS[:, sl], 0.0, None, OP.max))
            gc(g, lambda: g.tensor_tensor(GS[:, sl], GS[:, sl], GD[:, sl], OP.add))
            gc(g, lambda: g.tensor_scalar(GD[:, sl], GA[:, sl], BIAS, None, OP.subtract))
            gc(g, lambda: g.tensor_scalar(GD[:, sl], GD[:, sl], 1.0 / 512.0, None, OP.mult))
            gc(g, lambda: g.tensor_scalar(GB[:, sl], GS[:, sl], float(LAM0), None, OP.mult))
            gc(g, lambda: g.tensor_tensor(GB[:, sl], GB[:, sl], GD[:, sl], OP.add))
            gc(g, lambda: g.tensor_scalar(GB[:, sl], GB[:, sl], float(CONST_OFF), None, OP.subtract))
            for j in range(8):
                pa = float(2 * j + 1)
                gc(g, lambda pa=pa, j=j: g.tensor_scalar(GD[:, sl], GS[:, sl], pa, None, OP.max))
                gc(g, lambda j=j: g.tensor_scalar(GD[:, sl], GD[:, sl], float(PAIR_RHO[j]), None, OP.mult))
                gc(g, lambda pa=pa: g.tensor_scalar(GA[:, sl], GS[:, sl], pa + 1.0, None, OP.max))
                gc(g, lambda: g.tensor_tensor(GD[:, sl], GA[:, sl], GD[:, sl], OP.add))
                gc(g, lambda j=j: g.tensor_scalar(GD[:, sl], GD[:, sl], float(PAIR_EB[j]), None, OP.mult))
                gc(g, lambda: g.tensor_tensor(GB[:, sl], GB[:, sl], GD[:, sl], OP.add))
            gc(g, lambda: g.tensor_scalar(GB[:, sl], GB[:, sl], 256.0, None, OP.mult))
            gc(g, lambda: g.tensor_scalar(GB[:, sl], GB[:, sl], BIAS, None, OP.add))
            gc(g, lambda: g.tensor_copy(Wb[:, sl], GB[:, sl]))
            g_round2(g, Wb, GZ, GD, GS, GA, sl)
            gc(g, lambda: g.tensor_scalar(GA[:, sl], GA[:, sl], BIAS, None, OP.subtract))
            gc(g, lambda: g.tensor_scalar(GA[:, sl], GA[:, sl], 1.0 / 256.0, None, OP.mult))
            gc(g, lambda: g.tensor_copy(HB[:, cols_hb], GA[:, sl]))

        def floor2(v, Bv, Z, D, S, F, sl):
            vc(v, lambda: v.tensor_copy(Z[:, sl], Bv[:, sl]))
            for _ in range(2):
                si = Z if _ == 0 else F
                vc(v, lambda si=si: v.tensor_tensor(D[:, sl], Bv[:, sl], si[:, sl], OP.subtract))
                vc(v, lambda si=si: v.scalar_tensor_tensor(S[:, sl], D[:, sl], 1.0, si[:, sl], OP.is_ge, OP.add))
                vc(v, lambda: v.tensor_scalar(D[:, sl], D[:, sl], 0.0, None, OP.is_lt))
                vc(v, lambda: v.tensor_tensor(F[:, sl], S[:, sl], D[:, sl], OP.subtract))

        def rhe(v, W, Z, D, S, A, Bv, sl):
            vc(v, lambda: v.tensor_scalar(Bv[:, sl], W[:, sl], 0.5, 0.25, OP.mult, OP.add))
            floor2(v, Bv, Z, D, S, A, sl)
            vc(v, lambda: v.tensor_scalar(Bv[:, sl], W[:, sl], -0.5, 0.25, OP.mult, OP.add))
            floor2(v, Bv, Z, D, S, W, sl)
            vc(v, lambda: v.tensor_tensor(A[:, sl], A[:, sl], W[:, sl], OP.subtract))

        def round2(v, W, Z, D, S, A, sl):
            vc(v, lambda: v.tensor_copy(Z[:, sl], W[:, sl]))
            vc(v, lambda: v.tensor_tensor(D[:, sl], W[:, sl], Z[:, sl], OP.subtract))
            vc(v, lambda: v.scalar_tensor_tensor(A[:, sl], D[:, sl], 0.5, Z[:, sl], OP.is_gt, OP.add))
            vc(v, lambda: v.tensor_scalar(S[:, sl], D[:, sl], -0.5, None, OP.is_lt))
            vc(v, lambda: v.tensor_tensor(D[:, sl], A[:, sl], S[:, sl], OP.subtract))
            vc(v, lambda: v.tensor_tensor(A[:, sl], W[:, sl], D[:, sl], OP.subtract))
            vc(v, lambda: v.scalar_tensor_tensor(S[:, sl], A[:, sl], 0.5, D[:, sl], OP.is_gt, OP.add))
            vc(v, lambda: v.tensor_scalar(W[:, sl], A[:, sl], -0.5, None, OP.is_lt))
            vc(v, lambda: v.tensor_tensor(A[:, sl], S[:, sl], W[:, sl], OP.subtract))

        def load_x(i):
            Xb = X[i % 2]
            def f(sync):
                waits(sync, ["s_vd", "s_y"])
                for c in range(NC):
                    inst = sync.dma_start(
                        out=Xb[:, c * T:(c + 1) * T],
                        in_=xin[128 * c:128 * (c + 1), T * i:T * (i + 1)])
                    sig(inst, "s_x", 16)
            SY(f)

        def stats(src, is_ln01):
            def cast(scalar):
                waits(scalar, ["s_vd", "s_mm", "s_x", "s_gp"])
                sig(scalar.activation(RB[:], src[:], AF.Identity, bias=0.0, scale=1.0), "s_sc")
                sig(scalar.activation(PB[:], src[:], AF.Square, bias=0.0, scale=1.0), "s_sc")
            SC(cast)

            def mm(tensor):
                waits(tensor, ["s_sc"])
                for c in range(NC):
                    tensor.matmul(psS1[:], ONES[:], RB[:, c * T:(c + 1) * T],
                                  start=(c == 0), stop=(c == NC - 1))
                for c in range(NC):
                    inst = tensor.matmul(psS2[:], ONES[:], PB[:, c * T:(c + 1) * T],
                                         start=(c == 0), stop=(c == NC - 1))
                sig(inst, "s_mm")
            TE(mm)

            def chain(vector):
                waits(vector, ["s_mm"])
                vc(vector, lambda: vector.tensor_copy(CB1[:], psS1[:]))
                vc(vector, lambda: vector.tensor_mul(VB[:], CB1[:], CB1[:]))
                vc(vector, lambda: vector.tensor_tensor(VB[:], psS2[:], VB[:], OP.subtract))
            VE(chain)

            def chain_sc(scalar):
                waits(scalar, ["s_vd"])
                inst = scalar.activation(SQ[:], VB[:], AF.Sqrt, bias=CEPS[:], scale=1.0)
                sig(inst, "s_sc")
            SC(chain_sc)

            def recip1(vector):
                waits(vector, ["s_sc"])
                vc(vector, lambda: vector.reciprocal(BB[:], SQ[:]))
            VE(recip1)

            if is_ln01:
                def chain2(vector):
                    vc(vector, lambda: vector.tensor_scalar(SQ[:], VB[:], 1.0 + EPS, EPS * EPS,
                                                            OP.mult, OP.add))
                VE(chain2)

                def chain2_sc(scalar):
                    waits(scalar, ["s_vd"])
                    inst = scalar.activation(CB1[:], SQ[:], AF.Sqrt, bias=0.0, scale=1.0)
                    sig(inst, "s_sc")
                SC(chain2_sc)

                def recip2(vector):
                    waits(vector, ["s_sc"])
                    vc(vector, lambda: vector.reciprocal(AB[:], CB1[:]))
                VE(recip2)

        load_x(0)
        for i in range(NT):
            Xb = X[i % 2]

            stats(Xb, is_ln01=True)

            def apply01(vector, Xb=Xb):
                v = vector
                for c in range(NC):
                    sig(v.tensor_tensor(TA[:, c * T:(c + 1) * T],
                                        Xb[:, c * T:(c + 1) * T], psS1[:], OP.subtract), "s_vd")
                v.wait_ge(s_vd, cnt["s_vd"])
                for c in range(NC):
                    sig(v.tensor_mul(U[:, c * T:(c + 1) * T],
                                     TA[:, c * T:(c + 1) * T], AB[:]), "s_vd")
                v.wait_ge(s_vd, cnt["s_vd"])
                for c in range(NC):
                    sig(v.tensor_scalar(U[:, c * T:(c + 1) * T], U[:, c * T:(c + 1) * T],
                                        LNG[:, c:c + 1], LNB[:, c:c + 1],
                                        OP.mult, OP.add), "s_vd")
                v.wait_ge(s_vd, cnt["s_vd"])
                for c in range(NC):
                    sig(v.tensor_mul(TA[:, c * T:(c + 1) * T],
                                     TA[:, c * T:(c + 1) * T], BB[:]), "s_vd")
            VE(apply01)

            ps_sc = [psC0, psC1, psH0, psH1]
            for h in range(4):
                relu_cnt = []
                p_cnt = []
                out_cnt = []
                mm_cnt = []

                def scores_hg(tensor, h=h, hg=None, relu_cnt=relu_cnt):
                    if h == 0 and hg == 0:
                        waits(tensor, ["s_vd", "s_sc"])
                    for jj in range(4 * hg, 4 * hg + 4):
                        j = 32 * 0 + jj
                        pc = ps_sc[jj % 4]
                        if jj >= 4:
                            tensor.wait_ge(s_sc, relu_cnt[jj - 4])
                        tensor.matmul(pc[:], KTA[:, jj * 128:(jj + 1) * 128],
                                      U[:, (2 * h) * T:(2 * h + 1) * T],
                                      start=True, stop=False, skip_group_check=True)
                        inst = tensor.matmul(pc[:], KTB[:, jj * 128:(jj + 1) * 128],
                                             U[:, (2 * h + 1) * T:(2 * h + 2) * T],
                                             start=False, stop=True, skip_group_check=True)
                        sig(inst, "s_mm")
                        mm_cnt.append(cnt["s_mm"])

                def relu_hg(scalar, h=h, hg=None, mm_cnt=mm_cnt, p_cnt=p_cnt,
                            relu_cnt=relu_cnt):
                    if hg >= 2:
                        scalar.wait_ge(s_vd, p_cnt[hg - 2])
                    for jj in range(4 * hg, 4 * hg + 4):
                        pc = ps_sc[jj % 4]
                        scalar.wait_ge(s_mm, mm_cnt[jj])
                        col = (hg % 2) * 2048 + (jj % 4) * T
                        inst = scalar.activation(RB[:, col:col + T], pc[:],
                                                 AF.Relu, bias=0.0, scale=1.0)
                        sig(inst, "s_sc")
                        relu_cnt.append(cnt["s_sc"])

                def p_hg(vector, h=h, hg=None, relu_cnt=relu_cnt, out_cnt=out_cnt,
                         p_cnt=p_cnt):
                    vector.wait_ge(s_sc, relu_cnt[4 * hg + 3])
                    if hg >= 2:
                        vector.wait_ge(s_mm, out_cnt[hg - 2])
                    half = (hg % 2) * 2048
                    sig(vector.tensor_mul(PB[:, half:half + 2048],
                                          RB[:, half:half + 2048],
                                          RB[:, half:half + 2048]), "s_vd")
                    p_cnt.append(cnt["s_vd"])

                def out_hg(tensor, h=h, hg=None, p_cnt=p_cnt, out_cnt=out_cnt):
                    tensor.wait_ge(s_vd, p_cnt[hg])
                    half = (hg % 2) * 2048
                    for k in range(4):
                        jj = 4 * hg + k
                        tensor.matmul(psO0[:], VV[:, jj * 256:jj * 256 + 128],
                                      PB[:, half + k * T:half + (k + 1) * T],
                                      start=(jj == 0), stop=(jj == 31),
                                      skip_group_check=True)
                        inst = tensor.matmul(psO1[:], VV[:, jj * 256 + 128:jj * 256 + 256],
                                             PB[:, half + k * T:half + (k + 1) * T],
                                             start=(jj == 0), stop=(jj == 31),
                                             skip_group_check=True)
                    sig(inst, "s_mm")
                    out_cnt.append(cnt["s_mm"])

                for hg in range(8):
                    TE(lambda t, f=scores_hg, hg=hg: f(t, hg=hg))
                    SC(lambda s, f=relu_hg, hg=hg: f(s, hg=hg))
                    VE(lambda v, f=p_hg, hg=hg: f(v, hg=hg))
                    if hg >= 1:
                        TE(lambda t, f=out_hg, hg=hg - 1: f(t, hg=hg))
                TE(lambda t, f=out_hg: f(t, hg=7))

                def a_add(vector, h=h):
                    waits(vector, ["s_mm"])
                    sig(vector.tensor_tensor(TA[:, (2 * h) * T:(2 * h + 1) * T],
                                             TA[:, (2 * h) * T:(2 * h + 1) * T],
                                             psO0[:], OP.add), "s_vd")
                    sig(vector.tensor_tensor(TA[:, (2 * h + 1) * T:(2 * h + 2) * T],
                                             TA[:, (2 * h + 1) * T:(2 * h + 2) * T],
                                             psO1[:], OP.add), "s_vd")
                VE(a_add)

            stats(TA, is_ln01=False)

            def apply2(vector, Xb=Xb):
                v = vector
                for c in range(NC):
                    sig(v.tensor_tensor(TA[:, c * T:(c + 1) * T],
                                        TA[:, c * T:(c + 1) * T], psS1[:], OP.subtract), "s_vd")
                v.wait_ge(s_vd, cnt["s_vd"])
                for c in range(NC):
                    sig(v.tensor_mul(TA[:, c * T:(c + 1) * T],
                                     TA[:, c * T:(c + 1) * T], BB[:]), "s_vd")
                v.wait_ge(s_vd, cnt["s_vd"])
                for c in range(NC):
                    sig(v.tensor_add(Xb[:, c * T:(c + 1) * T],
                                     Xb[:, c * T:(c + 1) * T], TA[:, c * T:(c + 1) * T]), "s_vd")
            VE(apply2)

            stats(Xb, is_ln01=False)

            def apply3(vector, Xb=Xb):
                v = vector
                for c in range(NC):
                    sig(v.tensor_tensor(TA[:, c * T:(c + 1) * T],
                                        Xb[:, c * T:(c + 1) * T], psS1[:], OP.subtract), "s_vd")
                v.wait_ge(s_vd, cnt["s_vd"])
                for c in range(NC):
                    sig(v.tensor_mul(U[:, c * T:(c + 1) * T],
                                     TA[:, c * T:(c + 1) * T], BB[:]), "s_vd")
            VE(apply3)

            h_mm_cnt = {}
            h_ev_cnt = {}
            gelu_end = {}
            gelu_gp_end = {}
            w1_cnt = {}
            for q in range(4):
                def w1_load(sync, q=q):
                    if q >= 1:
                        sync.wait_ge(s_mm, h_mm_cnt[(q - 1, 3, 1)])
                    else:
                        sync.wait_ge(s_mm, xlast["w1_user"])
                    for c in range(NC):
                        inst = sync.dma_start(
                            out=W1B[:, c * 1024:(c + 1) * 1024],
                            in_=w1h[128 * c:128 * (c + 1), 1024 * q:1024 * (q + 1)])
                        sig(inst, "s_w1", 16)
                    w1_cnt[q] = cnt["s_w1"]
                SY(w1_load)

                for pair in range(4):
                    def h_mm(tensor, q=q, pair=pair):
                        if q == 0 and pair == 0:
                            waits(tensor, ["s_vd", "s_sc"])
                        tensor.wait_ge(s_w1, w1_cnt[q])
                        for k in range(2):
                            jf = 2 * pair + k
                            ph = [psH0, psH1][k]
                            prev = (q, pair - 1, k) if pair else (q - 1, 3, k)
                            if prev in h_ev_cnt:
                                tensor.wait_ge(s_sc, h_ev_cnt[prev])
                            for c in range(NC):
                                inst = tensor.matmul(
                                    ph[:], W1B[:, c * 1024 + jf * 128:c * 1024 + (jf + 1) * 128],
                                    U[:, c * T:(c + 1) * T],
                                    start=(c == 0), stop=(c == NC - 1))
                            sig(inst, "s_mm")
                            h_mm_cnt[(q, pair, k)] = cnt["s_mm"]
                            xlast["w1_user"] = cnt["s_mm"]
                    TE(h_mm)

                    def h_evac(scalar, q=q, pair=pair):
                        gp = 4 * q + pair
                        if gp >= 2:
                            pg = ((gp - 2) // 4, (gp - 2) % 4)
                            scalar.wait_ge(s_vd, gelu_end[pg])
                            scalar.wait_ge(s_gp, gelu_gp_end[pg])
                        Wb = [WQ, WQ2][gp % 2]
                        for k in range(2):
                            jf = 2 * pair + k
                            fg = 8 * q + jf
                            ph = [psH0, psH1][k]
                            scalar.wait_ge(s_mm, h_mm_cnt[(q, pair, k)])
                            inst = scalar.activation(Wb[:, k * T:(k + 1) * T], ph[:],
                                                     AF.Identity,
                                                     bias=B1Q[:, fg:fg + 1], scale=256.0)
                            sig(inst, "s_sc")
                            h_ev_cnt[(q, pair, k)] = cnt["s_sc"]
                    SC(h_evac)

                    def gelu(vector, q=q, pair=pair):
                        gp_ = 4 * q + pair
                        Wb = [WQ, WQ2][gp_ % 2]
                        vector.wait_ge(s_sc, h_ev_cnt[(q, pair, 1)])
                        fg0 = 8 * q + 2 * pair
                        sl = slice(0, GC)
                        v = vector
                        round2(v, Wb, GZ, GD, GS, GA, sl)
                        vc(v, lambda: v.tensor_scalar(GS[:, sl], GA[:, sl], BIAS, 1.0 / 64.0,
                                                      OP.subtract, OP.mult))
                        vc(v, lambda: v.scalar_tensor_tensor(GS[:, sl], GS[:, sl], -1.0, GS[:, sl],
                                                             OP.mult, OP.max))
                        vc(v, lambda: v.tensor_scalar(GD[:, sl], GA[:, sl], BIAS, 1.0 / 512.0,
                                                      OP.subtract, OP.mult))
                        vc(v, lambda: v.scalar_tensor_tensor(GB[:, sl], GS[:, sl], float(LAM0), GD[:, sl],
                                                             OP.mult, OP.add))
                        vc(v, lambda: v.tensor_scalar(GB[:, sl], GB[:, sl], float(CONST_OFF), None,
                                                      OP.subtract))
                        for j in range(8):
                            pa = float(2 * j + 1)
                            vc(v, lambda pa=pa, j=j: v.tensor_scalar(
                                GD[:, sl], GS[:, sl], pa, float(PAIR_RHO[j]), OP.max, OP.mult))
                            vc(v, lambda pa=pa, j=j: v.scalar_tensor_tensor(
                                GD[:, sl], GS[:, sl], pa + 1.0, GD[:, sl], OP.max, OP.add))
                            vc(v, lambda j=j: v.scalar_tensor_tensor(
                                GB[:, sl], GD[:, sl], float(PAIR_EB[j]), GB[:, sl], OP.mult, OP.add))
                        vc(v, lambda: v.tensor_scalar(Wb[:, sl], GB[:, sl], 256.0, BIAS,
                                                      OP.mult, OP.add))
                        round2(v, Wb, GZ, GD, GS, GA, sl)
                        vc(v, lambda: v.tensor_scalar(HB[:, fg0 * T:fg0 * T + GC], GA[:, sl],
                                                      BIAS, 1.0 / 256.0,
                                                      OP.subtract, OP.mult))
                        gelu_end[(q, pair)] = cnt["s_vd"]
                    VE(gelu)

                    def gelu_gp(g, q=q, pair=pair):
                        gp_ = 4 * q + pair
                        Wb = [WQ, WQ2][gp_ % 2]
                        g.wait_ge(s_sc, h_ev_cnt[(q, pair, 1)])
                        fg0 = 8 * q + 2 * pair
                        g_gelu(g, Wb, slice(GC, 1024),
                               slice(fg0 * T + GC, (fg0 + 2) * T))
                        gelu_gp_end[(q, pair)] = cnt["s_gp"]
                    GP(gelu_gp)

            o_mm_cnt = {}
            o_ev_cnt = {}
            o_rnd_end = {}
            o_rnd_gp_end = {}
            for Q in range(4):
                def w2_load(sync, Q=Q):
                    if Q >= 1:
                        sync.wait_ge(s_mm, o_mm_cnt[(Q - 1, 1)])
                    else:
                        sync.wait_ge(s_mm, xlast["w2_user"])
                    inst = sync.dma_start(
                        out=W2B[:],
                        in_=w2h_r[:, :, 256 * Q:256 * (Q + 1)])
                    sig(inst, "s_w2", 16)
                    o_mm_cnt[("w2x", Q)] = cnt["s_w2"]
                SY(w2_load)

                def o_mm(tensor, Q=Q):
                    if Q == 0:
                        waits(tensor, ["s_vd", "s_sc", "s_gp"])
                    tensor.wait_ge(s_w2, o_mm_cnt[("w2x", Q)])
                    for dd in range(2):
                        ph = [psH0, psH1][dd]
                        if (Q - 1, dd) in o_ev_cnt:
                            tensor.wait_ge(s_sc, o_ev_cnt[(Q - 1, dd)])
                        for j in range(NF):
                            inst = tensor.matmul(
                                ph[:], W2B[:, j * 256 + dd * 128:j * 256 + (dd + 1) * 128],
                                HB[:, j * T:(j + 1) * T],
                                start=(j == 0), stop=(j == NF - 1))
                        sig(inst, "s_mm")
                        o_mm_cnt[(Q, dd)] = cnt["s_mm"]
                        xlast["w2_user"] = cnt["s_mm"]
                TE(o_mm)

                def o_evac(scalar, Q=Q):
                    if Q >= 2:
                        scalar.wait_ge(s_vd, o_rnd_end[Q - 2])
                        scalar.wait_ge(s_gp, o_rnd_gp_end[Q - 2])
                    Wb = [WQ, WQ2][Q % 2]
                    for dd in range(2):
                        cq = 2 * Q + dd
                        ph = [psH0, psH1][dd]
                        scalar.wait_ge(s_mm, o_mm_cnt[(Q, dd)])
                        inst = scalar.activation(Wb[:, dd * T:(dd + 1) * T], ph[:],
                                                 AF.Identity,
                                                 bias=B2Q[:, cq:cq + 1], scale=256.0)
                        sig(inst, "s_sc")
                        o_ev_cnt[(Q, dd)] = cnt["s_sc"]
                SC(o_evac)

                def o_round(vector, Q=Q):
                    vector.wait_ge(s_sc, o_ev_cnt[(Q, 1)])
                    Wb = [WQ, WQ2][Q % 2]
                    v = vector
                    sl = slice(0, GC)
                    rhe(v, Wb, GZ, GD, GS, GA, GB, sl)
                    vc(v, lambda: v.tensor_scalar(TA[:, 2 * Q * T:2 * Q * T + GC],
                                                  GA[:, sl], BIAS, 1.0 / 256.0,
                                                  OP.subtract, OP.mult))
                    o_rnd_end[Q] = cnt["s_vd"]
                VE(o_round)

                def o_round_gp(g, Q=Q):
                    g.wait_ge(s_sc, o_ev_cnt[(Q, 1)])
                    Wb = [WQ, WQ2][Q % 2]
                    sl = slice(GC, 1024)
                    g_rhe(g, Wb, GZ, GD, GS, GA, GB, sl)
                    gc(g, lambda: g.tensor_scalar(GA[:, sl], GA[:, sl], BIAS, None, OP.subtract))
                    gc(g, lambda: g.tensor_scalar(GA[:, sl], GA[:, sl], 1.0 / 256.0, None, OP.mult))
                    gc(g, lambda: g.tensor_copy(TA[:, 2 * Q * T + GC:(2 * Q + 2) * T], GA[:, sl]))
                    o_rnd_gp_end[Q] = cnt["s_gp"]
                GP(o_round_gp)

            stats(TA, is_ln01=False)

            def apply4(vector, Xb=Xb):
                v = vector
                for c in range(NC):
                    sig(v.tensor_tensor(TA[:, c * T:(c + 1) * T],
                                        TA[:, c * T:(c + 1) * T], psS1[:], OP.subtract), "s_vd")
                v.wait_ge(s_vd, cnt["s_vd"])
                for c in range(NC):
                    sig(v.tensor_mul(TA[:, c * T:(c + 1) * T],
                                     TA[:, c * T:(c + 1) * T], BB[:]), "s_vd")
                v.wait_ge(s_vd, cnt["s_vd"])
                for c in range(NC):
                    sig(v.tensor_add(Xb[:, c * T:(c + 1) * T],
                                     Xb[:, c * T:(c + 1) * T], TA[:, c * T:(c + 1) * T]), "s_vd")
            VE(apply4)

            def store(sync, i=i, Xb=Xb):
                waits(sync, ["s_vd"])
                for c in range(NC):
                    inst = sync.dma_start(
                        out=yout[128 * c:128 * (c + 1), T * i:T * (i + 1)],
                        in_=Xb[:, c * T:(c + 1) * T])
                    sig(inst, "s_y", 16)
            SY(store)

            if i + 1 < NT:
                load_x(i + 1)

        def fin(sync):
            sync.wait_ge(s_y, cnt["s_y"])
        SY(fin)

    return nc



_NC_CACHE = {}


def kernel(L0_attn_residual, L0_attn_w_fc, L0_attn_w_proj, L0_attn_ln_g,
           L0_attn_ln_b, L0_ffn_w1, L0_ffn_b1, L0_ffn_w2, L0_ffn_b2):
    x = np.ascontiguousarray(np.asarray(L0_attn_residual, dtype=np.float32))
    w_fc = np.asarray(L0_attn_w_fc, dtype=np.float32)
    w_proj = np.asarray(L0_attn_w_proj, dtype=np.float32)
    ln_g = np.asarray(L0_attn_ln_g, dtype=np.float32)
    ln_b = np.asarray(L0_attn_ln_b, dtype=np.float32)
    w1 = np.ascontiguousarray(np.asarray(L0_ffn_w1, dtype=np.float32))
    b1 = np.asarray(L0_ffn_b1, dtype=np.float32)
    w2 = np.ascontiguousarray(np.asarray(L0_ffn_w2, dtype=np.float32))
    b2 = np.asarray(L0_ffn_b2, dtype=np.float32)

    wkt = np.ascontiguousarray(w_fc.reshape(FF, 256).T)
    wv = np.ascontiguousarray(np.ascontiguousarray(w_proj.T).reshape(FF, 256))
    lng = np.ascontiguousarray(ln_g.reshape(NC, 128).T)
    lnb = np.ascontiguousarray(ln_b.reshape(NC, 128).T)
    b1t = np.ascontiguousarray(b1.reshape(NF, 128).T)
    b2t = np.ascontiguousarray(b2.reshape(NC, 128).T)

    xf = x.reshape(B * TT, C)
    in_maps = []
    for core in range(N_CORES):
        xs = np.ascontiguousarray(xf[core * TOK:(core + 1) * TOK, :].T)
        in_maps.append(dict(x=xs, wkt=wkt, wv=wv, w1=w1, w2=w2,
                            lng=lng, lnb=lnb, b1t=b1t, b2t=b2t))

    if "nc" not in _NC_CACHE:
        _NC_CACHE["nc"] = build_nc()
    nc = _NC_CACHE["nc"]
    res = run_bass_kernel_spmd(nc, in_maps, core_ids=list(range(N_CORES)))

    out = np.empty((B * TT, C), dtype=np.float32)
    for core in range(N_CORES):
        out[core * TOK:(core + 1) * TOK, :] = res.results[core]["y"].T
    return out.reshape(B, TT, C)


# version 13
# speedup vs baseline: 3.6804x; 3.6804x over previous
import contextlib
import struct
import numpy as np
import ml_dtypes

import concourse.bass as bass
import concourse.mybir as mybir
from concourse.bass_utils import run_bass_kernel_spmd

F32 = mybir.dt.float32
F16 = mybir.dt.float16
I32 = mybir.dt.int32
AF = mybir.ActivationFunctionType
OP = mybir.AluOpType

N_CORES = 8
B, TT, C = 8, 8192, 1024
TOK = B * TT // N_CORES
T = 512
NT = TOK // T
NC = C // 128
FF = 4096
NF = FF // 128
EPS = 1e-5
BIAS = 8192.0

_YBITS = [0xb904c000, 0xb9adbb00, 0xba556b00, 0xbaf5ce80, 0xbb84b340, 0xbc064180,
          0xbc7e5920, 0xbce15244, 0xbd3a5e80, 0xbd8f926e, 0xbdcd3b48, 0xbe073b54,
          0xbe227686, 0xbe2e0cbe, 0xbe1df8a2, 0xbdcd765d, 0x00000000, 0x3e1944d2,
          0x3eb103af, 0x3f147cd0, 0x3f57625e, 0x3f8f1895, 0x3fb32c4c, 0x3fd706d9,
          0x3ffa2d0c, 0x400e3d5c, 0x401f01a7, 0x402f79be, 0x403fbda6, 0x404fe146,
          0x405ff2aa, 0x406ffa93, 0x407ffded]
_Y = np.array([struct.unpack('<f', struct.pack('<I', b))[0] for b in _YBITS],
              dtype=np.float32)
_X = np.linspace(-4, 4, 33).astype(np.float32)


def _gelu_consts():
    E = _Y.astype(np.float64)[16:] - _X.astype(np.float64)[16:] / 2.0
    K1 = np.diff(E)
    lam0 = np.float32(K1[0])
    e = np.diff(np.concatenate([K1, [0.125]])).astype(np.float32)
    const_off = np.float32(np.sum(e.astype(np.float64) * np.arange(1, 17)))
    return lam0, e, const_off


LAM0, ECOEF, CONST_OFF = _gelu_consts()
PAIR_RHO = [np.float32(np.float64(ECOEF[2 * j]) / np.float64(ECOEF[2 * j + 1]))
            for j in range(8)]
PAIR_EB = [ECOEF[2 * j + 1] for j in range(8)]


def _verify_gelu_chain():
    def ref(xq):
        y = np.interp(xq, _X.astype(np.float64), _Y.astype(np.float64))
        return np.where(xq > 4.0, xq, np.where(xq < -4.0, 0.0, y))

    A0 = np.arange(-16384, 16385).astype(np.float64) + BIAS
    s = np.abs(A0 * np.float64(1.0 / 64.0) + np.float64(-128.0))
    acc = (A0 - BIAS) * np.float64(1.0 / 512.0)
    acc = np.float64(LAM0) * s + acc
    acc = acc - np.float64(CONST_OFF)
    for j in range(8):
        rho, eb = PAIR_RHO[j], PAIR_EB[j]
        u = np.maximum(s, np.float64(2 * j + 1)) * np.float64(rho)
        u = np.maximum(s, np.float64(2 * j + 2)) + u
        acc = u * np.float64(eb) + acc
    w2 = acc * 256.0 + BIAS
    fr = w2 - np.floor(w2)
    assert np.abs(fr - 0.5).min() > 1e-4, "site-2 rounding tie on lattice"
    for conv in (np.trunc, np.round):
        for jit in (0.0, 0.9, -0.9):
            z2 = conv(w2 + jit)
            A2 = z2.copy()
            for _ in range(2):
                d2 = w2 - A2
                A2 = (A2 + (d2 > 0.5)) - (d2 < -0.5)
            q2 = A2 - BIAS
            y2_ref = np.round(256.0 * ref((A0 - BIAS) / 256.0))
            assert np.array_equal(q2, y2_ref), "gelu chain verification failed"


_verify_gelu_chain()



def build_nc():
    nc = bass.Bass("TRN2", target_bir_lowering=False)

    xin = nc.dram_tensor("x", [C, TOK], F32, kind="ExternalInput").ap()
    wkt = nc.dram_tensor("wkt", [256, FF], F32, kind="ExternalInput").ap()
    wv = nc.dram_tensor("wv", [FF, 256], F32, kind="ExternalInput").ap()
    w1 = nc.dram_tensor("w1", [C, FF], F32, kind="ExternalInput").ap()
    w2 = nc.dram_tensor("w2", [FF, C], F32, kind="ExternalInput").ap()
    lng_in = nc.dram_tensor("lng", [128, NC], F32, kind="ExternalInput").ap()
    lnb_in = nc.dram_tensor("lnb", [128, NC], F32, kind="ExternalInput").ap()
    b1_in = nc.dram_tensor("b1t", [128, NF], F32, kind="ExternalInput").ap()
    b2_in = nc.dram_tensor("b2t", [128, NC], F32, kind="ExternalInput").ap()
    yout = nc.dram_tensor("y", [C, TOK], F32, kind="ExternalOutput").ap()

    w1h = nc.dram_tensor("w1h", [C, FF], F16, kind="Internal").ap()
    w2h = nc.dram_tensor("w2h", [FF, C], F16, kind="Internal").ap()

    es = contextlib.ExitStack()
    sb = lambda name, shape, dt: es.enter_context(nc.sbuf_tensor(name, shape, dt))
    ps = lambda name, shape: es.enter_context(nc.psum_tensor(name, shape, F32))
    sem = lambda name: es.enter_context(nc.semaphore(name))

    with es:
        X0 = sb("X0", [128, NC * T], F32)
        X1 = sb("X1", [128, NC * T], F32)
        TA = sb("TA", [128, NC * T], F32)
        U = sb("U", [128, NC * T], F16)
        KTA = sb("KTA", [128, FF], F16)
        KTB = sb("KTB", [128, FF], F16)
        VV = sb("VV", [128, FF * 2], F16)
        W1B = sb("W1B", [128, 8192], F16)
        W2B = sb("W2B", [128, 8192], F16)
        RB = sb("RB", [128, 4096], F16)
        PB = sb("PB", [128, 4096], F16)
        HB = sb("HB", [128, NF * T], F16)
        WQ = sb("WQ", [128, 1024], F32)
        WQ2 = sb("WQ2", [128, 1024], F32)
        GZ = sb("GZ", [128, 1024], I32)
        GZ2 = sb("GZ2", [128, 1024], I32)
        GD = sb("GD", [128, 1024], F32)
        GS = sb("GS", [128, 1024], F32)
        GA = sb("GA", [128, 1024], F32)
        GB = sb("GB", [128, 1024], F32)
        VB = sb("VB", [128, T], F32)
        AB = sb("AB", [128, T], F32)
        BB = sb("BB", [128, T], F32)
        LNG = sb("LNG", [128, NC], F32)
        LNB = sb("LNB", [128, NC], F32)
        B1T = sb("B1T", [128, NF], F32)
        B2T = sb("B2T", [128, NC], F32)
        B1Q = sb("B1Q", [128, NF], F32)
        B2Q = sb("B2Q", [128, NC], F32)
        ONES = sb("ONES", [128, 128], F16)
        CEPS = sb("CEPS", [128, 1], F32)

        psS1 = ps("psS1", [128, T])
        psS2 = ps("psS2", [128, T])
        psC0 = ps("psC0", [128, T])
        psC1 = ps("psC1", [128, T])
        psO0 = ps("psO0", [128, T])
        psO1 = ps("psO1", [128, T])
        psH0 = ps("psH0", [128, T])
        psH1 = ps("psH1", [128, T])

        s_x = sem("s_x")
        s_w1 = sem("s_w1")
        s_w2 = sem("s_w2")
        s_y = sem("s_y")
        s_mm = sem("s_mm")
        s_sc = sem("s_sc")
        s_vd = sem("s_vd")

        cnt = {"s_x": 0, "s_y": 0, "s_mm": 0, "s_sc": 0, "s_vd": 0,
               "s_w1": 0, "s_w2": 0}
        hmap = {"s_x": s_x, "s_y": s_y, "s_mm": s_mm, "s_sc": s_sc, "s_vd": s_vd,
                "s_w1": s_w1, "s_w2": s_w2}

        blk = es.enter_context(nc.Block())

        def sig(inst, name, k=1):
            cnt[name] += k
            inst.then_inc(hmap[name], k)

        def waits(eng, names):
            for n in names:
                eng.wait_ge(hmap[n], cnt[n])

        X = [X0, X1]
        xlast = {"w1_user": 0, "w2_user": 0}

        def vc(vector, emit, dep=None):
            vector.wait_ge(s_vd, cnt["s_vd"] if dep is None else dep)
            sig(emit(), "s_vd")

        def sc_op(scalar, emit):
            scalar.wait_ge(s_sc, cnt["s_sc"])
            sig(emit(), "s_sc")

        def SY(f):
            blk.sync(f)

        def TE(f):
            blk.tensor(f)

        def VE(f):
            def g(vector):
                vector.wait_ge(s_vd, cnt["s_vd"])
                f(vector)
            blk.vector(g)

        def SC(f):
            def g(scalar):
                scalar.wait_ge(s_sc, cnt["s_sc"])
                f(scalar)
            blk.scalar(g)

        def startup_sync(sync):
            sig(sync.dma_start(out=LNG[:], in_=lng_in[:, :]), "s_x", 16)
            sig(sync.dma_start(out=LNB[:], in_=lnb_in[:, :]), "s_x", 16)
            sig(sync.dma_start(out=B1T[:], in_=b1_in[:, :]), "s_x", 16)
            sig(sync.dma_start(out=B2T[:], in_=b2_in[:, :]), "s_x", 16)
        SY(startup_sync)

        def startup_vec(vector):
            waits(vector, ["s_x"])
            sig(vector.memset(ONES[:], 1.0 / 1024.0), "s_vd")
            sig(vector.memset(CEPS[:], EPS), "s_vd")
            sig(vector.tensor_scalar(B1Q[:], B1T[:], 256.0, BIAS, OP.mult, OP.add), "s_vd")
            sig(vector.tensor_scalar(B2Q[:], B2T[:], 256.0, BIAS, OP.mult, OP.add), "s_vd")
        VE(startup_vec)

        def conv_round(src_ap, dst_sbuf_ap=None, dst_dram_ap=None):
            def ld(sync):
                waits(sync, ["s_vd"])
                sig(sync.dma_start(out=X0[:, 0:src_cols], in_=src_ap), "s_x", 16)
            def cv(vector):
                waits(vector, ["s_x"])
                if dst_sbuf_ap is not None:
                    sig(vector.tensor_copy(dst_sbuf_ap, X0[:, 0:src_cols]), "s_vd")
                else:
                    sig(vector.tensor_copy(RB[:, 0:src_cols], X0[:, 0:src_cols]), "s_vd")
            SY(ld)
            VE(cv)
            if dst_dram_ap is not None:
                def st(sync):
                    waits(sync, ["s_vd"])
                    sig(sync.dma_start(out=dst_dram_ap, in_=RB[:, 0:src_cols]), "s_x", 16)
                SY(st)

        src_cols = 4096
        conv_round(wkt[0:128, :], dst_sbuf_ap=KTA[:])
        conv_round(wkt[128:256, :], dst_sbuf_ap=KTB[:])
        wv_r = wv.rearrange("(j p) d -> p j d", p=128)
        conv_round(wv_r[:, 0:16, :], dst_sbuf_ap=VV[:, 0:4096])
        conv_round(wv_r[:, 16:32, :], dst_sbuf_ap=VV[:, 4096:8192])
        w1h_r = w1h
        for c in range(NC):
            conv_round(w1[128 * c:128 * (c + 1), :],
                       dst_dram_ap=w1h_r[128 * c:128 * (c + 1), :])
        w2_r = w2.rearrange("(j p) d -> p j d", p=128)
        w2h_r = w2h.rearrange("(j p) d -> p j d", p=128)
        for r in range(8):
            conv_round(w2_r[:, 4 * r:4 * (r + 1), :],
                       dst_dram_ap=w2h_r[:, 4 * r:4 * (r + 1), :])


        def floor2(v, Bv, Z, D, S, F):
            vc(v, lambda: v.tensor_copy(Z[:], Bv[:]))
            for _ in range(2):
                src_int = Z if _ == 0 else F
                vc(v, lambda: v.tensor_tensor(D[:], Bv[:], src_int[:], OP.subtract))
                vc(v, lambda: v.scalar_tensor_tensor(S[:], D[:], 1.0, src_int[:], OP.is_ge, OP.add))
                vc(v, lambda: v.tensor_scalar(D[:], D[:], 0.0, None, OP.is_lt))
                vc(v, lambda: v.tensor_tensor(F[:], S[:], D[:], OP.subtract))

        def rhe(v, W, Z, D, S, A, Bv):
            vc(v, lambda: v.tensor_scalar(Bv[:], W[:], 0.5, 0.25, OP.mult, OP.add))
            floor2(v, Bv, Z, D, S, A)
            vc(v, lambda: v.tensor_scalar(Bv[:], W[:], -0.5, 0.25, OP.mult, OP.add))
            floor2(v, Bv, Z, D, S, W)
            vc(v, lambda: v.tensor_tensor(A[:], A[:], W[:], OP.subtract))

        def round2(v, W, Z, D, S, A):
            vc(v, lambda: v.tensor_copy(Z[:], W[:]))
            vc(v, lambda: v.tensor_tensor(D[:], W[:], Z[:], OP.subtract))
            vc(v, lambda: v.scalar_tensor_tensor(A[:], D[:], 0.5, Z[:], OP.is_gt, OP.add))
            vc(v, lambda: v.tensor_scalar(S[:], D[:], -0.5, None, OP.is_lt))
            vc(v, lambda: v.tensor_tensor(D[:], A[:], S[:], OP.subtract))
            vc(v, lambda: v.tensor_tensor(A[:], W[:], D[:], OP.subtract))
            vc(v, lambda: v.scalar_tensor_tensor(S[:], A[:], 0.5, D[:], OP.is_gt, OP.add))
            vc(v, lambda: v.tensor_scalar(W[:], A[:], -0.5, None, OP.is_lt))
            vc(v, lambda: v.tensor_tensor(A[:], S[:], W[:], OP.subtract))

        def load_x(i):
            Xb = X[i % 2]
            def f(sync):
                waits(sync, ["s_vd", "s_y"])
                for c in range(NC):
                    inst = sync.dma_start(
                        out=Xb[:, c * T:(c + 1) * T],
                        in_=xin[128 * c:128 * (c + 1), T * i:T * (i + 1)])
                    sig(inst, "s_x", 16)
            SY(f)

        def stats(src, is_ln01):
            def cast(scalar):
                waits(scalar, ["s_vd", "s_mm", "s_x"])
                sig(scalar.activation(RB[:], src[:], AF.Identity, bias=0.0, scale=1.0), "s_sc")
                sig(scalar.activation(PB[:], src[:], AF.Square, bias=0.0, scale=1.0), "s_sc")
            SC(cast)

            def mm(tensor):
                waits(tensor, ["s_sc"])
                for c in range(NC):
                    tensor.matmul(psS1[:], ONES[:], RB[:, c * T:(c + 1) * T],
                                  start=(c == 0), stop=(c == NC - 1))
                for c in range(NC):
                    inst = tensor.matmul(psS2[:], ONES[:], PB[:, c * T:(c + 1) * T],
                                         start=(c == 0), stop=(c == NC - 1))
                sig(inst, "s_mm")
            TE(mm)

            def chain(vector):
                waits(vector, ["s_mm"])
                vc(vector, lambda: vector.tensor_copy(GS[:, 0:T], psS1[:]))
                vc(vector, lambda: vector.tensor_mul(VB[:], GS[:, 0:T], GS[:, 0:T]))
                vc(vector, lambda: vector.tensor_tensor(VB[:], psS2[:], VB[:], OP.subtract))
            VE(chain)

            def chain_sc(scalar):
                waits(scalar, ["s_vd"])
                inst = scalar.activation(GS[:, T:2 * T], VB[:], AF.Sqrt, bias=CEPS[:], scale=1.0)
                sig(inst, "s_sc")
            SC(chain_sc)

            def recip1(vector):
                waits(vector, ["s_sc"])
                vc(vector, lambda: vector.reciprocal(BB[:], GS[:, T:2 * T]))
            VE(recip1)

            if is_ln01:
                def chain2(vector):
                    vc(vector, lambda: vector.tensor_scalar(GS[:, T:2 * T], VB[:], 1.0 + EPS, EPS * EPS,
                                                            OP.mult, OP.add))
                VE(chain2)

                def chain2_sc(scalar):
                    waits(scalar, ["s_vd"])
                    inst = scalar.activation(GS[:, T:2 * T], GS[:, T:2 * T], AF.Sqrt, bias=0.0, scale=1.0)
                    sig(inst, "s_sc")
                SC(chain2_sc)

                def recip2(vector):
                    waits(vector, ["s_sc"])
                    vc(vector, lambda: vector.reciprocal(AB[:], GS[:, T:2 * T]))
                VE(recip2)

        load_x(0)
        for i in range(NT):
            Xb = X[i % 2]

            stats(Xb, is_ln01=True)

            def apply01(vector, Xb=Xb):
                v = vector
                for c in range(NC):
                    sig(v.tensor_tensor(TA[:, c * T:(c + 1) * T],
                                        Xb[:, c * T:(c + 1) * T], psS1[:], OP.subtract), "s_vd")
                v.wait_ge(s_vd, cnt["s_vd"])
                for c in range(NC):
                    sig(v.tensor_mul(U[:, c * T:(c + 1) * T],
                                     TA[:, c * T:(c + 1) * T], AB[:]), "s_vd")
                v.wait_ge(s_vd, cnt["s_vd"])
                for c in range(NC):
                    sig(v.tensor_scalar(U[:, c * T:(c + 1) * T], U[:, c * T:(c + 1) * T],
                                        LNG[:, c:c + 1], LNB[:, c:c + 1],
                                        OP.mult, OP.add), "s_vd")
                v.wait_ge(s_vd, cnt["s_vd"])
                for c in range(NC):
                    sig(v.tensor_mul(TA[:, c * T:(c + 1) * T],
                                     TA[:, c * T:(c + 1) * T], BB[:]), "s_vd")
            VE(apply01)

            ps_sc = [psC0, psC1, psH0, psH1]
            for h in range(4):
                relu_cnt = []
                p_cnt = []
                out_cnt = []
                mm_cnt = []

                def scores_hg(tensor, h=h, hg=None, relu_cnt=relu_cnt):
                    if h == 0 and hg == 0:
                        waits(tensor, ["s_vd", "s_sc"])
                    for jj in range(4 * hg, 4 * hg + 4):
                        j = 32 * 0 + jj
                        pc = ps_sc[jj % 4]
                        if jj >= 4:
                            tensor.wait_ge(s_sc, relu_cnt[jj - 4])
                        tensor.matmul(pc[:], KTA[:, jj * 128:(jj + 1) * 128],
                                      U[:, (2 * h) * T:(2 * h + 1) * T],
                                      start=True, stop=False, skip_group_check=True)
                        inst = tensor.matmul(pc[:], KTB[:, jj * 128:(jj + 1) * 128],
                                             U[:, (2 * h + 1) * T:(2 * h + 2) * T],
                                             start=False, stop=True, skip_group_check=True)
                        sig(inst, "s_mm")
                        mm_cnt.append(cnt["s_mm"])

                def relu_hg(scalar, h=h, hg=None, mm_cnt=mm_cnt, p_cnt=p_cnt,
                            relu_cnt=relu_cnt):
                    if hg >= 2:
                        scalar.wait_ge(s_vd, p_cnt[hg - 2])
                    for jj in range(4 * hg, 4 * hg + 4):
                        pc = ps_sc[jj % 4]
                        scalar.wait_ge(s_mm, mm_cnt[jj])
                        col = (hg % 2) * 2048 + (jj % 4) * T
                        inst = scalar.activation(RB[:, col:col + T], pc[:],
                                                 AF.Relu, bias=0.0, scale=1.0)
                        sig(inst, "s_sc")
                        relu_cnt.append(cnt["s_sc"])

                def p_hg(vector, h=h, hg=None, relu_cnt=relu_cnt, out_cnt=out_cnt,
                         p_cnt=p_cnt):
                    vector.wait_ge(s_sc, relu_cnt[4 * hg + 3])
                    if hg >= 2:
                        vector.wait_ge(s_mm, out_cnt[hg - 2])
                    half = (hg % 2) * 2048
                    sig(vector.tensor_mul(PB[:, half:half + 2048],
                                          RB[:, half:half + 2048],
                                          RB[:, half:half + 2048]), "s_vd")
                    p_cnt.append(cnt["s_vd"])

                def out_hg(tensor, h=h, hg=None, p_cnt=p_cnt, out_cnt=out_cnt):
                    tensor.wait_ge(s_vd, p_cnt[hg])
                    half = (hg % 2) * 2048
                    for k in range(4):
                        jj = 4 * hg + k
                        tensor.matmul(psO0[:], VV[:, jj * 256:jj * 256 + 128],
                                      PB[:, half + k * T:half + (k + 1) * T],
                                      start=(jj == 0), stop=(jj == 31),
                                      skip_group_check=True)
                        inst = tensor.matmul(psO1[:], VV[:, jj * 256 + 128:jj * 256 + 256],
                                             PB[:, half + k * T:half + (k + 1) * T],
                                             start=(jj == 0), stop=(jj == 31),
                                             skip_group_check=True)
                    sig(inst, "s_mm")
                    out_cnt.append(cnt["s_mm"])

                for hg in range(8):
                    TE(lambda t, f=scores_hg, hg=hg: f(t, hg=hg))
                    SC(lambda s, f=relu_hg, hg=hg: f(s, hg=hg))
                    VE(lambda v, f=p_hg, hg=hg: f(v, hg=hg))
                    if hg >= 1:
                        TE(lambda t, f=out_hg, hg=hg - 1: f(t, hg=hg))
                TE(lambda t, f=out_hg: f(t, hg=7))

                def a_add(vector, h=h):
                    waits(vector, ["s_mm"])
                    sig(vector.tensor_tensor(TA[:, (2 * h) * T:(2 * h + 1) * T],
                                             TA[:, (2 * h) * T:(2 * h + 1) * T],
                                             psO0[:], OP.add), "s_vd")
                    sig(vector.tensor_tensor(TA[:, (2 * h + 1) * T:(2 * h + 2) * T],
                                             TA[:, (2 * h + 1) * T:(2 * h + 2) * T],
                                             psO1[:], OP.add), "s_vd")
                VE(a_add)

            stats(TA, is_ln01=False)

            def apply2(vector, Xb=Xb):
                v = vector
                for c in range(NC):
                    sig(v.tensor_tensor(TA[:, c * T:(c + 1) * T],
                                        TA[:, c * T:(c + 1) * T], psS1[:], OP.subtract), "s_vd")
                v.wait_ge(s_vd, cnt["s_vd"])
                for c in range(NC):
                    sig(v.tensor_mul(TA[:, c * T:(c + 1) * T],
                                     TA[:, c * T:(c + 1) * T], BB[:]), "s_vd")
                v.wait_ge(s_vd, cnt["s_vd"])
                for c in range(NC):
                    sig(v.tensor_add(Xb[:, c * T:(c + 1) * T],
                                     Xb[:, c * T:(c + 1) * T], TA[:, c * T:(c + 1) * T]), "s_vd")
            VE(apply2)

            stats(Xb, is_ln01=False)

            def apply3(vector, Xb=Xb):
                v = vector
                for c in range(NC):
                    sig(v.tensor_tensor(TA[:, c * T:(c + 1) * T],
                                        Xb[:, c * T:(c + 1) * T], psS1[:], OP.subtract), "s_vd")
                v.wait_ge(s_vd, cnt["s_vd"])
                for c in range(NC):
                    sig(v.tensor_mul(U[:, c * T:(c + 1) * T],
                                     TA[:, c * T:(c + 1) * T], BB[:]), "s_vd")
            VE(apply3)

            h_mm_cnt = {}
            h_ev_cnt = {}
            gelu_end = {}
            w1_cnt = {}
            for q in range(4):
                def w1_load(sync, q=q):
                    if q >= 1:
                        sync.wait_ge(s_mm, h_mm_cnt[(q - 1, 3, 1)])
                    else:
                        sync.wait_ge(s_mm, xlast["w1_user"])
                    for c in range(NC):
                        inst = sync.dma_start(
                            out=W1B[:, c * 1024:(c + 1) * 1024],
                            in_=w1h[128 * c:128 * (c + 1), 1024 * q:1024 * (q + 1)])
                        sig(inst, "s_w1", 16)
                    w1_cnt[q] = cnt["s_w1"]
                SY(w1_load)

                for pair in range(4):
                    def h_mm(tensor, q=q, pair=pair):
                        if q == 0 and pair == 0:
                            waits(tensor, ["s_vd", "s_sc"])
                        tensor.wait_ge(s_w1, w1_cnt[q])
                        for k in range(2):
                            jf = 2 * pair + k
                            ph = [psH0, psH1][k]
                            prev = (q, pair - 1, k) if pair else (q - 1, 3, k)
                            if prev in h_ev_cnt:
                                tensor.wait_ge(s_sc, h_ev_cnt[prev])
                            for c in range(NC):
                                inst = tensor.matmul(
                                    ph[:], W1B[:, c * 1024 + jf * 128:c * 1024 + (jf + 1) * 128],
                                    U[:, c * T:(c + 1) * T],
                                    start=(c == 0), stop=(c == NC - 1))
                            sig(inst, "s_mm")
                            h_mm_cnt[(q, pair, k)] = cnt["s_mm"]
                            xlast["w1_user"] = cnt["s_mm"]
                    TE(h_mm)

                    def h_evac(scalar, q=q, pair=pair):
                        gp = 4 * q + pair
                        if gp >= 2:
                            pg = ((gp - 2) // 4, (gp - 2) % 4)
                            scalar.wait_ge(s_vd, gelu_end[pg])
                        Wb = [WQ, WQ2][gp % 2]
                        Zb = [GZ, GZ2][gp % 2]
                        for k in range(2):
                            jf = 2 * pair + k
                            fg = 8 * q + jf
                            ph = [psH0, psH1][k]
                            scalar.wait_ge(s_mm, h_mm_cnt[(q, pair, k)])
                            inst = scalar.activation(Wb[:, k * T:(k + 1) * T], ph[:],
                                                     AF.Identity,
                                                     bias=B1Q[:, fg:fg + 1], scale=256.0)
                            sig(inst, "s_sc")
                            inst = scalar.activation(Zb[:, k * T:(k + 1) * T], ph[:],
                                                     AF.Identity,
                                                     bias=B1Q[:, fg:fg + 1], scale=256.0)
                            sig(inst, "s_sc")
                            h_ev_cnt[(q, pair, k)] = cnt["s_sc"]
                    SC(h_evac)

                    def gelu(vector, q=q, pair=pair):
                        gp = 4 * q + pair
                        Wb = [WQ, WQ2][gp % 2]
                        Zb = [GZ, GZ2][gp % 2]
                        vector.wait_ge(s_sc, h_ev_cnt[(q, pair, 1)])
                        fg0 = 8 * q + 2 * pair
                        cols = slice(fg0 * T, (fg0 + 2) * T)
                        v = vector
                        vc(v, lambda: v.tensor_tensor(GD[:], Wb[:], Zb[:], OP.subtract))
                        vc(v, lambda: v.scalar_tensor_tensor(GA[:], GD[:], 0.5, Zb[:], OP.is_gt, OP.add))
                        vc(v, lambda: v.tensor_scalar(GS[:], GD[:], -0.5, None, OP.is_lt))
                        vc(v, lambda: v.tensor_tensor(GD[:], GA[:], GS[:], OP.subtract))
                        vc(v, lambda: v.tensor_tensor(GA[:], Wb[:], GD[:], OP.subtract))
                        vc(v, lambda: v.scalar_tensor_tensor(GS[:], GA[:], 0.5, GD[:], OP.is_gt, OP.add))
                        vc(v, lambda: v.tensor_scalar(Wb[:], GA[:], -0.5, None, OP.is_lt))
                        vc(v, lambda: v.tensor_tensor(GA[:], GS[:], Wb[:], OP.subtract))
                        vc(v, lambda: v.tensor_scalar(GS[:], GA[:], BIAS, 1.0 / 64.0,
                                                      OP.subtract, OP.mult))
                        vc(v, lambda: v.scalar_tensor_tensor(GS[:], GS[:], -1.0, GS[:],
                                                             OP.mult, OP.max))
                        vc(v, lambda: v.tensor_scalar(GD[:], GA[:], BIAS, 1.0 / 512.0,
                                                      OP.subtract, OP.mult))
                        vc(v, lambda: v.scalar_tensor_tensor(GB[:], GS[:], float(LAM0), GD[:],
                                                             OP.mult, OP.add))
                        vc(v, lambda: v.tensor_scalar(GB[:], GB[:], float(CONST_OFF), None,
                                                      OP.subtract))
                        mbuf = [GD, Wb, GA]
                        ts_c = [None] * 8
                        m_c = [None] * 8
                        acc_c = [None] * 9
                        acc_c[0] = cnt["s_vd"]
                        base = cnt["s_vd"]

                        def e_ts(j):
                            mb = mbuf[j % 3]
                            dep = acc_c[j - 2] if j >= 3 else base
                            vc(v, lambda: v.tensor_scalar(
                                mb[:], GS[:], float(2 * j + 1), float(PAIR_RHO[j]),
                                OP.max, OP.mult), dep=dep)
                            ts_c[j] = cnt["s_vd"]

                        def e_m(j):
                            mb = mbuf[j % 3]
                            vc(v, lambda: v.scalar_tensor_tensor(
                                mb[:], GS[:], float(2 * j + 2), mb[:], OP.max, OP.add),
                                dep=ts_c[j])
                            m_c[j] = cnt["s_vd"]

                        def e_acc(j):
                            mb = mbuf[j % 3]
                            vc(v, lambda: v.scalar_tensor_tensor(
                                GB[:], mb[:], float(PAIR_EB[j]), GB[:], OP.mult, OP.add),
                                dep=max(m_c[j], acc_c[j]))
                            acc_c[j + 1] = cnt["s_vd"]

                        e_ts(0); e_ts(1); e_m(0); e_ts(2); e_m(1); e_acc(0)
                        for j in range(1, 6):
                            e_m(j + 1); e_acc(j); e_ts(j + 2)
                        e_m(7); e_acc(6); e_acc(7)
                        vc(v, lambda: v.tensor_scalar(Wb[:], GB[:], 256.0, BIAS,
                                                      OP.mult, OP.add))
                        round2(v, Wb, Zb, GD, GS, GA)
                        vc(v, lambda: v.tensor_scalar(HB[:, cols], GA[:], BIAS, 1.0 / 256.0,
                                                      OP.subtract, OP.mult))
                        gelu_end[(q, pair)] = cnt["s_vd"]
                    VE(gelu)

            o_mm_cnt = {}
            o_ev_cnt = {}
            o_rnd_end = {}
            for Q in range(4):
                def w2_load(sync, Q=Q):
                    if Q >= 1:
                        sync.wait_ge(s_mm, o_mm_cnt[(Q - 1, 1)])
                    else:
                        sync.wait_ge(s_mm, xlast["w2_user"])
                    inst = sync.dma_start(
                        out=W2B[:],
                        in_=w2h_r[:, :, 256 * Q:256 * (Q + 1)])
                    sig(inst, "s_w2", 16)
                    o_mm_cnt[("w2x", Q)] = cnt["s_w2"]
                SY(w2_load)

                def o_mm(tensor, Q=Q):
                    if Q == 0:
                        waits(tensor, ["s_vd", "s_sc"])
                    tensor.wait_ge(s_w2, o_mm_cnt[("w2x", Q)])
                    for dd in range(2):
                        ph = [psH0, psH1][dd]
                        if (Q - 1, dd) in o_ev_cnt:
                            tensor.wait_ge(s_sc, o_ev_cnt[(Q - 1, dd)])
                        for j in range(NF):
                            inst = tensor.matmul(
                                ph[:], W2B[:, j * 256 + dd * 128:j * 256 + (dd + 1) * 128],
                                HB[:, j * T:(j + 1) * T],
                                start=(j == 0), stop=(j == NF - 1))
                        sig(inst, "s_mm")
                        o_mm_cnt[(Q, dd)] = cnt["s_mm"]
                        xlast["w2_user"] = cnt["s_mm"]
                TE(o_mm)

                def o_evac(scalar, Q=Q):
                    if Q >= 2:
                        scalar.wait_ge(s_vd, o_rnd_end[Q - 2])
                    Wb = [WQ, WQ2][Q % 2]
                    for dd in range(2):
                        cq = 2 * Q + dd
                        ph = [psH0, psH1][dd]
                        scalar.wait_ge(s_mm, o_mm_cnt[(Q, dd)])
                        inst = scalar.activation(Wb[:, dd * T:(dd + 1) * T], ph[:],
                                                 AF.Identity,
                                                 bias=B2Q[:, cq:cq + 1], scale=256.0)
                        sig(inst, "s_sc")
                        o_ev_cnt[(Q, dd)] = cnt["s_sc"]
                SC(o_evac)

                def o_round(vector, Q=Q):
                    vector.wait_ge(s_sc, o_ev_cnt[(Q, 1)])
                    Wb = [WQ, WQ2][Q % 2]
                    v = vector
                    rhe(v, Wb, GZ, GD, GS, GA, GB)
                    vc(v, lambda: v.tensor_scalar(TA[:, 2 * Q * T:(2 * Q + 2) * T],
                                                  GA[:], BIAS, 1.0 / 256.0,
                                                  OP.subtract, OP.mult))
                    o_rnd_end[Q] = cnt["s_vd"]
                VE(o_round)

            stats(TA, is_ln01=False)

            def apply4(vector, Xb=Xb):
                v = vector
                for c in range(NC):
                    sig(v.tensor_tensor(TA[:, c * T:(c + 1) * T],
                                        TA[:, c * T:(c + 1) * T], psS1[:], OP.subtract), "s_vd")
                v.wait_ge(s_vd, cnt["s_vd"])
                for c in range(NC):
                    sig(v.tensor_mul(TA[:, c * T:(c + 1) * T],
                                     TA[:, c * T:(c + 1) * T], BB[:]), "s_vd")
                v.wait_ge(s_vd, cnt["s_vd"])
                for c in range(NC):
                    sig(v.tensor_add(Xb[:, c * T:(c + 1) * T],
                                     Xb[:, c * T:(c + 1) * T], TA[:, c * T:(c + 1) * T]), "s_vd")
            VE(apply4)

            def store(sync, i=i, Xb=Xb):
                waits(sync, ["s_vd"])
                for c in range(NC):
                    inst = sync.dma_start(
                        out=yout[128 * c:128 * (c + 1), T * i:T * (i + 1)],
                        in_=Xb[:, c * T:(c + 1) * T])
                    sig(inst, "s_y", 16)
            SY(store)

            if i + 1 < NT:
                load_x(i + 1)

        def fin(sync):
            sync.wait_ge(s_y, cnt["s_y"])
        SY(fin)

    return nc



_NC_CACHE = {}


def kernel(L0_attn_residual, L0_attn_w_fc, L0_attn_w_proj, L0_attn_ln_g,
           L0_attn_ln_b, L0_ffn_w1, L0_ffn_b1, L0_ffn_w2, L0_ffn_b2):
    x = np.ascontiguousarray(np.asarray(L0_attn_residual, dtype=np.float32))
    w_fc = np.asarray(L0_attn_w_fc, dtype=np.float32)
    w_proj = np.asarray(L0_attn_w_proj, dtype=np.float32)
    ln_g = np.asarray(L0_attn_ln_g, dtype=np.float32)
    ln_b = np.asarray(L0_attn_ln_b, dtype=np.float32)
    w1 = np.ascontiguousarray(np.asarray(L0_ffn_w1, dtype=np.float32))
    b1 = np.asarray(L0_ffn_b1, dtype=np.float32)
    w2 = np.ascontiguousarray(np.asarray(L0_ffn_w2, dtype=np.float32))
    b2 = np.asarray(L0_ffn_b2, dtype=np.float32)

    wkt = np.ascontiguousarray(w_fc.reshape(FF, 256).T)
    wv = np.ascontiguousarray(np.ascontiguousarray(w_proj.T).reshape(FF, 256))
    lng = np.ascontiguousarray(ln_g.reshape(NC, 128).T)
    lnb = np.ascontiguousarray(ln_b.reshape(NC, 128).T)
    b1t = np.ascontiguousarray(b1.reshape(NF, 128).T)
    b2t = np.ascontiguousarray(b2.reshape(NC, 128).T)

    xf = x.reshape(B * TT, C)
    in_maps = []
    for core in range(N_CORES):
        xs = np.ascontiguousarray(xf[core * TOK:(core + 1) * TOK, :].T)
        in_maps.append(dict(x=xs, wkt=wkt, wv=wv, w1=w1, w2=w2,
                            lng=lng, lnb=lnb, b1t=b1t, b2t=b2t))

    if "nc" not in _NC_CACHE:
        _NC_CACHE["nc"] = build_nc()
    nc = _NC_CACHE["nc"]
    res = run_bass_kernel_spmd(nc, in_maps, core_ids=list(range(N_CORES)))

    out = np.empty((B * TT, C), dtype=np.float32)
    for core in range(N_CORES):
        out[core * TOK:(core + 1) * TOK, :] = res.results[core]["y"].T
    return out.reshape(B, TT, C)


# version 16
# speedup vs baseline: 3.7867x; 1.0289x over previous
import contextlib
import struct
import numpy as np
import ml_dtypes

import concourse.bass as bass
import concourse.mybir as mybir
from concourse.bass_utils import run_bass_kernel_spmd

F32 = mybir.dt.float32
F16 = mybir.dt.float16
I32 = mybir.dt.int32
AF = mybir.ActivationFunctionType
OP = mybir.AluOpType

N_CORES = 8
B, TT, C = 8, 8192, 1024
TOK = B * TT // N_CORES
T = 512
NT = TOK // T
NC = C // 128
FF = 4096
NF = FF // 128
EPS = 1e-5
BIAS = 8192.0

_YBITS = [0xb904c000, 0xb9adbb00, 0xba556b00, 0xbaf5ce80, 0xbb84b340, 0xbc064180,
          0xbc7e5920, 0xbce15244, 0xbd3a5e80, 0xbd8f926e, 0xbdcd3b48, 0xbe073b54,
          0xbe227686, 0xbe2e0cbe, 0xbe1df8a2, 0xbdcd765d, 0x00000000, 0x3e1944d2,
          0x3eb103af, 0x3f147cd0, 0x3f57625e, 0x3f8f1895, 0x3fb32c4c, 0x3fd706d9,
          0x3ffa2d0c, 0x400e3d5c, 0x401f01a7, 0x402f79be, 0x403fbda6, 0x404fe146,
          0x405ff2aa, 0x406ffa93, 0x407ffded]
_Y = np.array([struct.unpack('<f', struct.pack('<I', b))[0] for b in _YBITS],
              dtype=np.float32)
_X = np.linspace(-4, 4, 33).astype(np.float32)


def _gelu_consts():
    E = _Y.astype(np.float64)[16:] - _X.astype(np.float64)[16:] / 2.0
    K1 = np.diff(E)
    lam0 = np.float32(K1[0])
    e = np.diff(np.concatenate([K1, [0.125]])).astype(np.float32)
    const_off = np.float32(np.sum(e.astype(np.float64) * np.arange(1, 17)))
    return lam0, e, const_off


LAM0, ECOEF, CONST_OFF = _gelu_consts()
PAIR_RHO = [np.float32(np.float64(ECOEF[2 * j]) / np.float64(ECOEF[2 * j + 1]))
            for j in range(8)]
PAIR_EB = [ECOEF[2 * j + 1] for j in range(8)]


def _verify_gelu_chain():
    def ref(xq):
        y = np.interp(xq, _X.astype(np.float64), _Y.astype(np.float64))
        return np.where(xq > 4.0, xq, np.where(xq < -4.0, 0.0, y))

    A0 = np.arange(-16384, 16385).astype(np.float64) + BIAS
    s = np.abs(A0 * np.float64(1.0 / 64.0) + np.float64(-128.0))
    acc = (A0 - BIAS) * np.float64(1.0 / 512.0)
    acc = np.float64(LAM0) * s + acc
    acc = acc - np.float64(CONST_OFF)
    for j in range(8):
        rho, eb = PAIR_RHO[j], PAIR_EB[j]
        u = np.maximum(s, np.float64(2 * j + 1)) * np.float64(rho)
        u = np.maximum(s, np.float64(2 * j + 2)) + u
        acc = u * np.float64(eb) + acc
    w2 = acc * 256.0 + BIAS
    fr = w2 - np.floor(w2)
    assert np.abs(fr - 0.5).min() > 1e-4, "site-2 rounding tie on lattice"
    for conv in (np.trunc, np.round):
        for jit in (0.0, 0.9, -0.9):
            z2 = conv(w2 + jit)
            A2 = z2.copy()
            for _ in range(2):
                d2 = w2 - A2
                A2 = (A2 + (d2 > 0.5)) - (d2 < -0.5)
            q2 = A2 - BIAS
            y2_ref = np.round(256.0 * ref((A0 - BIAS) / 256.0))
            assert np.array_equal(q2, y2_ref), "gelu chain verification failed"


_verify_gelu_chain()



def build_nc():
    nc = bass.Bass("TRN2", target_bir_lowering=False)

    xin = nc.dram_tensor("x", [C, TOK], F32, kind="ExternalInput").ap()
    wkt = nc.dram_tensor("wkt", [256, FF], F32, kind="ExternalInput").ap()
    wv = nc.dram_tensor("wv", [FF, 256], F32, kind="ExternalInput").ap()
    w1 = nc.dram_tensor("w1", [C, FF], F32, kind="ExternalInput").ap()
    w2 = nc.dram_tensor("w2", [FF, C], F32, kind="ExternalInput").ap()
    lng_in = nc.dram_tensor("lng", [128, NC], F32, kind="ExternalInput").ap()
    lnb_in = nc.dram_tensor("lnb", [128, NC], F32, kind="ExternalInput").ap()
    b1_in = nc.dram_tensor("b1t", [128, NF], F32, kind="ExternalInput").ap()
    b2_in = nc.dram_tensor("b2t", [128, NC], F32, kind="ExternalInput").ap()
    yout = nc.dram_tensor("y", [C, TOK], F32, kind="ExternalOutput").ap()

    kth = nc.dram_tensor("kth", [256, FF], F16, kind="Internal").ap()
    wvh = nc.dram_tensor("wvh", [FF, 256], F16, kind="Internal").ap()
    w1h = nc.dram_tensor("w1h", [C, FF], F16, kind="Internal").ap()
    w2h = nc.dram_tensor("w2h", [FF, C], F16, kind="Internal").ap()

    es = contextlib.ExitStack()
    sb = lambda name, shape, dt: es.enter_context(nc.sbuf_tensor(name, shape, dt))
    ps = lambda name, shape: es.enter_context(nc.psum_tensor(name, shape, F32))
    sem = lambda name: es.enter_context(nc.semaphore(name))

    with es:
        X0 = sb("X0", [128, NC * T], F32)
        X1 = sb("X1", [128, NC * T], F32)
        TA = sb("TA", [128, NC * T], F32)
        U = sb("U", [128, NC * T], F16)
        W1B = sb("W1B", [128, 8192], F16)
        W2B = sb("W2B", [128, 8192], F16)
        RB = sb("RB", [128, 4096], F16)
        PB = sb("PB", [128, 4096], F16)
        HB = sb("HB", [128, NF * T], F16)
        WQ = sb("WQ", [128, 2048], F32)
        WQ2 = sb("WQ2", [128, 2048], F32)
        GZ = sb("GZ", [128, 2048], I32)
        GZ2 = sb("GZ2", [128, 2048], I32)
        GD = sb("GD", [128, 2048], F32)
        GS = sb("GS", [128, 2048], F32)
        GA = sb("GA", [128, 2048], F32)
        GB = sb("GB", [128, 2048], F32)
        VB = sb("VB", [128, T], F32)
        AB = sb("AB", [128, T], F32)
        BB = sb("BB", [128, T], F32)
        LNG = sb("LNG", [128, NC], F32)
        LNB = sb("LNB", [128, NC], F32)
        B1T = sb("B1T", [128, NF], F32)
        B2T = sb("B2T", [128, NC], F32)
        B1Q = sb("B1Q", [128, NF], F32)
        B2Q = sb("B2Q", [128, NC], F32)
        ONES = sb("ONES", [128, 128], F16)
        CEPS = sb("CEPS", [128, 1], F32)

        psS1 = ps("psS1", [128, T])
        psS2 = ps("psS2", [128, T])
        psC0 = ps("psC0", [128, T])
        psC1 = ps("psC1", [128, T])
        psO0 = ps("psO0", [128, T])
        psO1 = ps("psO1", [128, T])
        psH0 = ps("psH0", [128, T])
        psH1 = ps("psH1", [128, T])

        s_x = sem("s_x")
        s_w1 = sem("s_w1")
        s_w2 = sem("s_w2")
        s_su = sem("s_su")
        s_y = sem("s_y")
        s_mm = sem("s_mm")
        s_sc = sem("s_sc")
        s_vd = sem("s_vd")

        cnt = {"s_x": 0, "s_y": 0, "s_mm": 0, "s_sc": 0, "s_vd": 0,
               "s_w1": 0, "s_w2": 0, "s_su": 0}
        hmap = {"s_x": s_x, "s_y": s_y, "s_mm": s_mm, "s_sc": s_sc, "s_vd": s_vd,
                "s_w1": s_w1, "s_w2": s_w2, "s_su": s_su}

        blk = es.enter_context(nc.Block())

        def sig(inst, name, k=1):
            cnt[name] += k
            inst.then_inc(hmap[name], k)

        def waits(eng, names):
            for n in names:
                eng.wait_ge(hmap[n], cnt[n])

        X = [X0, X1]
        xlast = {"w1_user": 0, "w2_user": 0}

        def vc(vector, emit, dep=None):
            vector.wait_ge(s_vd, cnt["s_vd"] if dep is None else dep)
            sig(emit(), "s_vd")

        def sc_op(scalar, emit):
            scalar.wait_ge(s_sc, cnt["s_sc"])
            sig(emit(), "s_sc")

        def SY(f):
            blk.sync(f)

        def TE(f):
            blk.tensor(f)

        def VE(f):
            def g(vector):
                vector.wait_ge(s_vd, cnt["s_vd"])
                f(vector)
            blk.vector(g)

        def SC(f):
            def g(scalar):
                scalar.wait_ge(s_sc, cnt["s_sc"])
                f(scalar)
            blk.scalar(g)

        def startup_sync(sync):
            sig(sync.dma_start(out=LNG[:], in_=lng_in[:, :]), "s_su", 16)
            sig(sync.dma_start(out=LNB[:], in_=lnb_in[:, :]), "s_su", 16)
            sig(sync.dma_start(out=B1T[:], in_=b1_in[:, :]), "s_su", 16)
            sig(sync.dma_start(out=B2T[:], in_=b2_in[:, :]), "s_su", 16)
        SY(startup_sync)

        def startup_vec(vector):
            waits(vector, ["s_su"])
            sig(vector.memset(ONES[:], 1.0 / 1024.0), "s_vd")
            sig(vector.memset(CEPS[:], EPS), "s_vd")
            sig(vector.tensor_scalar(B1Q[:], B1T[:], 256.0, BIAS, OP.mult, OP.add), "s_vd")
            sig(vector.tensor_scalar(B2Q[:], B2T[:], 256.0, BIAS, OP.mult, OP.add), "s_vd")
        VE(startup_vec)

        def conv_round(loads, dst_sbuf_ap=None, stores=None):
            def ld(sync):
                waits(sync, ["s_vd"])
                for (ap, c0, nc_) in loads:
                    sig(sync.dma_start(out=X0[:, c0:c0 + nc_], in_=ap), "s_su", 16)
            def cv(vector):
                waits(vector, ["s_su"])
                if dst_sbuf_ap is not None:
                    sig(vector.tensor_copy(dst_sbuf_ap, X0[:, 0:src_cols]), "s_vd")
                else:
                    sig(vector.tensor_copy(RB[:, 0:src_cols], X0[:, 0:src_cols]), "s_vd")
            SY(ld)
            VE(cv)
            if stores:
                def st(sync):
                    waits(sync, ["s_vd"])
                    for (ap, c0, nc_) in stores:
                        sig(sync.dma_start(out=ap, in_=RB[:, c0:c0 + nc_]), "s_su", 16)
                SY(st)

        src_cols = 4096
        conv_round([(wkt[0:128, :], 0, 4096)], stores=[(kth[0:128, :], 0, 4096)])
        conv_round([(wkt[128:256, :], 0, 4096)], stores=[(kth[128:256, :], 0, 4096)])
        for half16 in range(2):
            j0 = 16 * half16
            conv_round([(wv[128 * (j0 + j):128 * (j0 + j + 1), :], j * 256, 256)
                        for j in range(16)],
                       stores=[(wvh[128 * (j0 + j):128 * (j0 + j + 1), :], j * 256, 256)
                               for j in range(16)])
        for c in range(NC):
            conv_round([(w1[128 * c:128 * (c + 1), :], 0, 4096)],
                       stores=[(w1h[128 * c:128 * (c + 1), :], 0, 4096)])
        for r in range(8):
            conv_round([(w2[128 * (4 * r + j):128 * (4 * r + j + 1), :], j * 1024, 1024)
                        for j in range(4)],
                       stores=[(w2h[128 * (4 * r + j):128 * (4 * r + j + 1), :], j * 1024, 1024)
                               for j in range(4)])
        startup_x_cnt = cnt["s_su"]


        def floor2(v, Bv, Z, D, S, F, nw):
            w_ = slice(0, nw)
            vc(v, lambda: v.tensor_copy(Z[:, w_], Bv[:, w_]))
            for _ in range(2):
                si = Z if _ == 0 else F
                vc(v, lambda si=si: v.tensor_tensor(D[:, w_], Bv[:, w_], si[:, w_], OP.subtract))
                vc(v, lambda si=si: v.scalar_tensor_tensor(S[:, w_], D[:, w_], 1.0, si[:, w_], OP.is_ge, OP.add))
                vc(v, lambda: v.tensor_scalar(D[:, w_], D[:, w_], 0.0, None, OP.is_lt))
                vc(v, lambda: v.tensor_tensor(F[:, w_], S[:, w_], D[:, w_], OP.subtract))

        def rhe(v, W, Z, D, S, A, Bv, nw):
            w_ = slice(0, nw)
            vc(v, lambda: v.tensor_scalar(Bv[:, w_], W[:, w_], 0.5, 0.25, OP.mult, OP.add))
            floor2(v, Bv, Z, D, S, A, nw)
            vc(v, lambda: v.tensor_scalar(Bv[:, w_], W[:, w_], -0.5, 0.25, OP.mult, OP.add))
            floor2(v, Bv, Z, D, S, W, nw)
            vc(v, lambda: v.tensor_tensor(A[:, w_], A[:, w_], W[:, w_], OP.subtract))

        def round2(v, W, Z, D, S, A):
            vc(v, lambda: v.tensor_copy(Z[:], W[:]))
            vc(v, lambda: v.tensor_tensor(D[:], W[:], Z[:], OP.subtract))
            vc(v, lambda: v.scalar_tensor_tensor(A[:], D[:], 0.5, Z[:], OP.is_gt, OP.add))
            vc(v, lambda: v.tensor_scalar(S[:], D[:], -0.5, None, OP.is_lt))
            vc(v, lambda: v.tensor_tensor(D[:], A[:], S[:], OP.subtract))
            vc(v, lambda: v.tensor_tensor(A[:], W[:], D[:], OP.subtract))
            vc(v, lambda: v.scalar_tensor_tensor(S[:], A[:], 0.5, D[:], OP.is_gt, OP.add))
            vc(v, lambda: v.tensor_scalar(W[:], A[:], -0.5, None, OP.is_lt))
            vc(v, lambda: v.tensor_tensor(A[:], S[:], W[:], OP.subtract))

        def load_x(i):
            Xb = X[i % 2]
            def f(sync):
                waits(sync, ["s_vd", "s_y"])
                for c in range(NC):
                    inst = sync.dma_start(
                        out=Xb[:, c * T:(c + 1) * T],
                        in_=xin[128 * c:128 * (c + 1), T * i:T * (i + 1)])
                    sig(inst, "s_x", 16)
            SY(f)

        def stats(src, is_ln01):
            def cast(scalar):
                waits(scalar, ["s_vd", "s_mm", "s_x", "s_su"])
                sig(scalar.activation(RB[:], src[:], AF.Identity, bias=0.0, scale=1.0), "s_sc")
                sig(scalar.activation(PB[:], src[:], AF.Square, bias=0.0, scale=1.0), "s_sc")
            SC(cast)

            def mm(tensor):
                waits(tensor, ["s_sc"])
                for c in range(NC):
                    tensor.matmul(psS1[:], ONES[:], RB[:, c * T:(c + 1) * T],
                                  start=(c == 0), stop=(c == NC - 1))
                for c in range(NC):
                    inst = tensor.matmul(psS2[:], ONES[:], PB[:, c * T:(c + 1) * T],
                                         start=(c == 0), stop=(c == NC - 1))
                sig(inst, "s_mm")
            TE(mm)

            def chain(vector):
                waits(vector, ["s_mm"])
                vc(vector, lambda: vector.tensor_copy(GS[:, 0:T], psS1[:]))
                vc(vector, lambda: vector.tensor_mul(VB[:], GS[:, 0:T], GS[:, 0:T]))
                vc(vector, lambda: vector.tensor_tensor(VB[:], psS2[:], VB[:], OP.subtract))
            VE(chain)

            def chain_sc(scalar):
                waits(scalar, ["s_vd"])
                inst = scalar.activation(GS[:, T:2 * T], VB[:], AF.Sqrt, bias=CEPS[:], scale=1.0)
                sig(inst, "s_sc")
            SC(chain_sc)

            def recip1(vector):
                waits(vector, ["s_sc"])
                vc(vector, lambda: vector.reciprocal(BB[:], GS[:, T:2 * T]))
            VE(recip1)

            if is_ln01:
                def chain2(vector):
                    vc(vector, lambda: vector.tensor_scalar(GS[:, T:2 * T], VB[:], 1.0 + EPS, EPS * EPS,
                                                            OP.mult, OP.add))
                VE(chain2)

                def chain2_sc(scalar):
                    waits(scalar, ["s_vd"])
                    inst = scalar.activation(GS[:, T:2 * T], GS[:, T:2 * T], AF.Sqrt, bias=0.0, scale=1.0)
                    sig(inst, "s_sc")
                SC(chain2_sc)

                def recip2(vector):
                    waits(vector, ["s_sc"])
                    vc(vector, lambda: vector.reciprocal(AB[:], GS[:, T:2 * T]))
                VE(recip2)

        load_x(0)
        for i in range(NT):
            Xb = X[i % 2]

            stats(Xb, is_ln01=True)

            def apply01(vector, Xb=Xb):
                v = vector
                for c in range(NC):
                    sig(v.tensor_tensor(TA[:, c * T:(c + 1) * T],
                                        Xb[:, c * T:(c + 1) * T], psS1[:], OP.subtract), "s_vd")
                v.wait_ge(s_vd, cnt["s_vd"])
                for c in range(NC):
                    sig(v.tensor_mul(U[:, c * T:(c + 1) * T],
                                     TA[:, c * T:(c + 1) * T], AB[:]), "s_vd")
                v.wait_ge(s_vd, cnt["s_vd"])
                for c in range(NC):
                    sig(v.tensor_scalar(U[:, c * T:(c + 1) * T], U[:, c * T:(c + 1) * T],
                                        LNG[:, c:c + 1], LNB[:, c:c + 1],
                                        OP.mult, OP.add), "s_vd")
                v.wait_ge(s_vd, cnt["s_vd"])
                for c in range(NC):
                    sig(v.tensor_mul(TA[:, c * T:(c + 1) * T],
                                     TA[:, c * T:(c + 1) * T], BB[:]), "s_vd")
            VE(apply01)

            att_cnt = {}

            def attn_w_load(sync):
                sync.wait_ge(s_su, startup_x_cnt)
                sync.wait_ge(s_mm, xlast["w1_user"])
                sig(sync.dma_start(out=W1B[:, 0:4096], in_=kth[0:128, :]), "s_w1", 16)
                sig(sync.dma_start(out=W1B[:, 4096:8192], in_=kth[128:256, :]), "s_w1", 16)
                att_cnt["kt"] = cnt["s_w1"]
                sync.wait_ge(s_mm, xlast["w2_user"])
                for j in range(NF):
                    sig(sync.dma_start(out=W2B[:, j * 256:(j + 1) * 256],
                                       in_=wvh[128 * j:128 * (j + 1), :]), "s_w2", 16)
                att_cnt["v"] = cnt["s_w2"]
            SY(attn_w_load)

            ps_sc = [psC0, psC1, psH0, psH1]
            for h in range(4):
                relu_cnt = []
                p_cnt = []
                out_cnt = []
                mm_cnt = []

                def scores_hg(tensor, h=h, hg=None, relu_cnt=relu_cnt):
                    if h == 0 and hg == 0:
                        waits(tensor, ["s_vd", "s_sc"])
                        tensor.wait_ge(s_w1, att_cnt["kt"])
                    for jj in range(4 * hg, 4 * hg + 4):
                        pc = ps_sc[jj % 4]
                        if jj >= 4:
                            tensor.wait_ge(s_sc, relu_cnt[jj - 4])
                        tensor.matmul(pc[:], W1B[:, jj * 128:(jj + 1) * 128],
                                      U[:, (2 * h) * T:(2 * h + 1) * T],
                                      start=True, stop=False, skip_group_check=True)
                        inst = tensor.matmul(pc[:], W1B[:, 4096 + jj * 128:4096 + (jj + 1) * 128],
                                             U[:, (2 * h + 1) * T:(2 * h + 2) * T],
                                             start=False, stop=True, skip_group_check=True)
                        sig(inst, "s_mm")
                        mm_cnt.append(cnt["s_mm"])
                        xlast["w1_user"] = cnt["s_mm"]

                def relu_hg(scalar, h=h, hg=None, mm_cnt=mm_cnt, p_cnt=p_cnt,
                            relu_cnt=relu_cnt):
                    if hg >= 2:
                        scalar.wait_ge(s_vd, p_cnt[hg - 2])
                    for jj in range(4 * hg, 4 * hg + 4):
                        pc = ps_sc[jj % 4]
                        scalar.wait_ge(s_mm, mm_cnt[jj])
                        col = (hg % 2) * 2048 + (jj % 4) * T
                        inst = scalar.activation(RB[:, col:col + T], pc[:],
                                                 AF.Relu, bias=0.0, scale=1.0)
                        sig(inst, "s_sc")
                        relu_cnt.append(cnt["s_sc"])

                def p_hg(vector, h=h, hg=None, relu_cnt=relu_cnt, out_cnt=out_cnt,
                         p_cnt=p_cnt):
                    vector.wait_ge(s_sc, relu_cnt[4 * hg + 3])
                    if hg >= 2:
                        vector.wait_ge(s_mm, out_cnt[hg - 2])
                    half = (hg % 2) * 2048
                    sig(vector.tensor_mul(PB[:, half:half + 2048],
                                          RB[:, half:half + 2048],
                                          RB[:, half:half + 2048]), "s_vd")
                    p_cnt.append(cnt["s_vd"])

                def out_hg(tensor, h=h, hg=None, p_cnt=p_cnt, out_cnt=out_cnt):
                    tensor.wait_ge(s_vd, p_cnt[hg])
                    if h == 0 and hg == 0:
                        tensor.wait_ge(s_w2, att_cnt["v"])
                    half = (hg % 2) * 2048
                    for k in range(4):
                        jj = 4 * hg + k
                        tensor.matmul(psO0[:], W2B[:, jj * 256:jj * 256 + 128],
                                      PB[:, half + k * T:half + (k + 1) * T],
                                      start=(jj == 0), stop=(jj == 31),
                                      skip_group_check=True)
                        inst = tensor.matmul(psO1[:], W2B[:, jj * 256 + 128:jj * 256 + 256],
                                             PB[:, half + k * T:half + (k + 1) * T],
                                             start=(jj == 0), stop=(jj == 31),
                                             skip_group_check=True)
                    sig(inst, "s_mm")
                    out_cnt.append(cnt["s_mm"])
                    xlast["w2_user"] = cnt["s_mm"]

                for hg in range(8):
                    TE(lambda t, f=scores_hg, hg=hg: f(t, hg=hg))
                    SC(lambda s, f=relu_hg, hg=hg: f(s, hg=hg))
                    VE(lambda v, f=p_hg, hg=hg: f(v, hg=hg))
                    if hg >= 1:
                        TE(lambda t, f=out_hg, hg=hg - 1: f(t, hg=hg))
                TE(lambda t, f=out_hg: f(t, hg=7))

                def a_add(vector, h=h):
                    waits(vector, ["s_mm"])
                    sig(vector.tensor_tensor(TA[:, (2 * h) * T:(2 * h + 1) * T],
                                             TA[:, (2 * h) * T:(2 * h + 1) * T],
                                             psO0[:], OP.add), "s_vd")
                    sig(vector.tensor_tensor(TA[:, (2 * h + 1) * T:(2 * h + 2) * T],
                                             TA[:, (2 * h + 1) * T:(2 * h + 2) * T],
                                             psO1[:], OP.add), "s_vd")
                VE(a_add)

            stats(TA, is_ln01=False)

            def apply2(vector, Xb=Xb):
                v = vector
                for c in range(NC):
                    sig(v.tensor_tensor(TA[:, c * T:(c + 1) * T],
                                        TA[:, c * T:(c + 1) * T], psS1[:], OP.subtract), "s_vd")
                v.wait_ge(s_vd, cnt["s_vd"])
                for c in range(NC):
                    sig(v.tensor_mul(TA[:, c * T:(c + 1) * T],
                                     TA[:, c * T:(c + 1) * T], BB[:]), "s_vd")
                v.wait_ge(s_vd, cnt["s_vd"])
                for c in range(NC):
                    sig(v.tensor_add(Xb[:, c * T:(c + 1) * T],
                                     Xb[:, c * T:(c + 1) * T], TA[:, c * T:(c + 1) * T]), "s_vd")
            VE(apply2)

            stats(Xb, is_ln01=False)

            def apply3(vector, Xb=Xb):
                v = vector
                for c in range(NC):
                    sig(v.tensor_tensor(TA[:, c * T:(c + 1) * T],
                                        Xb[:, c * T:(c + 1) * T], psS1[:], OP.subtract), "s_vd")
                v.wait_ge(s_vd, cnt["s_vd"])
                for c in range(NC):
                    sig(v.tensor_mul(U[:, c * T:(c + 1) * T],
                                     TA[:, c * T:(c + 1) * T], BB[:]), "s_vd")
            VE(apply3)

            h_mm_cnt = {}
            h_ev_cnt = {}
            gelu_end = {}
            w1_cnt = {}
            for q in range(4):
                def w1_load(sync, q=q):
                    sync.wait_ge(s_su, startup_x_cnt)
                    if q >= 1:
                        sync.wait_ge(s_mm, h_mm_cnt[(q - 1, 1, 3)])
                    else:
                        sync.wait_ge(s_mm, xlast["w1_user"])
                    for c in range(NC):
                        inst = sync.dma_start(
                            out=W1B[:, c * 1024:(c + 1) * 1024],
                            in_=w1h[128 * c:128 * (c + 1), 1024 * q:1024 * (q + 1)])
                        sig(inst, "s_w1", 16)
                    w1_cnt[q] = cnt["s_w1"]
                SY(w1_load)

                for half in range(2):
                    gb = 2 * q + half

                    for k in range(4):
                        def h_mm_k(tensor, q=q, half=half, k=k):
                            if q == 0 and half == 0 and k == 0:
                                waits(tensor, ["s_vd", "s_sc"])
                            tensor.wait_ge(s_w1, w1_cnt[q])
                            jf = 4 * half + k
                            ph = [psH0, psH1][k % 2]
                            prev = ((q, half, k - 2) if k >= 2 else
                                    ((q, half - 1, k + 2) if half else (q - 1, 1, k + 2)))
                            if prev in h_ev_cnt:
                                tensor.wait_ge(s_sc, h_ev_cnt[prev])
                            for c in range(NC):
                                inst = tensor.matmul(
                                    ph[:], W1B[:, c * 1024 + jf * 128:c * 1024 + (jf + 1) * 128],
                                    U[:, c * T:(c + 1) * T],
                                    start=(c == 0), stop=(c == NC - 1))
                            sig(inst, "s_mm")
                            h_mm_cnt[(q, half, k)] = cnt["s_mm"]
                            xlast["w1_user"] = cnt["s_mm"]
                        TE(h_mm_k)

                        def h_evac_k(scalar, q=q, half=half, gb=gb, k=k):
                            if k == 0 and gb >= 2:
                                scalar.wait_ge(s_vd, gelu_end[gb - 2])
                            Wb = [WQ, WQ2][gb % 2]
                            Zb = [GZ, GZ2][gb % 2]
                            jf = 4 * half + k
                            fg = 8 * q + jf
                            ph = [psH0, psH1][k % 2]
                            scalar.wait_ge(s_mm, h_mm_cnt[(q, half, k)])
                            inst = scalar.activation(Wb[:, k * T:(k + 1) * T], ph[:],
                                                     AF.Identity,
                                                     bias=B1Q[:, fg:fg + 1], scale=256.0)
                            sig(inst, "s_sc")
                            inst = scalar.activation(Zb[:, k * T:(k + 1) * T], ph[:],
                                                     AF.Identity,
                                                     bias=B1Q[:, fg:fg + 1], scale=256.0)
                            sig(inst, "s_sc")
                            h_ev_cnt[(q, half, k)] = cnt["s_sc"]
                        SC(h_evac_k)

                    def gelu(vector, q=q, half=half, gb=gb):
                        Wb = [WQ, WQ2][gb % 2]
                        Zb = [GZ, GZ2][gb % 2]
                        vector.wait_ge(s_sc, h_ev_cnt[(q, half, 3)])
                        fg0 = 8 * q + 4 * half
                        cols = slice(fg0 * T, (fg0 + 4) * T)
                        v = vector
                        vc(v, lambda: v.tensor_tensor(GD[:], Wb[:], Zb[:], OP.subtract))
                        vc(v, lambda: v.scalar_tensor_tensor(GA[:], GD[:], 0.5, Zb[:], OP.is_gt, OP.add))
                        vc(v, lambda: v.tensor_scalar(GS[:], GD[:], -0.5, None, OP.is_lt))
                        vc(v, lambda: v.tensor_tensor(GD[:], GA[:], GS[:], OP.subtract))
                        vc(v, lambda: v.tensor_tensor(GA[:], Wb[:], GD[:], OP.subtract))
                        vc(v, lambda: v.scalar_tensor_tensor(GS[:], GA[:], 0.5, GD[:], OP.is_gt, OP.add))
                        vc(v, lambda: v.tensor_scalar(Wb[:], GA[:], -0.5, None, OP.is_lt))
                        vc(v, lambda: v.tensor_tensor(GA[:], GS[:], Wb[:], OP.subtract))
                        vc(v, lambda: v.tensor_scalar(GS[:], GA[:], BIAS, 1.0 / 64.0,
                                                      OP.subtract, OP.mult))
                        vc(v, lambda: v.scalar_tensor_tensor(GS[:], GS[:], -1.0, GS[:],
                                                             OP.mult, OP.max))
                        vc(v, lambda: v.tensor_scalar(GD[:], GA[:], BIAS, 1.0 / 512.0,
                                                      OP.subtract, OP.mult))
                        vc(v, lambda: v.scalar_tensor_tensor(GB[:], GS[:], float(LAM0), GD[:],
                                                             OP.mult, OP.add))
                        vc(v, lambda: v.tensor_scalar(GB[:], GB[:], float(CONST_OFF), None,
                                                      OP.subtract))
                        mbuf = [GD, Wb, GA]
                        ts_c = [None] * 8
                        m_c = [None] * 8
                        acc_c = [None] * 9
                        acc_c[0] = cnt["s_vd"]
                        base = cnt["s_vd"]

                        def e_ts(j):
                            mb = mbuf[j % 3]
                            dep = acc_c[j - 2] if j >= 3 else base
                            vc(v, lambda: v.tensor_scalar(
                                mb[:], GS[:], float(2 * j + 1), float(PAIR_RHO[j]),
                                OP.max, OP.mult), dep=dep)
                            ts_c[j] = cnt["s_vd"]

                        def e_m(j):
                            mb = mbuf[j % 3]
                            vc(v, lambda: v.scalar_tensor_tensor(
                                mb[:], GS[:], float(2 * j + 2), mb[:], OP.max, OP.add),
                                dep=ts_c[j])
                            m_c[j] = cnt["s_vd"]

                        def e_acc(j):
                            mb = mbuf[j % 3]
                            vc(v, lambda: v.scalar_tensor_tensor(
                                GB[:], mb[:], float(PAIR_EB[j]), GB[:], OP.mult, OP.add),
                                dep=max(m_c[j], acc_c[j]))
                            acc_c[j + 1] = cnt["s_vd"]

                        e_ts(0); e_ts(1); e_m(0); e_ts(2); e_m(1); e_acc(0)
                        for j in range(1, 6):
                            e_m(j + 1); e_acc(j); e_ts(j + 2)
                        e_m(7); e_acc(6); e_acc(7)
                        vc(v, lambda: v.tensor_scalar(Wb[:], GB[:], 256.0, BIAS,
                                                      OP.mult, OP.add))
                        round2(v, Wb, Zb, GD, GS, GA)
                        vc(v, lambda: v.tensor_scalar(HB[:, cols], GA[:], BIAS, 1.0 / 256.0,
                                                      OP.subtract, OP.mult))
                        gelu_end[gb] = cnt["s_vd"]
                    VE(gelu)

            o_mm_cnt = {}
            o_ev_cnt = {}
            o_rnd_end = {}
            for Q in range(4):
                def w2_load(sync, Q=Q):
                    sync.wait_ge(s_su, startup_x_cnt)
                    if Q >= 1:
                        sync.wait_ge(s_mm, o_mm_cnt[(Q - 1, 1)])
                    else:
                        sync.wait_ge(s_mm, xlast["w2_user"])
                    for j in range(NF):
                        inst = sync.dma_start(
                            out=W2B[:, j * 256:(j + 1) * 256],
                            in_=w2h[128 * j:128 * (j + 1), 256 * Q:256 * (Q + 1)])
                        sig(inst, "s_w2", 16)
                    o_mm_cnt[("w2x", Q)] = cnt["s_w2"]
                SY(w2_load)

                def o_mm(tensor, Q=Q):
                    if Q == 0:
                        waits(tensor, ["s_vd", "s_sc"])
                    tensor.wait_ge(s_w2, o_mm_cnt[("w2x", Q)])
                    for dd in range(2):
                        ph = [psH0, psH1][dd]
                        if (Q - 1, dd) in o_ev_cnt:
                            tensor.wait_ge(s_sc, o_ev_cnt[(Q - 1, dd)])
                        for j in range(NF):
                            inst = tensor.matmul(
                                ph[:], W2B[:, j * 256 + dd * 128:j * 256 + (dd + 1) * 128],
                                HB[:, j * T:(j + 1) * T],
                                start=(j == 0), stop=(j == NF - 1))
                        sig(inst, "s_mm")
                        o_mm_cnt[(Q, dd)] = cnt["s_mm"]
                        xlast["w2_user"] = cnt["s_mm"]
                TE(o_mm)

                def o_evac(scalar, Q=Q):
                    if Q >= 2:
                        scalar.wait_ge(s_vd, o_rnd_end[Q - 2])
                    Wb = [WQ, WQ2][Q % 2]
                    for dd in range(2):
                        cq = 2 * Q + dd
                        ph = [psH0, psH1][dd]
                        scalar.wait_ge(s_mm, o_mm_cnt[(Q, dd)])
                        inst = scalar.activation(Wb[:, dd * T:(dd + 1) * T], ph[:],
                                                 AF.Identity,
                                                 bias=B2Q[:, cq:cq + 1], scale=256.0)
                        sig(inst, "s_sc")
                        o_ev_cnt[(Q, dd)] = cnt["s_sc"]
                SC(o_evac)

                def o_round(vector, Q=Q):
                    vector.wait_ge(s_sc, o_ev_cnt[(Q, 1)])
                    Wb = [WQ, WQ2][Q % 2]
                    v = vector
                    rhe(v, Wb, GZ, GD, GS, GA, GB, 2 * T)
                    vc(v, lambda: v.tensor_scalar(TA[:, 2 * Q * T:(2 * Q + 2) * T],
                                                  GA[:, 0:2 * T], BIAS, 1.0 / 256.0,
                                                  OP.subtract, OP.mult))
                    o_rnd_end[Q] = cnt["s_vd"]
                VE(o_round)

            stats(TA, is_ln01=False)

            def apply4(vector, Xb=Xb):
                v = vector
                for c in range(NC):
                    sig(v.tensor_tensor(TA[:, c * T:(c + 1) * T],
                                        TA[:, c * T:(c + 1) * T], psS1[:], OP.subtract), "s_vd")
                v.wait_ge(s_vd, cnt["s_vd"])
                for c in range(NC):
                    sig(v.tensor_mul(TA[:, c * T:(c + 1) * T],
                                     TA[:, c * T:(c + 1) * T], BB[:]), "s_vd")
                v.wait_ge(s_vd, cnt["s_vd"])
                for c in range(NC):
                    sig(v.tensor_add(Xb[:, c * T:(c + 1) * T],
                                     Xb[:, c * T:(c + 1) * T], TA[:, c * T:(c + 1) * T]), "s_vd")
            VE(apply4)

            def store(sync, i=i, Xb=Xb):
                waits(sync, ["s_vd"])
                for c in range(NC):
                    inst = sync.dma_start(
                        out=yout[128 * c:128 * (c + 1), T * i:T * (i + 1)],
                        in_=Xb[:, c * T:(c + 1) * T])
                    sig(inst, "s_y", 16)
            SY(store)

            if i + 1 < NT:
                load_x(i + 1)

        def fin(sync):
            sync.wait_ge(s_y, cnt["s_y"])
        SY(fin)

    return nc



_NC_CACHE = {}


def kernel(L0_attn_residual, L0_attn_w_fc, L0_attn_w_proj, L0_attn_ln_g,
           L0_attn_ln_b, L0_ffn_w1, L0_ffn_b1, L0_ffn_w2, L0_ffn_b2):
    x = np.ascontiguousarray(np.asarray(L0_attn_residual, dtype=np.float32))
    w_fc = np.asarray(L0_attn_w_fc, dtype=np.float32)
    w_proj = np.asarray(L0_attn_w_proj, dtype=np.float32)
    ln_g = np.asarray(L0_attn_ln_g, dtype=np.float32)
    ln_b = np.asarray(L0_attn_ln_b, dtype=np.float32)
    w1 = np.ascontiguousarray(np.asarray(L0_ffn_w1, dtype=np.float32))
    b1 = np.asarray(L0_ffn_b1, dtype=np.float32)
    w2 = np.ascontiguousarray(np.asarray(L0_ffn_w2, dtype=np.float32))
    b2 = np.asarray(L0_ffn_b2, dtype=np.float32)

    wkt = np.ascontiguousarray(w_fc.reshape(FF, 256).T)
    wv = np.ascontiguousarray(np.ascontiguousarray(w_proj.T).reshape(FF, 256))
    lng = np.ascontiguousarray(ln_g.reshape(NC, 128).T)
    lnb = np.ascontiguousarray(ln_b.reshape(NC, 128).T)
    b1t = np.ascontiguousarray(b1.reshape(NF, 128).T)
    b2t = np.ascontiguousarray(b2.reshape(NC, 128).T)

    xf = x.reshape(B * TT, C)
    in_maps = []
    for core in range(N_CORES):
        xs = np.ascontiguousarray(xf[core * TOK:(core + 1) * TOK, :].T)
        in_maps.append(dict(x=xs, wkt=wkt, wv=wv, w1=w1, w2=w2,
                            lng=lng, lnb=lnb, b1t=b1t, b2t=b2t))

    if "nc" not in _NC_CACHE:
        _NC_CACHE["nc"] = build_nc()
    nc = _NC_CACHE["nc"]
    res = run_bass_kernel_spmd(nc, in_maps, core_ids=list(range(N_CORES)))

    out = np.empty((B * TT, C), dtype=np.float32)
    for core in range(N_CORES):
        out[core * TOK:(core + 1) * TOK, :] = res.results[core]["y"].T
    return out.reshape(B, TT, C)


# version 17
# speedup vs baseline: 3.9110x; 1.0328x over previous
import contextlib
import struct
import numpy as np
import ml_dtypes

import concourse.bass as bass
import concourse.mybir as mybir
from concourse.bass_utils import run_bass_kernel_spmd

F32 = mybir.dt.float32
F16 = mybir.dt.float16
I32 = mybir.dt.int32
AF = mybir.ActivationFunctionType
OP = mybir.AluOpType

N_CORES = 8
B, TT, C = 8, 8192, 1024
TOK = B * TT // N_CORES
T = 512
NT = TOK // T
NC = C // 128
FF = 4096
NF = FF // 128
EPS = 1e-5
BIAS = 8192.0

_YBITS = [0xb904c000, 0xb9adbb00, 0xba556b00, 0xbaf5ce80, 0xbb84b340, 0xbc064180,
          0xbc7e5920, 0xbce15244, 0xbd3a5e80, 0xbd8f926e, 0xbdcd3b48, 0xbe073b54,
          0xbe227686, 0xbe2e0cbe, 0xbe1df8a2, 0xbdcd765d, 0x00000000, 0x3e1944d2,
          0x3eb103af, 0x3f147cd0, 0x3f57625e, 0x3f8f1895, 0x3fb32c4c, 0x3fd706d9,
          0x3ffa2d0c, 0x400e3d5c, 0x401f01a7, 0x402f79be, 0x403fbda6, 0x404fe146,
          0x405ff2aa, 0x406ffa93, 0x407ffded]
_Y = np.array([struct.unpack('<f', struct.pack('<I', b))[0] for b in _YBITS],
              dtype=np.float32)
_X = np.linspace(-4, 4, 33).astype(np.float32)


def _gelu_consts():
    E = _Y.astype(np.float64)[16:] - _X.astype(np.float64)[16:] / 2.0
    K1 = np.diff(E)
    lam0 = np.float32(K1[0])
    e = np.diff(np.concatenate([K1, [0.125]])).astype(np.float32)
    const_off = np.float32(np.sum(e.astype(np.float64) * np.arange(1, 17)))
    return lam0, e, const_off


LAM0, ECOEF, CONST_OFF = _gelu_consts()
PAIR_RHO = [np.float32(np.float64(ECOEF[2 * j]) / np.float64(ECOEF[2 * j + 1]))
            for j in range(8)]
PAIR_EB = [ECOEF[2 * j + 1] for j in range(8)]


def _verify_gelu_chain():
    def ref(xq):
        y = np.interp(xq, _X.astype(np.float64), _Y.astype(np.float64))
        return np.where(xq > 4.0, xq, np.where(xq < -4.0, 0.0, y))

    A0 = np.arange(-16384, 16385).astype(np.float64) + BIAS
    s = np.abs(A0 * np.float64(1.0 / 64.0) + np.float64(-128.0))
    acc = (A0 - BIAS) * np.float64(1.0 / 512.0)
    acc = np.float64(LAM0) * s + acc
    acc = acc - np.float64(CONST_OFF)
    for j in range(8):
        rho, eb = PAIR_RHO[j], PAIR_EB[j]
        u = np.maximum(s, np.float64(2 * j + 1)) * np.float64(rho)
        u = np.maximum(s, np.float64(2 * j + 2)) + u
        acc = u * np.float64(eb) + acc
    w2 = acc * 256.0 + BIAS
    fr = w2 - np.floor(w2)
    assert np.abs(fr - 0.5).min() > 1e-4, "site-2 rounding tie on lattice"
    for conv in (np.trunc, np.round):
        for jit in (0.0, 0.9, -0.9):
            z2 = conv(w2 + jit)
            A2 = z2.copy()
            for _ in range(2):
                d2 = w2 - A2
                A2 = (A2 + (d2 > 0.5)) - (d2 < -0.5)
            q2 = A2 - BIAS
            y2_ref = np.round(256.0 * ref((A0 - BIAS) / 256.0))
            assert np.array_equal(q2, y2_ref), "gelu chain verification failed"


_verify_gelu_chain()



def build_nc():
    nc = bass.Bass("TRN2", target_bir_lowering=False)

    xin = nc.dram_tensor("x", [C, TOK], F32, kind="ExternalInput").ap()
    wkt = nc.dram_tensor("wkt", [256, FF], F32, kind="ExternalInput").ap()
    wv = nc.dram_tensor("wv", [FF, 256], F32, kind="ExternalInput").ap()
    w1 = nc.dram_tensor("w1", [C, FF], F32, kind="ExternalInput").ap()
    w2 = nc.dram_tensor("w2", [FF, C], F32, kind="ExternalInput").ap()
    lng_in = nc.dram_tensor("lng", [128, NC], F32, kind="ExternalInput").ap()
    lnb_in = nc.dram_tensor("lnb", [128, NC], F32, kind="ExternalInput").ap()
    b1_in = nc.dram_tensor("b1t", [128, NF], F32, kind="ExternalInput").ap()
    b2_in = nc.dram_tensor("b2t", [128, NC], F32, kind="ExternalInput").ap()
    yout = nc.dram_tensor("y", [C, TOK], F32, kind="ExternalOutput").ap()

    kth = nc.dram_tensor("kth", [256, FF], F16, kind="Internal").ap()
    wvh = nc.dram_tensor("wvh", [FF, 256], F16, kind="Internal").ap()
    w1h = nc.dram_tensor("w1h", [C, FF], F16, kind="Internal").ap()
    w2h = nc.dram_tensor("w2h", [FF, C], F16, kind="Internal").ap()

    es = contextlib.ExitStack()
    sb = lambda name, shape, dt: es.enter_context(nc.sbuf_tensor(name, shape, dt))
    ps = lambda name, shape: es.enter_context(nc.psum_tensor(name, shape, F32))
    sem = lambda name: es.enter_context(nc.semaphore(name))

    with es:
        X0 = sb("X0", [128, NC * T], F32)
        X1 = sb("X1", [128, NC * T], F32)
        TA = sb("TA", [128, NC * T], F32)
        U = sb("U", [128, NC * T], F16)
        W1B = sb("W1B", [128, 8192], F16)
        W2B = sb("W2B", [128, 8192], F16)
        RB = sb("RB", [128, 4096], F16)
        PB = sb("PB", [128, 4096], F16)
        HB = sb("HB", [128, NF * T], F16)
        WQ = sb("WQ", [128, 2048], F32)
        WQ2 = sb("WQ2", [128, 2048], F32)
        GZ = sb("GZ", [128, 2048], I32)
        GZ2 = sb("GZ2", [128, 2048], I32)
        GD = sb("GD", [128, 2048], F32)
        GS = sb("GS", [128, 2048], F32)
        GA = sb("GA", [128, 2048], F32)
        GB = sb("GB", [128, 2048], F32)
        VB = sb("VB", [128, T], F32)
        AB = sb("AB", [128, T], F32)
        BB = sb("BB", [128, T], F32)
        LNG = sb("LNG", [128, NC], F32)
        LNB = sb("LNB", [128, NC], F32)
        B1T = sb("B1T", [128, NF], F32)
        B2T = sb("B2T", [128, NC], F32)
        B1Q = sb("B1Q", [128, NF], F32)
        B2Q = sb("B2Q", [128, NC], F32)
        ONES = sb("ONES", [128, 128], F16)
        CEPS = sb("CEPS", [128, 1], F32)

        psS1 = ps("psS1", [128, T])
        psS2 = ps("psS2", [128, T])
        psC0 = ps("psC0", [128, T])
        psC1 = ps("psC1", [128, T])
        psO0 = ps("psO0", [128, T])
        psO1 = ps("psO1", [128, T])
        psH0 = ps("psH0", [128, T])
        psH1 = ps("psH1", [128, T])

        s_x = sem("s_x")
        s_w1 = sem("s_w1")
        s_w2 = sem("s_w2")
        s_su = sem("s_su")
        s_y = sem("s_y")
        s_mm = sem("s_mm")
        s_sc = sem("s_sc")
        s_vd = sem("s_vd")

        cnt = {"s_x": 0, "s_y": 0, "s_mm": 0, "s_sc": 0, "s_vd": 0,
               "s_w1": 0, "s_w2": 0, "s_su": 0}
        hmap = {"s_x": s_x, "s_y": s_y, "s_mm": s_mm, "s_sc": s_sc, "s_vd": s_vd,
                "s_w1": s_w1, "s_w2": s_w2, "s_su": s_su}

        blk = es.enter_context(nc.Block())

        def sig(inst, name, k=1):
            cnt[name] += k
            inst.then_inc(hmap[name], k)

        def waits(eng, names):
            for n in names:
                eng.wait_ge(hmap[n], cnt[n])

        X = [X0, X1]
        xlast = {"w1_user": 0, "w2_user": 0}

        def vc(vector, emit, dep=None):
            vector.wait_ge(s_vd, cnt["s_vd"] if dep is None else dep)
            sig(emit(), "s_vd")

        def sc_op(scalar, emit):
            scalar.wait_ge(s_sc, cnt["s_sc"])
            sig(emit(), "s_sc")

        def SY(f):
            blk.sync(f)

        def TE(f):
            blk.tensor(f)

        def VE(f):
            def g(vector):
                vector.wait_ge(s_vd, cnt["s_vd"])
                f(vector)
            blk.vector(g)

        def SC(f):
            def g(scalar):
                scalar.wait_ge(s_sc, cnt["s_sc"])
                f(scalar)
            blk.scalar(g)

        def startup_sync(sync):
            sig(sync.dma_start(out=LNG[:], in_=lng_in[:, :]), "s_su", 16)
            sig(sync.dma_start(out=LNB[:], in_=lnb_in[:, :]), "s_su", 16)
            sig(sync.dma_start(out=B1T[:], in_=b1_in[:, :]), "s_su", 16)
            sig(sync.dma_start(out=B2T[:], in_=b2_in[:, :]), "s_su", 16)
        SY(startup_sync)

        def startup_vec(vector):
            waits(vector, ["s_su"])
            sig(vector.memset(ONES[:], 1.0 / 1024.0), "s_vd")
            sig(vector.memset(CEPS[:], EPS), "s_vd")
            sig(vector.tensor_scalar(B1Q[:], B1T[:], 256.0, BIAS, OP.mult, OP.add), "s_vd")
            sig(vector.tensor_scalar(B2Q[:], B2T[:], 256.0, BIAS, OP.mult, OP.add), "s_vd")
        VE(startup_vec)

        def conv_round(loads, dst_sbuf_ap=None, stores=None):
            def ld(sync):
                waits(sync, ["s_vd"])
                for (ap, c0, nc_) in loads:
                    sig(sync.dma_start(out=X0[:, c0:c0 + nc_], in_=ap), "s_su", 16)
            def cv(vector):
                waits(vector, ["s_su"])
                if dst_sbuf_ap is not None:
                    sig(vector.tensor_copy(dst_sbuf_ap, X0[:, 0:src_cols]), "s_vd")
                else:
                    sig(vector.tensor_copy(RB[:, 0:src_cols], X0[:, 0:src_cols]), "s_vd")
            SY(ld)
            VE(cv)
            if stores:
                def st(sync):
                    waits(sync, ["s_vd"])
                    for (ap, c0, nc_) in stores:
                        sig(sync.dma_start(out=ap, in_=RB[:, c0:c0 + nc_]), "s_su", 16)
                SY(st)

        src_cols = 4096
        conv_round([(wkt[0:128, :], 0, 4096)], stores=[(kth[0:128, :], 0, 4096)])
        conv_round([(wkt[128:256, :], 0, 4096)], stores=[(kth[128:256, :], 0, 4096)])
        for half16 in range(2):
            j0 = 16 * half16
            conv_round([(wv[128 * (j0 + j):128 * (j0 + j + 1), :], j * 256, 256)
                        for j in range(16)],
                       stores=[(wvh[128 * (j0 + j):128 * (j0 + j + 1), :], j * 256, 256)
                               for j in range(16)])
        for c in range(NC):
            conv_round([(w1[128 * c:128 * (c + 1), :], 0, 4096)],
                       stores=[(w1h[128 * c:128 * (c + 1), :], 0, 4096)])
        for r in range(8):
            conv_round([(w2[128 * (4 * r + j):128 * (4 * r + j + 1), :], j * 1024, 1024)
                        for j in range(4)],
                       stores=[(w2h[128 * (4 * r + j):128 * (4 * r + j + 1), :], j * 1024, 1024)
                               for j in range(4)])
        startup_x_cnt = cnt["s_su"]


        def floor2(v, Bv, Z, D, S, F, nw):
            w_ = slice(0, nw)
            vc(v, lambda: v.tensor_copy(Z[:, w_], Bv[:, w_]))
            for _ in range(2):
                si = Z if _ == 0 else F
                vc(v, lambda si=si: v.tensor_tensor(D[:, w_], Bv[:, w_], si[:, w_], OP.subtract))
                vc(v, lambda si=si: v.scalar_tensor_tensor(S[:, w_], D[:, w_], 1.0, si[:, w_], OP.is_ge, OP.add))
                vc(v, lambda: v.tensor_scalar(D[:, w_], D[:, w_], 0.0, None, OP.is_lt))
                vc(v, lambda: v.tensor_tensor(F[:, w_], S[:, w_], D[:, w_], OP.subtract))

        def rhe(v, W, Z, D, S, A, Bv, nw):
            w_ = slice(0, nw)
            vc(v, lambda: v.tensor_scalar(Bv[:, w_], W[:, w_], 0.5, 0.25, OP.mult, OP.add))
            floor2(v, Bv, Z, D, S, A, nw)
            vc(v, lambda: v.tensor_scalar(Bv[:, w_], W[:, w_], -0.5, 0.25, OP.mult, OP.add))
            floor2(v, Bv, Z, D, S, W, nw)
            vc(v, lambda: v.tensor_tensor(A[:, w_], A[:, w_], W[:, w_], OP.subtract))

        def round2(v, W, Z, D, S, A):
            vc(v, lambda: v.tensor_copy(Z[:], W[:]))
            vc(v, lambda: v.tensor_tensor(D[:], W[:], Z[:], OP.subtract))
            vc(v, lambda: v.scalar_tensor_tensor(A[:], D[:], 0.5, Z[:], OP.is_gt, OP.add))
            vc(v, lambda: v.tensor_scalar(S[:], D[:], -0.5, None, OP.is_lt))
            vc(v, lambda: v.tensor_tensor(D[:], A[:], S[:], OP.subtract))
            vc(v, lambda: v.tensor_tensor(A[:], W[:], D[:], OP.subtract))
            vc(v, lambda: v.scalar_tensor_tensor(S[:], A[:], 0.5, D[:], OP.is_gt, OP.add))
            vc(v, lambda: v.tensor_scalar(W[:], A[:], -0.5, None, OP.is_lt))
            vc(v, lambda: v.tensor_tensor(A[:], S[:], W[:], OP.subtract))

        def load_x(i):
            Xb = X[i % 2]
            def f(sync):
                waits(sync, ["s_vd", "s_y"])
                for c in range(NC):
                    inst = sync.dma_start(
                        out=Xb[:, c * T:(c + 1) * T],
                        in_=xin[128 * c:128 * (c + 1), T * i:T * (i + 1)])
                    sig(inst, "s_x", 16)
            SY(f)

        def stats(src, is_ln01):
            def cast(scalar):
                waits(scalar, ["s_vd", "s_mm", "s_x", "s_su"])
                sig(scalar.activation(RB[:], src[:], AF.Identity, bias=0.0, scale=1.0), "s_sc")
                sig(scalar.activation(PB[:], src[:], AF.Square, bias=0.0, scale=1.0), "s_sc")
            SC(cast)

            def mm(tensor):
                waits(tensor, ["s_sc"])
                for c in range(NC):
                    tensor.matmul(psS1[:], ONES[:], RB[:, c * T:(c + 1) * T],
                                  start=(c == 0), stop=(c == NC - 1))
                for c in range(NC):
                    inst = tensor.matmul(psS2[:], ONES[:], PB[:, c * T:(c + 1) * T],
                                         start=(c == 0), stop=(c == NC - 1))
                sig(inst, "s_mm")
            TE(mm)

            def chain(vector):
                waits(vector, ["s_mm"])
                vc(vector, lambda: vector.tensor_copy(GS[:, 0:T], psS1[:]))
                vc(vector, lambda: vector.tensor_mul(VB[:], GS[:, 0:T], GS[:, 0:T]))
                vc(vector, lambda: vector.tensor_tensor(VB[:], psS2[:], VB[:], OP.subtract))
            VE(chain)

            def chain_sc(scalar):
                waits(scalar, ["s_vd"])
                inst = scalar.activation(GS[:, T:2 * T], VB[:], AF.Sqrt, bias=CEPS[:], scale=1.0)
                sig(inst, "s_sc")
            SC(chain_sc)

            def recip1(vector):
                waits(vector, ["s_sc"])
                vc(vector, lambda: vector.reciprocal(BB[:], GS[:, T:2 * T]))
            VE(recip1)

            if is_ln01:
                def chain2(vector):
                    vc(vector, lambda: vector.tensor_scalar(GS[:, T:2 * T], VB[:], 1.0 + EPS, EPS * EPS,
                                                            OP.mult, OP.add))
                VE(chain2)

                def chain2_sc(scalar):
                    waits(scalar, ["s_vd"])
                    inst = scalar.activation(GS[:, T:2 * T], GS[:, T:2 * T], AF.Sqrt, bias=0.0, scale=1.0)
                    sig(inst, "s_sc")
                SC(chain2_sc)

                def recip2(vector):
                    waits(vector, ["s_sc"])
                    vc(vector, lambda: vector.reciprocal(AB[:], GS[:, T:2 * T]))
                VE(recip2)

        load_x(0)
        for i in range(NT):
            Xb = X[i % 2]

            stats(Xb, is_ln01=True)

            def apply01(vector, Xb=Xb):
                v = vector
                for c in range(NC):
                    sig(v.tensor_tensor(TA[:, c * T:(c + 1) * T],
                                        Xb[:, c * T:(c + 1) * T], psS1[:], OP.subtract), "s_vd")
                v.wait_ge(s_vd, cnt["s_vd"])
                for c in range(NC):
                    sig(v.tensor_mul(U[:, c * T:(c + 1) * T],
                                     TA[:, c * T:(c + 1) * T], AB[:]), "s_vd")
                v.wait_ge(s_vd, cnt["s_vd"])
                for c in range(NC):
                    sig(v.tensor_scalar(U[:, c * T:(c + 1) * T], U[:, c * T:(c + 1) * T],
                                        LNG[:, c:c + 1], LNB[:, c:c + 1],
                                        OP.mult, OP.add), "s_vd")
                v.wait_ge(s_vd, cnt["s_vd"])
                for c in range(NC):
                    sig(v.tensor_mul(TA[:, c * T:(c + 1) * T],
                                     TA[:, c * T:(c + 1) * T], BB[:]), "s_vd")
            VE(apply01)

            att_cnt = {}

            def attn_w_load(sync):
                sync.wait_ge(s_su, startup_x_cnt)
                sync.wait_ge(s_mm, xlast["w1_user"])
                sig(sync.dma_start(out=W1B[:, 0:4096], in_=kth[0:128, :]), "s_w1", 16)
                sig(sync.dma_start(out=W1B[:, 4096:8192], in_=kth[128:256, :]), "s_w1", 16)
                att_cnt["kt"] = cnt["s_w1"]
                sync.wait_ge(s_mm, xlast["w2_user"])
                sync.wait_ge(s_w2, cnt["s_w2"])
                for j in range(NF):
                    sig(sync.dma_start(out=W2B[:, j * 256:(j + 1) * 256],
                                       in_=wvh[128 * j:128 * (j + 1), :]), "s_w2", 16)
                att_cnt["v"] = cnt["s_w2"]
            SY(attn_w_load)

            ps_sc = [psC0, psC1, psH0, psH1]
            for h in range(4):
                relu_cnt = []
                p_cnt = []
                out_cnt = []
                mm_cnt = []

                def scores_hg(tensor, h=h, hg=None, relu_cnt=relu_cnt):
                    if h == 0 and hg == 0:
                        waits(tensor, ["s_vd", "s_sc"])
                        tensor.wait_ge(s_w1, att_cnt["kt"])
                    for jj in range(4 * hg, 4 * hg + 4):
                        pc = ps_sc[jj % 4]
                        if jj >= 4:
                            tensor.wait_ge(s_sc, relu_cnt[jj - 4])
                        tensor.matmul(pc[:], W1B[:, jj * 128:(jj + 1) * 128],
                                      U[:, (2 * h) * T:(2 * h + 1) * T],
                                      start=True, stop=False, skip_group_check=True)
                        inst = tensor.matmul(pc[:], W1B[:, 4096 + jj * 128:4096 + (jj + 1) * 128],
                                             U[:, (2 * h + 1) * T:(2 * h + 2) * T],
                                             start=False, stop=True, skip_group_check=True)
                        sig(inst, "s_mm")
                        mm_cnt.append(cnt["s_mm"])
                        xlast["w1_user"] = cnt["s_mm"]

                def relu_hg(scalar, h=h, hg=None, mm_cnt=mm_cnt, p_cnt=p_cnt,
                            relu_cnt=relu_cnt):
                    if hg >= 2:
                        scalar.wait_ge(s_vd, p_cnt[hg - 2])
                    for jj in range(4 * hg, 4 * hg + 4):
                        pc = ps_sc[jj % 4]
                        scalar.wait_ge(s_mm, mm_cnt[jj])
                        col = (hg % 2) * 2048 + (jj % 4) * T
                        inst = scalar.activation(RB[:, col:col + T], pc[:],
                                                 AF.Relu, bias=0.0, scale=1.0)
                        sig(inst, "s_sc")
                        relu_cnt.append(cnt["s_sc"])

                def p_hg(vector, h=h, hg=None, relu_cnt=relu_cnt, out_cnt=out_cnt,
                         p_cnt=p_cnt):
                    vector.wait_ge(s_sc, relu_cnt[4 * hg + 3])
                    if hg >= 2:
                        vector.wait_ge(s_mm, out_cnt[hg - 2])
                    half = (hg % 2) * 2048
                    sig(vector.tensor_mul(PB[:, half:half + 2048],
                                          RB[:, half:half + 2048],
                                          RB[:, half:half + 2048]), "s_vd")
                    p_cnt.append(cnt["s_vd"])

                def out_hg(tensor, h=h, hg=None, p_cnt=p_cnt, out_cnt=out_cnt):
                    tensor.wait_ge(s_vd, p_cnt[hg])
                    if h == 0 and hg == 0:
                        tensor.wait_ge(s_w2, att_cnt["v"])
                    half = (hg % 2) * 2048
                    for k in range(4):
                        jj = 4 * hg + k
                        tensor.matmul(psO0[:], W2B[:, jj * 256:jj * 256 + 128],
                                      PB[:, half + k * T:half + (k + 1) * T],
                                      start=(jj == 0), stop=(jj == 31),
                                      skip_group_check=True)
                        inst = tensor.matmul(psO1[:], W2B[:, jj * 256 + 128:jj * 256 + 256],
                                             PB[:, half + k * T:half + (k + 1) * T],
                                             start=(jj == 0), stop=(jj == 31),
                                             skip_group_check=True)
                    sig(inst, "s_mm")
                    out_cnt.append(cnt["s_mm"])
                    xlast["w2_user"] = cnt["s_mm"]

                for hg in range(8):
                    TE(lambda t, f=scores_hg, hg=hg: f(t, hg=hg))
                    SC(lambda s, f=relu_hg, hg=hg: f(s, hg=hg))
                    VE(lambda v, f=p_hg, hg=hg: f(v, hg=hg))
                    if hg >= 1:
                        TE(lambda t, f=out_hg, hg=hg - 1: f(t, hg=hg))
                TE(lambda t, f=out_hg: f(t, hg=7))

                def a_add(vector, h=h):
                    waits(vector, ["s_mm"])
                    sig(vector.tensor_tensor(TA[:, (2 * h) * T:(2 * h + 1) * T],
                                             TA[:, (2 * h) * T:(2 * h + 1) * T],
                                             psO0[:], OP.add), "s_vd")
                    sig(vector.tensor_tensor(TA[:, (2 * h + 1) * T:(2 * h + 2) * T],
                                             TA[:, (2 * h + 1) * T:(2 * h + 2) * T],
                                             psO1[:], OP.add), "s_vd")
                VE(a_add)

            stats(TA, is_ln01=False)

            def apply2(vector, Xb=Xb):
                v = vector
                for c in range(NC):
                    sig(v.tensor_tensor(TA[:, c * T:(c + 1) * T],
                                        TA[:, c * T:(c + 1) * T], psS1[:], OP.subtract), "s_vd")
                v.wait_ge(s_vd, cnt["s_vd"])
                for c in range(NC):
                    sig(v.tensor_mul(TA[:, c * T:(c + 1) * T],
                                     TA[:, c * T:(c + 1) * T], BB[:]), "s_vd")
                v.wait_ge(s_vd, cnt["s_vd"])
                for c in range(NC):
                    sig(v.tensor_add(Xb[:, c * T:(c + 1) * T],
                                     Xb[:, c * T:(c + 1) * T], TA[:, c * T:(c + 1) * T]), "s_vd")
            VE(apply2)

            stats(Xb, is_ln01=False)

            def apply3(vector, Xb=Xb):
                v = vector
                for c in range(NC):
                    sig(v.tensor_tensor(TA[:, c * T:(c + 1) * T],
                                        Xb[:, c * T:(c + 1) * T], psS1[:], OP.subtract), "s_vd")
                v.wait_ge(s_vd, cnt["s_vd"])
                for c in range(NC):
                    sig(v.tensor_mul(U[:, c * T:(c + 1) * T],
                                     TA[:, c * T:(c + 1) * T], BB[:]), "s_vd")
            VE(apply3)

            h_mm_cnt = {}
            h_ev_cnt = {}
            gelu_end = {}
            w1_cnt = {}
            for q in range(4):
                def w1_load(sync, q=q):
                    sync.wait_ge(s_su, startup_x_cnt)
                    if q >= 1:
                        sync.wait_ge(s_mm, h_mm_cnt[(q - 1, 1, 3)])
                    else:
                        sync.wait_ge(s_mm, xlast["w1_user"])
                    for c in range(NC):
                        inst = sync.dma_start(
                            out=W1B[:, c * 1024:(c + 1) * 1024],
                            in_=w1h[128 * c:128 * (c + 1), 1024 * q:1024 * (q + 1)])
                        sig(inst, "s_w1", 16)
                    w1_cnt[q] = cnt["s_w1"]
                SY(w1_load)

                for half in range(2):
                    gb = 2 * q + half

                    for k in range(4):
                        def h_mm_k(tensor, q=q, half=half, k=k):
                            if q == 0 and half == 0 and k == 0:
                                waits(tensor, ["s_vd", "s_sc"])
                            tensor.wait_ge(s_w1, w1_cnt[q])
                            jf = 4 * half + k
                            ph = [psH0, psH1][k % 2]
                            prev = ((q, half, k - 2) if k >= 2 else
                                    ((q, half - 1, k + 2) if half else (q - 1, 1, k + 2)))
                            if prev in h_ev_cnt:
                                tensor.wait_ge(s_sc, h_ev_cnt[prev])
                            for c in range(NC):
                                inst = tensor.matmul(
                                    ph[:], W1B[:, c * 1024 + jf * 128:c * 1024 + (jf + 1) * 128],
                                    U[:, c * T:(c + 1) * T],
                                    start=(c == 0), stop=(c == NC - 1))
                            sig(inst, "s_mm")
                            h_mm_cnt[(q, half, k)] = cnt["s_mm"]
                            xlast["w1_user"] = cnt["s_mm"]
                        TE(h_mm_k)

                        def h_evac_k(scalar, q=q, half=half, gb=gb, k=k):
                            if k == 0 and gb >= 2:
                                scalar.wait_ge(s_vd, gelu_end[gb - 2])
                            Wb = [WQ, WQ2][gb % 2]
                            Zb = [GZ, GZ2][gb % 2]
                            jf = 4 * half + k
                            fg = 8 * q + jf
                            ph = [psH0, psH1][k % 2]
                            scalar.wait_ge(s_mm, h_mm_cnt[(q, half, k)])
                            inst = scalar.activation(Wb[:, k * T:(k + 1) * T], ph[:],
                                                     AF.Identity,
                                                     bias=B1Q[:, fg:fg + 1], scale=256.0)
                            sig(inst, "s_sc")
                            inst = scalar.activation(Zb[:, k * T:(k + 1) * T], ph[:],
                                                     AF.Identity,
                                                     bias=B1Q[:, fg:fg + 1], scale=256.0)
                            sig(inst, "s_sc")
                            h_ev_cnt[(q, half, k)] = cnt["s_sc"]
                        SC(h_evac_k)

                    def gelu(vector, q=q, half=half, gb=gb):
                        Wb = [WQ, WQ2][gb % 2]
                        Zb = [GZ, GZ2][gb % 2]
                        vector.wait_ge(s_sc, h_ev_cnt[(q, half, 3)])
                        fg0 = 8 * q + 4 * half
                        cols = slice(fg0 * T, (fg0 + 4) * T)
                        v = vector
                        vc(v, lambda: v.tensor_tensor(GD[:], Wb[:], Zb[:], OP.subtract))
                        vc(v, lambda: v.scalar_tensor_tensor(GA[:], GD[:], 0.5, Zb[:], OP.is_gt, OP.add))
                        vc(v, lambda: v.tensor_scalar(GS[:], GD[:], -0.5, None, OP.is_lt))
                        vc(v, lambda: v.tensor_tensor(GD[:], GA[:], GS[:], OP.subtract))
                        vc(v, lambda: v.tensor_tensor(GA[:], Wb[:], GD[:], OP.subtract))
                        vc(v, lambda: v.scalar_tensor_tensor(GS[:], GA[:], 0.5, GD[:], OP.is_gt, OP.add))
                        vc(v, lambda: v.tensor_scalar(Wb[:], GA[:], -0.5, None, OP.is_lt))
                        vc(v, lambda: v.tensor_tensor(GA[:], GS[:], Wb[:], OP.subtract))
                        vc(v, lambda: v.tensor_scalar(GS[:], GA[:], BIAS, 1.0 / 64.0,
                                                      OP.subtract, OP.mult))
                        vc(v, lambda: v.scalar_tensor_tensor(GS[:], GS[:], -1.0, GS[:],
                                                             OP.mult, OP.max))
                        vc(v, lambda: v.tensor_scalar(GD[:], GA[:], BIAS, 1.0 / 512.0,
                                                      OP.subtract, OP.mult))
                        vc(v, lambda: v.scalar_tensor_tensor(GB[:], GS[:], float(LAM0), GD[:],
                                                             OP.mult, OP.add))
                        vc(v, lambda: v.tensor_scalar(GB[:], GB[:], float(CONST_OFF), None,
                                                      OP.subtract))
                        mbuf = [GD, Wb, GA]
                        ts_c = [None] * 8
                        m_c = [None] * 8
                        acc_c = [None] * 9
                        acc_c[0] = cnt["s_vd"]
                        base = cnt["s_vd"]

                        def e_ts(j):
                            mb = mbuf[j % 3]
                            dep = acc_c[j - 2] if j >= 3 else base
                            vc(v, lambda: v.tensor_scalar(
                                mb[:], GS[:], float(2 * j + 1), float(PAIR_RHO[j]),
                                OP.max, OP.mult), dep=dep)
                            ts_c[j] = cnt["s_vd"]

                        def e_m(j):
                            mb = mbuf[j % 3]
                            vc(v, lambda: v.scalar_tensor_tensor(
                                mb[:], GS[:], float(2 * j + 2), mb[:], OP.max, OP.add),
                                dep=ts_c[j])
                            m_c[j] = cnt["s_vd"]

                        def e_acc(j):
                            mb = mbuf[j % 3]
                            vc(v, lambda: v.scalar_tensor_tensor(
                                GB[:], mb[:], float(PAIR_EB[j]), GB[:], OP.mult, OP.add),
                                dep=max(m_c[j], acc_c[j]))
                            acc_c[j + 1] = cnt["s_vd"]

                        e_ts(0); e_ts(1); e_m(0); e_ts(2); e_m(1); e_acc(0)
                        for j in range(1, 6):
                            e_m(j + 1); e_acc(j); e_ts(j + 2)
                        e_m(7); e_acc(6); e_acc(7)
                        vc(v, lambda: v.tensor_scalar(Wb[:], GB[:], 256.0, BIAS,
                                                      OP.mult, OP.add))
                        round2(v, Wb, Zb, GD, GS, GA)
                        vc(v, lambda: v.tensor_scalar(HB[:, cols], GA[:], BIAS, 1.0 / 256.0,
                                                      OP.subtract, OP.mult))
                        gelu_end[gb] = cnt["s_vd"]
                    VE(gelu)

            o_mm_cnt = {}
            o_ev_cnt = {}
            o_rnd_end = {}
            for Q in range(4):
                def w2_load(sync, Q=Q):
                    sync.wait_ge(s_su, startup_x_cnt)
                    WB = [W2B, W1B][Q % 2]
                    if Q >= 2:
                        sync.wait_ge(s_mm, o_mm_cnt[(Q - 2, 1)])
                    elif Q == 0:
                        sync.wait_ge(s_mm, xlast["w2_user"])
                    else:
                        sync.wait_ge(s_mm, xlast["w1_user"])
                    sync.wait_ge(s_w2, cnt["s_w2"])
                    for j in range(NF):
                        inst = sync.dma_start(
                            out=WB[:, j * 256:(j + 1) * 256],
                            in_=w2h[128 * j:128 * (j + 1), 256 * Q:256 * (Q + 1)])
                        sig(inst, "s_w2", 16)
                    o_mm_cnt[("w2x", Q)] = cnt["s_w2"]
                SY(w2_load)

                def o_mm(tensor, Q=Q):
                    if Q == 0:
                        waits(tensor, ["s_vd", "s_sc"])
                    WB = [W2B, W1B][Q % 2]
                    tensor.wait_ge(s_w2, o_mm_cnt[("w2x", Q)])
                    for dd in range(2):
                        ph = [psH0, psH1][dd]
                        if (Q - 1, dd) in o_ev_cnt:
                            tensor.wait_ge(s_sc, o_ev_cnt[(Q - 1, dd)])
                        for j in range(NF):
                            inst = tensor.matmul(
                                ph[:], WB[:, j * 256 + dd * 128:j * 256 + (dd + 1) * 128],
                                HB[:, j * T:(j + 1) * T],
                                start=(j == 0), stop=(j == NF - 1))
                        sig(inst, "s_mm")
                        o_mm_cnt[(Q, dd)] = cnt["s_mm"]
                        if Q % 2 == 0:
                            xlast["w2_user"] = cnt["s_mm"]
                        else:
                            xlast["w1_user"] = cnt["s_mm"]
                TE(o_mm)

                def o_evac(scalar, Q=Q):
                    if Q >= 2:
                        scalar.wait_ge(s_vd, o_rnd_end[Q - 2])
                    Wb = [WQ, WQ2][Q % 2]
                    for dd in range(2):
                        cq = 2 * Q + dd
                        ph = [psH0, psH1][dd]
                        scalar.wait_ge(s_mm, o_mm_cnt[(Q, dd)])
                        inst = scalar.activation(Wb[:, dd * T:(dd + 1) * T], ph[:],
                                                 AF.Identity,
                                                 bias=B2Q[:, cq:cq + 1], scale=256.0)
                        sig(inst, "s_sc")
                        o_ev_cnt[(Q, dd)] = cnt["s_sc"]
                SC(o_evac)

                def o_round(vector, Q=Q):
                    vector.wait_ge(s_sc, o_ev_cnt[(Q, 1)])
                    Wb = [WQ, WQ2][Q % 2]
                    v = vector
                    rhe(v, Wb, GZ, GD, GS, GA, GB, 2 * T)
                    vc(v, lambda: v.tensor_scalar(TA[:, 2 * Q * T:(2 * Q + 2) * T],
                                                  GA[:, 0:2 * T], BIAS, 1.0 / 256.0,
                                                  OP.subtract, OP.mult))
                    o_rnd_end[Q] = cnt["s_vd"]
                VE(o_round)

            stats(TA, is_ln01=False)

            def apply4(vector, Xb=Xb):
                v = vector
                for c in range(NC):
                    sig(v.tensor_tensor(TA[:, c * T:(c + 1) * T],
                                        TA[:, c * T:(c + 1) * T], psS1[:], OP.subtract), "s_vd")
                v.wait_ge(s_vd, cnt["s_vd"])
                for c in range(NC):
                    sig(v.tensor_mul(TA[:, c * T:(c + 1) * T],
                                     TA[:, c * T:(c + 1) * T], BB[:]), "s_vd")
                v.wait_ge(s_vd, cnt["s_vd"])
                for c in range(NC):
                    sig(v.tensor_add(Xb[:, c * T:(c + 1) * T],
                                     Xb[:, c * T:(c + 1) * T], TA[:, c * T:(c + 1) * T]), "s_vd")
            VE(apply4)

            def store(sync, i=i, Xb=Xb):
                waits(sync, ["s_vd"])
                for c in range(NC):
                    inst = sync.dma_start(
                        out=yout[128 * c:128 * (c + 1), T * i:T * (i + 1)],
                        in_=Xb[:, c * T:(c + 1) * T])
                    sig(inst, "s_y", 16)
            SY(store)

            if i + 1 < NT:
                load_x(i + 1)

        def fin(sync):
            sync.wait_ge(s_y, cnt["s_y"])
        SY(fin)

    return nc



_NC_CACHE = {}


def kernel(L0_attn_residual, L0_attn_w_fc, L0_attn_w_proj, L0_attn_ln_g,
           L0_attn_ln_b, L0_ffn_w1, L0_ffn_b1, L0_ffn_w2, L0_ffn_b2):
    x = np.ascontiguousarray(np.asarray(L0_attn_residual, dtype=np.float32))
    w_fc = np.asarray(L0_attn_w_fc, dtype=np.float32)
    w_proj = np.asarray(L0_attn_w_proj, dtype=np.float32)
    ln_g = np.asarray(L0_attn_ln_g, dtype=np.float32)
    ln_b = np.asarray(L0_attn_ln_b, dtype=np.float32)
    w1 = np.ascontiguousarray(np.asarray(L0_ffn_w1, dtype=np.float32))
    b1 = np.asarray(L0_ffn_b1, dtype=np.float32)
    w2 = np.ascontiguousarray(np.asarray(L0_ffn_w2, dtype=np.float32))
    b2 = np.asarray(L0_ffn_b2, dtype=np.float32)

    wkt = np.ascontiguousarray(w_fc.reshape(FF, 256).T)
    wv = np.ascontiguousarray(np.ascontiguousarray(w_proj.T).reshape(FF, 256))
    lng = np.ascontiguousarray(ln_g.reshape(NC, 128).T)
    lnb = np.ascontiguousarray(ln_b.reshape(NC, 128).T)
    b1t = np.ascontiguousarray(b1.reshape(NF, 128).T)
    b2t = np.ascontiguousarray(b2.reshape(NC, 128).T)

    xf = x.reshape(B * TT, C)
    in_maps = []
    for core in range(N_CORES):
        xs = np.ascontiguousarray(xf[core * TOK:(core + 1) * TOK, :].T)
        in_maps.append(dict(x=xs, wkt=wkt, wv=wv, w1=w1, w2=w2,
                            lng=lng, lnb=lnb, b1t=b1t, b2t=b2t))

    if "nc" not in _NC_CACHE:
        _NC_CACHE["nc"] = build_nc()
    nc = _NC_CACHE["nc"]
    res = run_bass_kernel_spmd(nc, in_maps, core_ids=list(range(N_CORES)))

    out = np.empty((B * TT, C), dtype=np.float32)
    for core in range(N_CORES):
        out[core * TOK:(core + 1) * TOK, :] = res.results[core]["y"].T
    return out.reshape(B, TT, C)


# version 18
# speedup vs baseline: 4.0094x; 1.0251x over previous
import contextlib
import struct
import numpy as np
import ml_dtypes

import concourse.bass as bass
import concourse.mybir as mybir
from concourse.bass_utils import run_bass_kernel_spmd

F32 = mybir.dt.float32
F16 = mybir.dt.float16
I32 = mybir.dt.int32
AF = mybir.ActivationFunctionType
OP = mybir.AluOpType

N_CORES = 8
B, TT, C = 8, 8192, 1024
TOK = B * TT // N_CORES
T = 512
NT = TOK // T
NC = C // 128
FF = 4096
NF = FF // 128
EPS = 1e-5
BIAS = 8192.0

_YBITS = [0xb904c000, 0xb9adbb00, 0xba556b00, 0xbaf5ce80, 0xbb84b340, 0xbc064180,
          0xbc7e5920, 0xbce15244, 0xbd3a5e80, 0xbd8f926e, 0xbdcd3b48, 0xbe073b54,
          0xbe227686, 0xbe2e0cbe, 0xbe1df8a2, 0xbdcd765d, 0x00000000, 0x3e1944d2,
          0x3eb103af, 0x3f147cd0, 0x3f57625e, 0x3f8f1895, 0x3fb32c4c, 0x3fd706d9,
          0x3ffa2d0c, 0x400e3d5c, 0x401f01a7, 0x402f79be, 0x403fbda6, 0x404fe146,
          0x405ff2aa, 0x406ffa93, 0x407ffded]
_Y = np.array([struct.unpack('<f', struct.pack('<I', b))[0] for b in _YBITS],
              dtype=np.float32)
_X = np.linspace(-4, 4, 33).astype(np.float32)


def _gelu_consts():
    E = _Y.astype(np.float64)[16:] - _X.astype(np.float64)[16:] / 2.0
    K1 = np.diff(E)
    lam0 = np.float32(K1[0])
    e = np.diff(np.concatenate([K1, [0.125]])).astype(np.float32)
    const_off = np.float32(np.sum(e.astype(np.float64) * np.arange(1, 17)))
    return lam0, e, const_off


LAM0, ECOEF, CONST_OFF = _gelu_consts()
PAIR_RHO = [np.float32(np.float64(ECOEF[2 * j]) / np.float64(ECOEF[2 * j + 1]))
            for j in range(8)]
PAIR_EB = [ECOEF[2 * j + 1] for j in range(8)]


def _verify_gelu_chain():
    def ref(xq):
        y = np.interp(xq, _X.astype(np.float64), _Y.astype(np.float64))
        return np.where(xq > 4.0, xq, np.where(xq < -4.0, 0.0, y))

    A0 = np.arange(-16384, 16385).astype(np.float64) + BIAS
    s = np.abs(A0 * np.float64(1.0 / 64.0) + np.float64(-128.0))
    acc = (A0 - BIAS) * np.float64(1.0 / 512.0)
    acc = np.float64(LAM0) * s + acc
    acc = acc - np.float64(CONST_OFF)
    for j in range(8):
        rho, eb = PAIR_RHO[j], PAIR_EB[j]
        u = np.maximum(s, np.float64(2 * j + 1)) * np.float64(rho)
        u = np.maximum(s, np.float64(2 * j + 2)) + u
        acc = u * np.float64(eb) + acc
    w2 = acc * 256.0 + BIAS
    fr = w2 - np.floor(w2)
    assert np.abs(fr - 0.5).min() > 1e-4, "site-2 rounding tie on lattice"
    for conv in (np.trunc, np.round):
        for jit in (0.0, 0.9, -0.9):
            z2 = conv(w2 + jit)
            A2 = z2.copy()
            for _ in range(2):
                d2 = w2 - A2
                A2 = (A2 + (d2 > 0.5)) - (d2 < -0.5)
            q2 = A2 - BIAS
            y2_ref = np.round(256.0 * ref((A0 - BIAS) / 256.0))
            assert np.array_equal(q2, y2_ref), "gelu chain verification failed"


_verify_gelu_chain()



def build_nc():
    nc = bass.Bass("TRN2", target_bir_lowering=False)

    xin = nc.dram_tensor("x", [C, TOK], F32, kind="ExternalInput").ap()
    wkt = nc.dram_tensor("wkt", [256, FF], F32, kind="ExternalInput").ap()
    wv = nc.dram_tensor("wv", [FF, 256], F32, kind="ExternalInput").ap()
    w1 = nc.dram_tensor("w1", [C, FF], F32, kind="ExternalInput").ap()
    w2 = nc.dram_tensor("w2", [FF, C], F32, kind="ExternalInput").ap()
    lng_in = nc.dram_tensor("lng", [128, NC], F32, kind="ExternalInput").ap()
    lnb_in = nc.dram_tensor("lnb", [128, NC], F32, kind="ExternalInput").ap()
    b1_in = nc.dram_tensor("b1t", [128, NF], F32, kind="ExternalInput").ap()
    b2_in = nc.dram_tensor("b2t", [128, NC], F32, kind="ExternalInput").ap()
    yout = nc.dram_tensor("y", [C, TOK], F32, kind="ExternalOutput").ap()

    kth = nc.dram_tensor("kth", [256, FF], F16, kind="Internal").ap()
    wvh = nc.dram_tensor("wvh", [FF, 256], F16, kind="Internal").ap()
    w1h = nc.dram_tensor("w1h", [C, FF], F16, kind="Internal").ap()
    w2h = nc.dram_tensor("w2h", [FF, C], F16, kind="Internal").ap()

    es = contextlib.ExitStack()
    sb = lambda name, shape, dt: es.enter_context(nc.sbuf_tensor(name, shape, dt))
    ps = lambda name, shape: es.enter_context(nc.psum_tensor(name, shape, F32))
    sem = lambda name: es.enter_context(nc.semaphore(name))

    with es:
        X0 = sb("X0", [128, NC * T], F32)
        X1 = sb("X1", [128, NC * T], F32)
        TA = sb("TA", [128, NC * T], F32)
        U = sb("U", [128, NC * T], F16)
        W1B = sb("W1B", [128, 8192], F16)
        W2B = sb("W2B", [128, 8192], F16)
        RB = sb("RB", [128, 4096], F16)
        PB = sb("PB", [128, 4096], F16)
        HB = sb("HB", [128, NF * T], F16)
        WQ = sb("WQ", [128, 2048], F32)
        WQ2 = sb("WQ2", [128, 2048], F32)
        GZ = sb("GZ", [128, 2048], I32)
        GZ2 = sb("GZ2", [128, 2048], I32)
        GD = sb("GD", [128, 2048], F32)
        GS = sb("GS", [128, 2048], F32)
        GA = sb("GA", [128, 2048], F32)
        GB = sb("GB", [128, 2048], F32)
        VB = sb("VB", [128, T], F32)
        AB = sb("AB", [128, T], F32)
        BB = sb("BB", [128, T], F32)
        LNG = sb("LNG", [128, NC], F32)
        LNB = sb("LNB", [128, NC], F32)
        B1T = sb("B1T", [128, NF], F32)
        B2T = sb("B2T", [128, NC], F32)
        B1Q = sb("B1Q", [128, NF], F32)
        B2Q = sb("B2Q", [128, NC], F32)
        ONES = sb("ONES", [128, 128], F16)
        CEPS = sb("CEPS", [128, 1], F32)

        psS1 = ps("psS1", [128, T])
        psS2 = ps("psS2", [128, T])
        psC0 = ps("psC0", [128, T])
        psC1 = ps("psC1", [128, T])
        psO0 = ps("psO0", [128, T])
        psO1 = ps("psO1", [128, T])
        psH0 = ps("psH0", [128, T])
        psH1 = ps("psH1", [128, T])

        s_x = sem("s_x")
        s_w1 = sem("s_w1")
        s_w2 = sem("s_w2")
        s_su = sem("s_su")
        s_y = sem("s_y")
        s_mm = sem("s_mm")
        s_sc = sem("s_sc")
        s_vd = sem("s_vd")

        cnt = {"s_x": 0, "s_y": 0, "s_mm": 0, "s_sc": 0, "s_vd": 0,
               "s_w1": 0, "s_w2": 0, "s_su": 0}
        hmap = {"s_x": s_x, "s_y": s_y, "s_mm": s_mm, "s_sc": s_sc, "s_vd": s_vd,
                "s_w1": s_w1, "s_w2": s_w2, "s_su": s_su}

        blk = es.enter_context(nc.Block())

        def sig(inst, name, k=1):
            cnt[name] += k
            inst.then_inc(hmap[name], k)

        def waits(eng, names):
            for n in names:
                eng.wait_ge(hmap[n], cnt[n])

        X = [X0, X1]
        xlast = {"w1_user": 0, "w2_user": 0}

        def vc(vector, emit, dep=None):
            vector.wait_ge(s_vd, cnt["s_vd"] if dep is None else dep)
            sig(emit(), "s_vd")

        def sc_op(scalar, emit):
            scalar.wait_ge(s_sc, cnt["s_sc"])
            sig(emit(), "s_sc")

        def SY(f):
            blk.sync(f)

        def TE(f):
            blk.tensor(f)

        def VE(f):
            def g(vector):
                vector.wait_ge(s_vd, cnt["s_vd"])
                f(vector)
            blk.vector(g)

        def SC(f):
            def g(scalar):
                scalar.wait_ge(s_sc, cnt["s_sc"])
                f(scalar)
            blk.scalar(g)

        def startup_sync(sync):
            sig(sync.dma_start(out=LNG[:], in_=lng_in[:, :]), "s_su", 16)
            sig(sync.dma_start(out=LNB[:], in_=lnb_in[:, :]), "s_su", 16)
            sig(sync.dma_start(out=B1T[:], in_=b1_in[:, :]), "s_su", 16)
            sig(sync.dma_start(out=B2T[:], in_=b2_in[:, :]), "s_su", 16)
        SY(startup_sync)

        def startup_vec(vector):
            waits(vector, ["s_su"])
            sig(vector.memset(ONES[:], 1.0 / 1024.0), "s_vd")
            sig(vector.memset(CEPS[:], EPS), "s_vd")
            sig(vector.tensor_scalar(B1Q[:], B1T[:], 256.0, BIAS, OP.mult, OP.add), "s_vd")
            sig(vector.tensor_scalar(B2Q[:], B2T[:], 256.0, BIAS, OP.mult, OP.add), "s_vd")
        VE(startup_vec)

        def conv_round(loads, dst_sbuf_ap=None, stores=None):
            def ld(sync):
                waits(sync, ["s_vd"])
                for (ap, c0, nc_) in loads:
                    sig(sync.dma_start(out=X0[:, c0:c0 + nc_], in_=ap), "s_su", 16)
            def cv(vector):
                waits(vector, ["s_su"])
                if dst_sbuf_ap is not None:
                    sig(vector.tensor_copy(dst_sbuf_ap, X0[:, 0:src_cols]), "s_vd")
                else:
                    sig(vector.tensor_copy(RB[:, 0:src_cols], X0[:, 0:src_cols]), "s_vd")
            SY(ld)
            VE(cv)
            if stores:
                def st(sync):
                    waits(sync, ["s_vd"])
                    for (ap, c0, nc_) in stores:
                        sig(sync.dma_start(out=ap, in_=RB[:, c0:c0 + nc_]), "s_su", 16)
                SY(st)

        src_cols = 4096
        conv_round([(wkt[0:128, :], 0, 4096)], stores=[(kth[0:128, :], 0, 4096)])
        conv_round([(wkt[128:256, :], 0, 4096)], stores=[(kth[128:256, :], 0, 4096)])
        for half16 in range(2):
            j0 = 16 * half16
            conv_round([(wv[128 * (j0 + j):128 * (j0 + j + 1), :], j * 256, 256)
                        for j in range(16)],
                       stores=[(wvh[128 * (j0 + j):128 * (j0 + j + 1), :], j * 256, 256)
                               for j in range(16)])
        for c in range(NC):
            conv_round([(w1[128 * c:128 * (c + 1), :], 0, 4096)],
                       stores=[(w1h[128 * c:128 * (c + 1), :], 0, 4096)])
        for r in range(8):
            conv_round([(w2[128 * (4 * r + j):128 * (4 * r + j + 1), :], j * 1024, 1024)
                        for j in range(4)],
                       stores=[(w2h[128 * (4 * r + j):128 * (4 * r + j + 1), :], j * 1024, 1024)
                               for j in range(4)])
        startup_x_cnt = cnt["s_su"]


        def floor2(v, Bv, Z, D, S, F, nw):
            w_ = slice(0, nw)
            vc(v, lambda: v.tensor_copy(Z[:, w_], Bv[:, w_]))
            for _ in range(2):
                si = Z if _ == 0 else F
                vc(v, lambda si=si: v.tensor_tensor(D[:, w_], Bv[:, w_], si[:, w_], OP.subtract))
                vc(v, lambda si=si: v.scalar_tensor_tensor(S[:, w_], D[:, w_], 1.0, si[:, w_], OP.is_ge, OP.add))
                vc(v, lambda: v.tensor_scalar(D[:, w_], D[:, w_], 0.0, None, OP.is_lt))
                vc(v, lambda: v.tensor_tensor(F[:, w_], S[:, w_], D[:, w_], OP.subtract))

        def rhe(v, W, Z, D, S, A, Bv, nw):
            w_ = slice(0, nw)
            vc(v, lambda: v.tensor_scalar(Bv[:, w_], W[:, w_], 0.5, 0.25, OP.mult, OP.add))
            floor2(v, Bv, Z, D, S, A, nw)
            vc(v, lambda: v.tensor_scalar(Bv[:, w_], W[:, w_], -0.5, 0.25, OP.mult, OP.add))
            floor2(v, Bv, Z, D, S, W, nw)
            vc(v, lambda: v.tensor_tensor(A[:, w_], A[:, w_], W[:, w_], OP.subtract))

        def round2(v, W, Z, D, S, A):
            vc(v, lambda: v.tensor_copy(Z[:], W[:]))
            vc(v, lambda: v.tensor_tensor(D[:], W[:], Z[:], OP.subtract))
            vc(v, lambda: v.scalar_tensor_tensor(A[:], D[:], 0.5, Z[:], OP.is_gt, OP.add))
            vc(v, lambda: v.tensor_scalar(S[:], D[:], -0.5, None, OP.is_lt))
            vc(v, lambda: v.tensor_tensor(D[:], A[:], S[:], OP.subtract))
            vc(v, lambda: v.tensor_tensor(A[:], W[:], D[:], OP.subtract))
            vc(v, lambda: v.scalar_tensor_tensor(S[:], A[:], 0.5, D[:], OP.is_gt, OP.add))
            vc(v, lambda: v.tensor_scalar(W[:], A[:], -0.5, None, OP.is_lt))
            vc(v, lambda: v.tensor_tensor(A[:], S[:], W[:], OP.subtract))

        def load_x(i, dep=None):
            Xb = X[i % 2]
            def f(sync):
                if dep is None:
                    waits(sync, ["s_vd", "s_y"])
                else:
                    sync.wait_ge(s_vd, dep["vd"])
                    sync.wait_ge(s_y, dep["y"])
                    sync.wait_ge(s_sc, dep["sc"])
                for c in range(NC):
                    inst = sync.dma_start(
                        out=Xb[:, c * T:(c + 1) * T],
                        in_=xin[128 * c:128 * (c + 1), T * i:T * (i + 1)])
                    sig(inst, "s_x", 16)
            SY(f)

        def stats(src, is_ln01, cast_deps=None, mm_vd=None):
            rec = {}
            def cast(scalar):
                if cast_deps is None:
                    waits(scalar, ["s_vd", "s_mm", "s_x", "s_su"])
                else:
                    for sm, val in cast_deps:
                        scalar.wait_ge(sm, val)
                sig(scalar.activation(RB[:], src[:], AF.Identity, bias=0.0, scale=1.0), "s_sc")
                sig(scalar.activation(PB[:], src[:], AF.Square, bias=0.0, scale=1.0), "s_sc")
            SC(cast)
            rec["cast"] = cnt["s_sc"]

            def mm(tensor):
                waits(tensor, ["s_sc"])
                if mm_vd is not None:
                    tensor.wait_ge(s_vd, mm_vd)
                for c in range(NC):
                    tensor.matmul(psS1[:], ONES[:], RB[:, c * T:(c + 1) * T],
                                  start=(c == 0), stop=(c == NC - 1))
                for c in range(NC):
                    inst = tensor.matmul(psS2[:], ONES[:], PB[:, c * T:(c + 1) * T],
                                         start=(c == 0), stop=(c == NC - 1))
                sig(inst, "s_mm")
            TE(mm)
            rec["mm"] = cnt["s_mm"]

            def chain(vector):
                waits(vector, ["s_mm"])
                vc(vector, lambda: vector.tensor_copy(GS[:, 0:T], psS1[:]))
                vc(vector, lambda: vector.tensor_mul(VB[:], GS[:, 0:T], GS[:, 0:T]))
                vc(vector, lambda: vector.tensor_tensor(VB[:], psS2[:], VB[:], OP.subtract))
            VE(chain)
            rec["chain"] = cnt["s_vd"]

            def chain_sc(scalar):
                waits(scalar, ["s_vd"])
                inst = scalar.activation(GS[:, T:2 * T], VB[:], AF.Sqrt, bias=CEPS[:], scale=1.0)
                sig(inst, "s_sc")
            SC(chain_sc)

            def recip1(vector):
                waits(vector, ["s_sc"])
                vc(vector, lambda: vector.reciprocal(BB[:], GS[:, T:2 * T]))
            VE(recip1)

            if is_ln01:
                def chain2(vector):
                    vc(vector, lambda: vector.tensor_scalar(GS[:, T:2 * T], VB[:], 1.0 + EPS, EPS * EPS,
                                                            OP.mult, OP.add))
                VE(chain2)

                def chain2_sc(scalar):
                    waits(scalar, ["s_vd"])
                    inst = scalar.activation(GS[:, T:2 * T], GS[:, T:2 * T], AF.Sqrt, bias=0.0, scale=1.0)
                    sig(inst, "s_sc")
                SC(chain2_sc)

                def recip2(vector):
                    waits(vector, ["s_sc"])
                    vc(vector, lambda: vector.reciprocal(AB[:], GS[:, T:2 * T]))
                VE(recip2)
            return rec

        load_x(0)
        ap4_end = {}; store_end = {}; st01_cast = {}; xload_end = {}
        ln4_mm = {}; ln4_chain = {}; rb_free = {}
        for i in range(NT):
            Xb = X[i % 2]

            if i == 0:
                st01 = stats(Xb, is_ln01=True)
            else:
                st01 = stats(Xb, is_ln01=True,
                             cast_deps=[(s_x, xload_end[i]),
                                        (s_mm, ln4_mm[i - 1]),
                                        (s_vd, rb_free[i - 1]),
                                        (s_su, startup_x_cnt)],
                             mm_vd=ln4_chain[i - 1])
            st01_cast[i] = st01["cast"]

            def apply01(vector, Xb=Xb):
                v = vector
                for c in range(NC):
                    sig(v.tensor_tensor(TA[:, c * T:(c + 1) * T],
                                        Xb[:, c * T:(c + 1) * T], GS[:, 0:T], OP.subtract), "s_vd")
                v.wait_ge(s_vd, cnt["s_vd"])
                for c in range(NC):
                    sig(v.tensor_mul(U[:, c * T:(c + 1) * T],
                                     TA[:, c * T:(c + 1) * T], AB[:]), "s_vd")
                v.wait_ge(s_vd, cnt["s_vd"])
                for c in range(NC):
                    sig(v.tensor_scalar(U[:, c * T:(c + 1) * T], U[:, c * T:(c + 1) * T],
                                        LNG[:, c:c + 1], LNB[:, c:c + 1],
                                        OP.mult, OP.add), "s_vd")
                v.wait_ge(s_vd, cnt["s_vd"])
                for c in range(NC):
                    sig(v.tensor_mul(TA[:, c * T:(c + 1) * T],
                                     TA[:, c * T:(c + 1) * T], BB[:]), "s_vd")
            VE(apply01)

            att_cnt = {}

            def attn_w_load(sync):
                sync.wait_ge(s_su, startup_x_cnt)
                sync.wait_ge(s_mm, xlast["w1_user"])
                sig(sync.dma_start(out=W1B[:, 0:4096], in_=kth[0:128, :]), "s_w1", 16)
                sig(sync.dma_start(out=W1B[:, 4096:8192], in_=kth[128:256, :]), "s_w1", 16)
                att_cnt["kt"] = cnt["s_w1"]
                sync.wait_ge(s_mm, xlast["w2_user"])
                sync.wait_ge(s_w2, cnt["s_w2"])
                for j in range(NF):
                    sig(sync.dma_start(out=W2B[:, j * 256:(j + 1) * 256],
                                       in_=wvh[128 * j:128 * (j + 1), :]), "s_w2", 16)
                att_cnt["v"] = cnt["s_w2"]
            SY(attn_w_load)

            ps_sc = [psC0, psC1, psH0, psH1]
            for h in range(4):
                relu_cnt = []
                p_cnt = []
                out_cnt = []
                mm_cnt = []

                def scores_hg(tensor, h=h, hg=None, relu_cnt=relu_cnt):
                    if h == 0 and hg == 0:
                        waits(tensor, ["s_vd", "s_sc"])
                        tensor.wait_ge(s_w1, att_cnt["kt"])
                    for jj in range(4 * hg, 4 * hg + 4):
                        pc = ps_sc[jj % 4]
                        if jj >= 4:
                            tensor.wait_ge(s_sc, relu_cnt[jj - 4])
                        tensor.matmul(pc[:], W1B[:, jj * 128:(jj + 1) * 128],
                                      U[:, (2 * h) * T:(2 * h + 1) * T],
                                      start=True, stop=False, skip_group_check=True)
                        inst = tensor.matmul(pc[:], W1B[:, 4096 + jj * 128:4096 + (jj + 1) * 128],
                                             U[:, (2 * h + 1) * T:(2 * h + 2) * T],
                                             start=False, stop=True, skip_group_check=True)
                        sig(inst, "s_mm")
                        mm_cnt.append(cnt["s_mm"])
                        xlast["w1_user"] = cnt["s_mm"]

                def relu_hg(scalar, h=h, hg=None, mm_cnt=mm_cnt, p_cnt=p_cnt,
                            relu_cnt=relu_cnt):
                    if hg >= 2:
                        scalar.wait_ge(s_vd, p_cnt[hg - 2])
                    for jj in range(4 * hg, 4 * hg + 4):
                        pc = ps_sc[jj % 4]
                        scalar.wait_ge(s_mm, mm_cnt[jj])
                        col = (hg % 2) * 2048 + (jj % 4) * T
                        inst = scalar.activation(RB[:, col:col + T], pc[:],
                                                 AF.Relu, bias=0.0, scale=1.0)
                        sig(inst, "s_sc")
                        relu_cnt.append(cnt["s_sc"])

                def p_hg(vector, h=h, hg=None, relu_cnt=relu_cnt, out_cnt=out_cnt,
                         p_cnt=p_cnt):
                    vector.wait_ge(s_sc, relu_cnt[4 * hg + 3])
                    if hg >= 2:
                        vector.wait_ge(s_mm, out_cnt[hg - 2])
                    half = (hg % 2) * 2048
                    sig(vector.tensor_mul(PB[:, half:half + 2048],
                                          RB[:, half:half + 2048],
                                          RB[:, half:half + 2048]), "s_vd")
                    p_cnt.append(cnt["s_vd"])

                def out_hg(tensor, h=h, hg=None, p_cnt=p_cnt, out_cnt=out_cnt):
                    tensor.wait_ge(s_vd, p_cnt[hg])
                    if h == 0 and hg == 0:
                        tensor.wait_ge(s_w2, att_cnt["v"])
                    half = (hg % 2) * 2048
                    for k in range(4):
                        jj = 4 * hg + k
                        tensor.matmul(psO0[:], W2B[:, jj * 256:jj * 256 + 128],
                                      PB[:, half + k * T:half + (k + 1) * T],
                                      start=(jj == 0), stop=(jj == 31),
                                      skip_group_check=True)
                        inst = tensor.matmul(psO1[:], W2B[:, jj * 256 + 128:jj * 256 + 256],
                                             PB[:, half + k * T:half + (k + 1) * T],
                                             start=(jj == 0), stop=(jj == 31),
                                             skip_group_check=True)
                    sig(inst, "s_mm")
                    out_cnt.append(cnt["s_mm"])
                    xlast["w2_user"] = cnt["s_mm"]

                for hg in range(8):
                    TE(lambda t, f=scores_hg, hg=hg: f(t, hg=hg))
                    SC(lambda s, f=relu_hg, hg=hg: f(s, hg=hg))
                    VE(lambda v, f=p_hg, hg=hg: f(v, hg=hg))
                    if hg >= 1:
                        TE(lambda t, f=out_hg, hg=hg - 1: f(t, hg=hg))
                TE(lambda t, f=out_hg: f(t, hg=7))

                def a_add(vector, h=h):
                    waits(vector, ["s_mm"])
                    sig(vector.tensor_tensor(TA[:, (2 * h) * T:(2 * h + 1) * T],
                                             TA[:, (2 * h) * T:(2 * h + 1) * T],
                                             psO0[:], OP.add), "s_vd")
                    sig(vector.tensor_tensor(TA[:, (2 * h + 1) * T:(2 * h + 2) * T],
                                             TA[:, (2 * h + 1) * T:(2 * h + 2) * T],
                                             psO1[:], OP.add), "s_vd")
                VE(a_add)
            rb_free[i] = cnt["s_vd"]

            stats(TA, is_ln01=False)

            def apply2(vector, Xb=Xb):
                v = vector
                for c in range(NC):
                    sig(v.tensor_tensor(TA[:, c * T:(c + 1) * T],
                                        TA[:, c * T:(c + 1) * T], GS[:, 0:T], OP.subtract), "s_vd")
                v.wait_ge(s_vd, cnt["s_vd"])
                for c in range(NC):
                    sig(v.tensor_mul(TA[:, c * T:(c + 1) * T],
                                     TA[:, c * T:(c + 1) * T], BB[:]), "s_vd")
                v.wait_ge(s_vd, cnt["s_vd"])
                for c in range(NC):
                    sig(v.tensor_add(Xb[:, c * T:(c + 1) * T],
                                     Xb[:, c * T:(c + 1) * T], TA[:, c * T:(c + 1) * T]), "s_vd")
            VE(apply2)

            stats(Xb, is_ln01=False)

            def apply3(vector, Xb=Xb):
                v = vector
                for c in range(NC):
                    sig(v.tensor_tensor(TA[:, c * T:(c + 1) * T],
                                        Xb[:, c * T:(c + 1) * T], GS[:, 0:T], OP.subtract), "s_vd")
                v.wait_ge(s_vd, cnt["s_vd"])
                for c in range(NC):
                    sig(v.tensor_mul(U[:, c * T:(c + 1) * T],
                                     TA[:, c * T:(c + 1) * T], BB[:]), "s_vd")
            VE(apply3)

            h_mm_cnt = {}
            h_ev_cnt = {}
            gelu_end = {}
            w1_cnt = {}
            for q in range(4):
                def w1_load(sync, q=q):
                    sync.wait_ge(s_su, startup_x_cnt)
                    if q >= 1:
                        sync.wait_ge(s_mm, h_mm_cnt[(q - 1, 1, 3)])
                    else:
                        sync.wait_ge(s_mm, xlast["w1_user"])
                    for c in range(NC):
                        inst = sync.dma_start(
                            out=W1B[:, c * 1024:(c + 1) * 1024],
                            in_=w1h[128 * c:128 * (c + 1), 1024 * q:1024 * (q + 1)])
                        sig(inst, "s_w1", 16)
                    w1_cnt[q] = cnt["s_w1"]
                SY(w1_load)

                for half in range(2):
                    gb = 2 * q + half

                    for k in range(4):
                        def h_mm_k(tensor, q=q, half=half, k=k):
                            if q == 0 and half == 0 and k == 0:
                                waits(tensor, ["s_vd", "s_sc"])
                            tensor.wait_ge(s_w1, w1_cnt[q])
                            jf = 4 * half + k
                            ph = [psH0, psH1][k % 2]
                            prev = ((q, half, k - 2) if k >= 2 else
                                    ((q, half - 1, k + 2) if half else (q - 1, 1, k + 2)))
                            if prev in h_ev_cnt:
                                tensor.wait_ge(s_sc, h_ev_cnt[prev])
                            for c in range(NC):
                                inst = tensor.matmul(
                                    ph[:], W1B[:, c * 1024 + jf * 128:c * 1024 + (jf + 1) * 128],
                                    U[:, c * T:(c + 1) * T],
                                    start=(c == 0), stop=(c == NC - 1))
                            sig(inst, "s_mm")
                            h_mm_cnt[(q, half, k)] = cnt["s_mm"]
                            xlast["w1_user"] = cnt["s_mm"]
                        TE(h_mm_k)

                        def h_evac_k(scalar, q=q, half=half, gb=gb, k=k):
                            if k == 0 and gb >= 2:
                                scalar.wait_ge(s_vd, gelu_end[gb - 2])
                            Wb = [WQ, WQ2][gb % 2]
                            Zb = [GZ, GZ2][gb % 2]
                            jf = 4 * half + k
                            fg = 8 * q + jf
                            ph = [psH0, psH1][k % 2]
                            scalar.wait_ge(s_mm, h_mm_cnt[(q, half, k)])
                            inst = scalar.activation(Wb[:, k * T:(k + 1) * T], ph[:],
                                                     AF.Identity,
                                                     bias=B1Q[:, fg:fg + 1], scale=256.0)
                            sig(inst, "s_sc")
                            inst = scalar.activation(Zb[:, k * T:(k + 1) * T], ph[:],
                                                     AF.Identity,
                                                     bias=B1Q[:, fg:fg + 1], scale=256.0)
                            sig(inst, "s_sc")
                            h_ev_cnt[(q, half, k)] = cnt["s_sc"]
                        SC(h_evac_k)

                    def gelu(vector, q=q, half=half, gb=gb):
                        Wb = [WQ, WQ2][gb % 2]
                        Zb = [GZ, GZ2][gb % 2]
                        vector.wait_ge(s_sc, h_ev_cnt[(q, half, 3)])
                        fg0 = 8 * q + 4 * half
                        cols = slice(fg0 * T, (fg0 + 4) * T)
                        v = vector
                        vc(v, lambda: v.tensor_tensor(GD[:], Wb[:], Zb[:], OP.subtract))
                        vc(v, lambda: v.scalar_tensor_tensor(GA[:], GD[:], 0.5, Zb[:], OP.is_gt, OP.add))
                        vc(v, lambda: v.tensor_scalar(GS[:], GD[:], -0.5, None, OP.is_lt))
                        vc(v, lambda: v.tensor_tensor(GD[:], GA[:], GS[:], OP.subtract))
                        vc(v, lambda: v.tensor_tensor(GA[:], Wb[:], GD[:], OP.subtract))
                        vc(v, lambda: v.scalar_tensor_tensor(GS[:], GA[:], 0.5, GD[:], OP.is_gt, OP.add))
                        vc(v, lambda: v.tensor_scalar(Wb[:], GA[:], -0.5, None, OP.is_lt))
                        vc(v, lambda: v.tensor_tensor(GA[:], GS[:], Wb[:], OP.subtract))
                        vc(v, lambda: v.tensor_scalar(GS[:], GA[:], BIAS, 1.0 / 64.0,
                                                      OP.subtract, OP.mult))
                        vc(v, lambda: v.scalar_tensor_tensor(GS[:], GS[:], -1.0, GS[:],
                                                             OP.mult, OP.max))
                        vc(v, lambda: v.tensor_scalar(GD[:], GA[:], BIAS, 1.0 / 512.0,
                                                      OP.subtract, OP.mult))
                        vc(v, lambda: v.scalar_tensor_tensor(GB[:], GS[:], float(LAM0), GD[:],
                                                             OP.mult, OP.add))
                        vc(v, lambda: v.tensor_scalar(GB[:], GB[:], float(CONST_OFF), None,
                                                      OP.subtract))
                        mbuf = [GD, Wb, GA]
                        ts_c = [None] * 8
                        m_c = [None] * 8
                        acc_c = [None] * 9
                        acc_c[0] = cnt["s_vd"]
                        base = cnt["s_vd"]

                        def e_ts(j):
                            mb = mbuf[j % 3]
                            dep = acc_c[j - 2] if j >= 3 else base
                            vc(v, lambda: v.tensor_scalar(
                                mb[:], GS[:], float(2 * j + 1), float(PAIR_RHO[j]),
                                OP.max, OP.mult), dep=dep)
                            ts_c[j] = cnt["s_vd"]

                        def e_m(j):
                            mb = mbuf[j % 3]
                            vc(v, lambda: v.scalar_tensor_tensor(
                                mb[:], GS[:], float(2 * j + 2), mb[:], OP.max, OP.add),
                                dep=ts_c[j])
                            m_c[j] = cnt["s_vd"]

                        def e_acc(j):
                            mb = mbuf[j % 3]
                            vc(v, lambda: v.scalar_tensor_tensor(
                                GB[:], mb[:], float(PAIR_EB[j]), GB[:], OP.mult, OP.add),
                                dep=max(m_c[j], acc_c[j]))
                            acc_c[j + 1] = cnt["s_vd"]

                        e_ts(0); e_ts(1); e_m(0); e_ts(2); e_m(1); e_acc(0)
                        for j in range(1, 6):
                            e_m(j + 1); e_acc(j); e_ts(j + 2)
                        e_m(7); e_acc(6); e_acc(7)
                        vc(v, lambda: v.tensor_scalar(Wb[:], GB[:], 256.0, BIAS,
                                                      OP.mult, OP.add))
                        round2(v, Wb, Zb, GD, GS, GA)
                        vc(v, lambda: v.tensor_scalar(HB[:, cols], GA[:], BIAS, 1.0 / 256.0,
                                                      OP.subtract, OP.mult))
                        gelu_end[gb] = cnt["s_vd"]
                    VE(gelu)

            o_mm_cnt = {}
            o_ev_cnt = {}
            o_rnd_end = {}
            for Q in range(4):
                def w2_load(sync, Q=Q):
                    sync.wait_ge(s_su, startup_x_cnt)
                    WB = [W2B, W1B][Q % 2]
                    if Q >= 2:
                        sync.wait_ge(s_mm, o_mm_cnt[(Q - 2, 1)])
                    elif Q == 0:
                        sync.wait_ge(s_mm, xlast["w2_user"])
                    else:
                        sync.wait_ge(s_mm, xlast["w1_user"])
                    sync.wait_ge(s_w2, cnt["s_w2"])
                    for j in range(NF):
                        inst = sync.dma_start(
                            out=WB[:, j * 256:(j + 1) * 256],
                            in_=w2h[128 * j:128 * (j + 1), 256 * Q:256 * (Q + 1)])
                        sig(inst, "s_w2", 16)
                    o_mm_cnt[("w2x", Q)] = cnt["s_w2"]
                SY(w2_load)

                def o_mm(tensor, Q=Q):
                    if Q == 0:
                        waits(tensor, ["s_vd", "s_sc"])
                    WB = [W2B, W1B][Q % 2]
                    tensor.wait_ge(s_w2, o_mm_cnt[("w2x", Q)])
                    for dd in range(2):
                        ph = [psH0, psH1][dd]
                        if (Q - 1, dd) in o_ev_cnt:
                            tensor.wait_ge(s_sc, o_ev_cnt[(Q - 1, dd)])
                        for j in range(NF):
                            inst = tensor.matmul(
                                ph[:], WB[:, j * 256 + dd * 128:j * 256 + (dd + 1) * 128],
                                HB[:, j * T:(j + 1) * T],
                                start=(j == 0), stop=(j == NF - 1))
                        sig(inst, "s_mm")
                        o_mm_cnt[(Q, dd)] = cnt["s_mm"]
                        if Q % 2 == 0:
                            xlast["w2_user"] = cnt["s_mm"]
                        else:
                            xlast["w1_user"] = cnt["s_mm"]
                TE(o_mm)

                def o_evac(scalar, Q=Q):
                    if Q >= 2:
                        scalar.wait_ge(s_vd, o_rnd_end[Q - 2])
                    Wb = [WQ, WQ2][Q % 2]
                    for dd in range(2):
                        cq = 2 * Q + dd
                        ph = [psH0, psH1][dd]
                        scalar.wait_ge(s_mm, o_mm_cnt[(Q, dd)])
                        inst = scalar.activation(Wb[:, dd * T:(dd + 1) * T], ph[:],
                                                 AF.Identity,
                                                 bias=B2Q[:, cq:cq + 1], scale=256.0)
                        sig(inst, "s_sc")
                        o_ev_cnt[(Q, dd)] = cnt["s_sc"]
                SC(o_evac)

                def o_round(vector, Q=Q):
                    vector.wait_ge(s_sc, o_ev_cnt[(Q, 1)])
                    Wb = [WQ, WQ2][Q % 2]
                    v = vector
                    rhe(v, Wb, GZ, GD, GS, GA, GB, 2 * T)
                    vc(v, lambda: v.tensor_scalar(TA[:, 2 * Q * T:(2 * Q + 2) * T],
                                                  GA[:, 0:2 * T], BIAS, 1.0 / 256.0,
                                                  OP.subtract, OP.mult))
                    o_rnd_end[Q] = cnt["s_vd"]
                VE(o_round)

            if i + 1 < NT:
                load_x(i + 1, dep=dict(vd=ap4_end.get(i - 1, 0),
                                       y=store_end.get(i - 1, 0),
                                       sc=st01_cast.get(i - 1, 0)))
                xload_end[i + 1] = cnt["s_x"]

            ln4 = stats(TA, is_ln01=False)
            ln4_mm[i] = ln4["mm"]
            ln4_chain[i] = ln4["chain"]

            def apply4(vector, Xb=Xb):
                v = vector
                for c in range(NC):
                    sig(v.tensor_tensor(TA[:, c * T:(c + 1) * T],
                                        TA[:, c * T:(c + 1) * T], GS[:, 0:T], OP.subtract), "s_vd")
                v.wait_ge(s_vd, cnt["s_vd"])
                for c in range(NC):
                    sig(v.tensor_mul(TA[:, c * T:(c + 1) * T],
                                     TA[:, c * T:(c + 1) * T], BB[:]), "s_vd")
                v.wait_ge(s_vd, cnt["s_vd"])
                for c in range(NC):
                    sig(v.tensor_add(Xb[:, c * T:(c + 1) * T],
                                     Xb[:, c * T:(c + 1) * T], TA[:, c * T:(c + 1) * T]), "s_vd")
            VE(apply4)
            ap4_end[i] = cnt["s_vd"]

            def store(sync, i=i, Xb=Xb):
                waits(sync, ["s_vd"])
                for c in range(NC):
                    inst = sync.dma_start(
                        out=yout[128 * c:128 * (c + 1), T * i:T * (i + 1)],
                        in_=Xb[:, c * T:(c + 1) * T])
                    sig(inst, "s_y", 16)
            SY(store)
            store_end[i] = cnt["s_y"]

        def fin(sync):
            sync.wait_ge(s_y, cnt["s_y"])
        SY(fin)

    return nc



_NC_CACHE = {}


def kernel(L0_attn_residual, L0_attn_w_fc, L0_attn_w_proj, L0_attn_ln_g,
           L0_attn_ln_b, L0_ffn_w1, L0_ffn_b1, L0_ffn_w2, L0_ffn_b2):
    x = np.ascontiguousarray(np.asarray(L0_attn_residual, dtype=np.float32))
    w_fc = np.asarray(L0_attn_w_fc, dtype=np.float32)
    w_proj = np.asarray(L0_attn_w_proj, dtype=np.float32)
    ln_g = np.asarray(L0_attn_ln_g, dtype=np.float32)
    ln_b = np.asarray(L0_attn_ln_b, dtype=np.float32)
    w1 = np.ascontiguousarray(np.asarray(L0_ffn_w1, dtype=np.float32))
    b1 = np.asarray(L0_ffn_b1, dtype=np.float32)
    w2 = np.ascontiguousarray(np.asarray(L0_ffn_w2, dtype=np.float32))
    b2 = np.asarray(L0_ffn_b2, dtype=np.float32)

    wkt = np.ascontiguousarray(w_fc.reshape(FF, 256).T)
    wv = np.ascontiguousarray(np.ascontiguousarray(w_proj.T).reshape(FF, 256))
    lng = np.ascontiguousarray(ln_g.reshape(NC, 128).T)
    lnb = np.ascontiguousarray(ln_b.reshape(NC, 128).T)
    b1t = np.ascontiguousarray(b1.reshape(NF, 128).T)
    b2t = np.ascontiguousarray(b2.reshape(NC, 128).T)

    xf = x.reshape(B * TT, C)
    in_maps = []
    for core in range(N_CORES):
        xs = np.ascontiguousarray(xf[core * TOK:(core + 1) * TOK, :].T)
        in_maps.append(dict(x=xs, wkt=wkt, wv=wv, w1=w1, w2=w2,
                            lng=lng, lnb=lnb, b1t=b1t, b2t=b2t))

    if "nc" not in _NC_CACHE:
        _NC_CACHE["nc"] = build_nc()
    nc = _NC_CACHE["nc"]
    res = run_bass_kernel_spmd(nc, in_maps, core_ids=list(range(N_CORES)))

    out = np.empty((B * TT, C), dtype=np.float32)
    for core in range(N_CORES):
        out[core * TOK:(core + 1) * TOK, :] = res.results[core]["y"].T
    return out.reshape(B, TT, C)


# version 19
# speedup vs baseline: 4.7343x; 1.1808x over previous
import contextlib
import struct
import numpy as np
import ml_dtypes

import concourse.bass as bass
import concourse.mybir as mybir
from concourse.bass_utils import run_bass_kernel_spmd

F32 = mybir.dt.float32
F16 = mybir.dt.float16
I32 = mybir.dt.int32
AF = mybir.ActivationFunctionType
OP = mybir.AluOpType

N_CORES = 8
B, TT, C = 8, 8192, 1024
TOK = B * TT // N_CORES
T = 512
NT = TOK // T
NC = C // 128
FF = 4096
NF = FF // 128
EPS = 1e-5
BIAS = 8192.0

_YBITS = [0xb904c000, 0xb9adbb00, 0xba556b00, 0xbaf5ce80, 0xbb84b340, 0xbc064180,
          0xbc7e5920, 0xbce15244, 0xbd3a5e80, 0xbd8f926e, 0xbdcd3b48, 0xbe073b54,
          0xbe227686, 0xbe2e0cbe, 0xbe1df8a2, 0xbdcd765d, 0x00000000, 0x3e1944d2,
          0x3eb103af, 0x3f147cd0, 0x3f57625e, 0x3f8f1895, 0x3fb32c4c, 0x3fd706d9,
          0x3ffa2d0c, 0x400e3d5c, 0x401f01a7, 0x402f79be, 0x403fbda6, 0x404fe146,
          0x405ff2aa, 0x406ffa93, 0x407ffded]
_Y = np.array([struct.unpack('<f', struct.pack('<I', b))[0] for b in _YBITS],
              dtype=np.float32)
_X = np.linspace(-4, 4, 33).astype(np.float32)


def _gelu_consts():
    E = _Y.astype(np.float64)[16:] - _X.astype(np.float64)[16:] / 2.0
    K1 = np.diff(E)
    lam0 = np.float32(K1[0])
    e = np.diff(np.concatenate([K1, [0.125]])).astype(np.float32)
    const_off = np.float32(np.sum(e.astype(np.float64) * np.arange(1, 17)))
    w2off = np.float32(BIAS - 256.0 * np.float64(const_off))
    return lam0, e, w2off


LAM0, ECOEF, W2OFF = _gelu_consts()
PAIR_RHO = [np.float32(np.float64(ECOEF[2 * j]) / np.float64(ECOEF[2 * j + 1]))
            for j in range(8)]
PAIR_EB = [ECOEF[2 * j + 1] for j in range(8)]


def _verify_gelu_chain():
    def ref(xq):
        y = np.interp(xq, _X.astype(np.float64), _Y.astype(np.float64))
        return np.where(xq > 4.0, xq, np.where(xq < -4.0, 0.0, y))

    A0 = np.arange(-16384, 16385).astype(np.float64) + BIAS
    y2_ref = np.round(256.0 * ref((A0 - BIAS) / 256.0))
    for conv in (np.trunc, np.round):
        for jit in (0.0, 0.45, -0.45):
            z = conv(A0 + jit)
            d = A0 - z
            A = ((d > 0.5) + z) - (d < -0.5)
            assert np.array_equal(A, A0), "site-1 round failed"
            s = np.abs((A - BIAS) * np.float64(1.0 / 64.0))
            acc = (A - BIAS) * np.float64(1.0 / 512.0)
            acc = np.float64(LAM0) * s + acc
            for j in range(8):
                rho, eb = PAIR_RHO[j], PAIR_EB[j]
                u = np.maximum(s, np.float64(2 * j + 1)) * np.float64(rho)
                u = np.maximum(s, np.float64(2 * j + 2)) + u
                acc = u * np.float64(eb) + acc
            w2 = acc * 256.0 + np.float64(W2OFF)
            fr = w2 - np.floor(w2)
            assert np.abs(fr - 0.5).min() > 1e-4, "site-2 rounding tie on lattice"
            z2 = conv(w2 + jit)
            d2 = w2 - z2
            t2 = (d2 > 0.5) + z2
            n2 = (d2 < -0.5) - t2
            q2 = -n2 - BIAS
            assert np.array_equal(q2, y2_ref), "gelu chain verification failed"


def _verify_rhe_chain():
    k = np.arange(-40000, 40001).astype(np.float64)
    w = np.concatenate([k * 0.5, k * 0.5 + 0.25, k * (1.0 / 3.0)])
    ref = np.round(w)
    half_away = lambda x: np.trunc(x) + np.trunc((x - np.trunc(x)) * 2.0)

    def nfloor(bv, conv):
        z = conv(bv)
        d = bv - z
        t = (d >= 1.0) + z
        return (d < 0.0) - t

    for conv in (np.trunc, np.round, np.floor, np.ceil, half_away):
        nf1 = nfloor(w * 0.5 + 0.25, conv)
        nf2 = nfloor(w * (-0.5) + 0.25, conv)
        neg_rhe = nf1 - nf2
        assert np.array_equal(-neg_rhe, ref), "rhe chain verification failed"


_verify_gelu_chain()
_verify_rhe_chain()



def build_nc():
    nc = bass.Bass("TRN2", target_bir_lowering=False)

    xin = nc.dram_tensor("x", [C, TOK], F32, kind="ExternalInput").ap()
    wkt = nc.dram_tensor("wkt", [256, FF], F32, kind="ExternalInput").ap()
    wv = nc.dram_tensor("wv", [FF, 256], F32, kind="ExternalInput").ap()
    w1 = nc.dram_tensor("w1", [C, FF], F32, kind="ExternalInput").ap()
    w2 = nc.dram_tensor("w2", [FF, C], F32, kind="ExternalInput").ap()
    lng_in = nc.dram_tensor("lng", [128, NC], F32, kind="ExternalInput").ap()
    lnb_in = nc.dram_tensor("lnb", [128, NC], F32, kind="ExternalInput").ap()
    b1_in = nc.dram_tensor("b1t", [128, NF], F32, kind="ExternalInput").ap()
    b2_in = nc.dram_tensor("b2t", [128, NC], F32, kind="ExternalInput").ap()
    yout = nc.dram_tensor("y", [C, TOK], F32, kind="ExternalOutput").ap()

    kth = nc.dram_tensor("kth", [256, FF], F16, kind="Internal").ap()
    wvh = nc.dram_tensor("wvh", [FF, 256], F16, kind="Internal").ap()
    w1h = nc.dram_tensor("w1h", [C, FF], F16, kind="Internal").ap()
    w2h = nc.dram_tensor("w2h", [FF, C], F16, kind="Internal").ap()

    es = contextlib.ExitStack()
    sb = lambda name, shape, dt: es.enter_context(nc.sbuf_tensor(name, shape, dt))
    ps = lambda name, shape: es.enter_context(nc.psum_tensor(name, shape, F32))
    sem = lambda name: es.enter_context(nc.semaphore(name))

    with es:
        X0 = sb("X0", [128, NC * T], F32)
        X1 = sb("X1", [128, NC * T], F32)
        TA = sb("TA", [128, NC * T], F32)
        U = sb("U", [128, NC * T], F16)
        W1B = sb("W1B", [128, 8192], F16)
        W2B = sb("W2B", [128, 8192], F16)
        RB = sb("RB", [128, 4096], F16)
        PB = sb("PB", [128, 4096], F16)
        HB = sb("HB", [128, NF * T], F16)
        WQ = sb("WQ", [128, 2048], F32)
        WQ2 = sb("WQ2", [128, 2048], F32)
        GZ = sb("GZ", [128, 2048], I32)
        GZ2 = sb("GZ2", [128, 2048], I32)
        GD = sb("GD", [128, 2048], F32)
        GS = sb("GS", [128, 2048], F32)
        GA = sb("GA", [128, 2048], F32)
        GB = sb("GB", [128, 2048], F32)
        VB = sb("VB", [128, T], F32)
        AB = sb("AB", [128, T], F32)
        BB = sb("BB", [128, T], F32)
        LNG = sb("LNG", [128, NC], F32)
        LNB = sb("LNB", [128, NC], F32)
        B1T = sb("B1T", [128, NF], F32)
        B2T = sb("B2T", [128, NC], F32)
        B1Q = sb("B1Q", [128, NF], F32)
        B2Q = sb("B2Q", [128, NC], F32)
        ONES = sb("ONES", [128, 128], F16)
        CEPS = sb("CEPS", [128, 1], F32)

        psS1 = ps("psS1", [128, T])
        psS2 = ps("psS2", [128, T])
        psC0 = ps("psC0", [128, T])
        psC1 = ps("psC1", [128, T])
        psO0 = ps("psO0", [128, T])
        psO1 = ps("psO1", [128, T])
        psH0 = ps("psH0", [128, T])
        psH1 = ps("psH1", [128, T])

        s_x = sem("s_x")
        s_w1 = sem("s_w1")
        s_w2 = sem("s_w2")
        s_su = sem("s_su")
        s_y = sem("s_y")
        s_mm = sem("s_mm")
        s_sc = sem("s_sc")
        s_vd = sem("s_vd")

        cnt = {"s_x": 0, "s_y": 0, "s_mm": 0, "s_sc": 0, "s_vd": 0,
               "s_w1": 0, "s_w2": 0, "s_su": 0}
        hmap = {"s_x": s_x, "s_y": s_y, "s_mm": s_mm, "s_sc": s_sc, "s_vd": s_vd,
                "s_w1": s_w1, "s_w2": s_w2, "s_su": s_su}

        blk = es.enter_context(nc.Block())

        def sig(inst, name, k=1):
            cnt[name] += k
            inst.then_inc(hmap[name], k)

        def waits(eng, names):
            for n in names:
                eng.wait_ge(hmap[n], cnt[n])

        X = [X0, X1]
        xlast = {"w1_user": 0, "w2_user": 0}

        def vc(vector, emit, dep=None):
            vector.wait_ge(s_vd, cnt["s_vd"] if dep is None else dep)
            sig(emit(), "s_vd")

        def sc_op(scalar, emit):
            scalar.wait_ge(s_sc, cnt["s_sc"])
            sig(emit(), "s_sc")

        def SY(f):
            blk.sync(f)

        def TE(f):
            blk.tensor(f)

        def VE(f):
            def g(vector):
                vector.wait_ge(s_vd, cnt["s_vd"])
                f(vector)
            blk.vector(g)

        def SC(f):
            def g(scalar):
                scalar.wait_ge(s_sc, cnt["s_sc"])
                f(scalar)
            blk.scalar(g)

        def startup_sync(sync):
            sig(sync.dma_start(out=LNG[:], in_=lng_in[:, :]), "s_su", 16)
            sig(sync.dma_start(out=LNB[:], in_=lnb_in[:, :]), "s_su", 16)
            sig(sync.dma_start(out=B1T[:], in_=b1_in[:, :]), "s_su", 16)
            sig(sync.dma_start(out=B2T[:], in_=b2_in[:, :]), "s_su", 16)
        SY(startup_sync)

        def startup_vec(vector):
            waits(vector, ["s_su"])
            sig(vector.memset(ONES[:], 1.0 / 1024.0), "s_vd")
            sig(vector.memset(CEPS[:], EPS), "s_vd")
            sig(vector.tensor_scalar(B1Q[:], B1T[:], 256.0, BIAS, OP.mult, OP.add), "s_vd")
            sig(vector.tensor_scalar(B2Q[:], B2T[:], 256.0, BIAS, OP.mult, OP.add), "s_vd")
        VE(startup_vec)

        def conv_round(loads, dst_sbuf_ap=None, stores=None):
            def ld(sync):
                waits(sync, ["s_vd"])
                for (ap, c0, nc_) in loads:
                    sig(sync.dma_start(out=X0[:, c0:c0 + nc_], in_=ap), "s_su", 16)
            def cv(vector):
                waits(vector, ["s_su"])
                if dst_sbuf_ap is not None:
                    sig(vector.tensor_copy(dst_sbuf_ap, X0[:, 0:src_cols]), "s_vd")
                else:
                    sig(vector.tensor_copy(RB[:, 0:src_cols], X0[:, 0:src_cols]), "s_vd")
            SY(ld)
            VE(cv)
            if stores:
                def st(sync):
                    waits(sync, ["s_vd"])
                    for (ap, c0, nc_) in stores:
                        sig(sync.dma_start(out=ap, in_=RB[:, c0:c0 + nc_]), "s_su", 16)
                SY(st)

        src_cols = 4096
        conv_round([(wkt[0:128, :], 0, 4096)], stores=[(kth[0:128, :], 0, 4096)])
        conv_round([(wkt[128:256, :], 0, 4096)], stores=[(kth[128:256, :], 0, 4096)])
        for half16 in range(2):
            j0 = 16 * half16
            conv_round([(wv[128 * (j0 + j):128 * (j0 + j + 1), :], j * 256, 256)
                        for j in range(16)],
                       stores=[(wvh[128 * (j0 + j):128 * (j0 + j + 1), :], j * 256, 256)
                               for j in range(16)])
        for c in range(NC):
            conv_round([(w1[128 * c:128 * (c + 1), :], 0, 4096)],
                       stores=[(w1h[128 * c:128 * (c + 1), :], 0, 4096)])
        for r in range(8):
            conv_round([(w2[128 * (4 * r + j):128 * (4 * r + j + 1), :], j * 1024, 1024)
                        for j in range(4)],
                       stores=[(w2h[128 * (4 * r + j):128 * (4 * r + j + 1), :], j * 1024, 1024)
                               for j in range(4)])
        startup_x_cnt = cnt["s_su"]


        def floor2n(v, Bv, Z, D, S, nw):
            w_ = slice(0, nw)
            vc(v, lambda: v.tensor_copy(Z[:, w_], Bv[:, w_]))
            vc(v, lambda: v.tensor_tensor(D[:, w_], Bv[:, w_], Z[:, w_], OP.subtract))
            vc(v, lambda: v.scalar_tensor_tensor(S[:, w_], D[:, w_], 1.0, Z[:, w_], OP.is_ge, OP.add))
            vc(v, lambda: v.scalar_tensor_tensor(S[:, w_], D[:, w_], 0.0, S[:, w_], OP.is_lt, OP.subtract))

        def rhe(v, W, Z, D, S, A, Bv, nw):
            w_ = slice(0, nw)
            vc(v, lambda: v.tensor_scalar(Bv[:, w_], W[:, w_], 0.5, 0.25, OP.mult, OP.add))
            floor2n(v, Bv, Z, D, A, nw)
            vc(v, lambda: v.tensor_scalar(Bv[:, w_], W[:, w_], -0.5, 0.25, OP.mult, OP.add))
            floor2n(v, Bv, Z, D, S, nw)
            vc(v, lambda: v.tensor_tensor(A[:, w_], A[:, w_], S[:, w_], OP.subtract))

        def load_x(i, dep=None):
            Xb = X[i % 2]
            def f(sync):
                if dep is None:
                    waits(sync, ["s_vd", "s_y"])
                else:
                    sync.wait_ge(s_vd, dep["vd"])
                    sync.wait_ge(s_y, dep["y"])
                    sync.wait_ge(s_sc, dep["sc"])
                for c in range(NC):
                    inst = sync.dma_start(
                        out=Xb[:, c * T:(c + 1) * T],
                        in_=xin[128 * c:128 * (c + 1), T * i:T * (i + 1)])
                    sig(inst, "s_x", 16)
            SY(f)

        def stats(src, is_ln01, cast_deps=None, mm_vd=None):
            rec = {}
            def cast(scalar):
                if cast_deps is None:
                    waits(scalar, ["s_vd", "s_mm", "s_x", "s_su"])
                else:
                    for sm, val in cast_deps:
                        scalar.wait_ge(sm, val)
                sig(scalar.activation(RB[:], src[:], AF.Identity, bias=0.0, scale=1.0), "s_sc")
                sig(scalar.activation(PB[:], src[:], AF.Square, bias=0.0, scale=1.0), "s_sc")
            SC(cast)
            rec["cast"] = cnt["s_sc"]

            def mm(tensor):
                waits(tensor, ["s_sc"])
                if mm_vd is not None:
                    tensor.wait_ge(s_vd, mm_vd)
                for c in range(NC):
                    tensor.matmul(psS1[:], ONES[:], RB[:, c * T:(c + 1) * T],
                                  start=(c == 0), stop=(c == NC - 1))
                for c in range(NC):
                    inst = tensor.matmul(psS2[:], ONES[:], PB[:, c * T:(c + 1) * T],
                                         start=(c == 0), stop=(c == NC - 1))
                sig(inst, "s_mm")
            TE(mm)
            rec["mm"] = cnt["s_mm"]

            def chain(vector):
                waits(vector, ["s_mm"])
                vc(vector, lambda: vector.tensor_copy(GS[:, 0:T], psS1[:]))
                vc(vector, lambda: vector.tensor_mul(VB[:], GS[:, 0:T], GS[:, 0:T]))
                vc(vector, lambda: vector.tensor_tensor(VB[:], psS2[:], VB[:], OP.subtract))
            VE(chain)
            rec["chain"] = cnt["s_vd"]

            def chain_sc(scalar):
                waits(scalar, ["s_vd"])
                inst = scalar.activation(GS[:, T:2 * T], VB[:], AF.Sqrt, bias=CEPS[:], scale=1.0)
                sig(inst, "s_sc")
            SC(chain_sc)

            def recip1(vector):
                waits(vector, ["s_sc"])
                vc(vector, lambda: vector.reciprocal(BB[:], GS[:, T:2 * T]))
            VE(recip1)

            if is_ln01:
                def chain2(vector):
                    vc(vector, lambda: vector.tensor_scalar(GS[:, T:2 * T], VB[:], 1.0 + EPS, EPS * EPS,
                                                            OP.mult, OP.add))
                VE(chain2)

                def chain2_sc(scalar):
                    waits(scalar, ["s_vd"])
                    inst = scalar.activation(GS[:, T:2 * T], GS[:, T:2 * T], AF.Sqrt, bias=0.0, scale=1.0)
                    sig(inst, "s_sc")
                SC(chain2_sc)

                def recip2(vector):
                    waits(vector, ["s_sc"])
                    vc(vector, lambda: vector.reciprocal(AB[:], GS[:, T:2 * T]))
                VE(recip2)
            return rec

        load_x(0)
        ap4_end = {}; store_end = {}; st01_cast = {}; xload_end = {}
        ln4_mm = {}; ln4_chain = {}; rb_free = {}
        for i in range(NT):
            Xb = X[i % 2]

            if i == 0:
                st01 = stats(Xb, is_ln01=True)
            else:
                st01 = stats(Xb, is_ln01=True,
                             cast_deps=[(s_x, xload_end[i]),
                                        (s_mm, ln4_mm[i - 1]),
                                        (s_vd, rb_free[i - 1]),
                                        (s_su, startup_x_cnt)],
                             mm_vd=ln4_chain[i - 1])
            st01_cast[i] = st01["cast"]

            def apply01(vector, Xb=Xb):
                v = vector
                for c in range(NC):
                    sig(v.tensor_tensor(TA[:, c * T:(c + 1) * T],
                                        Xb[:, c * T:(c + 1) * T], GS[:, 0:T], OP.subtract), "s_vd")
                v.wait_ge(s_vd, cnt["s_vd"])
                for c in range(NC):
                    sig(v.tensor_mul(U[:, c * T:(c + 1) * T],
                                     TA[:, c * T:(c + 1) * T], AB[:]), "s_vd")
                v.wait_ge(s_vd, cnt["s_vd"])
                for c in range(NC):
                    sig(v.tensor_scalar(U[:, c * T:(c + 1) * T], U[:, c * T:(c + 1) * T],
                                        LNG[:, c:c + 1], LNB[:, c:c + 1],
                                        OP.mult, OP.add), "s_vd")
                v.wait_ge(s_vd, cnt["s_vd"])
                for c in range(NC):
                    sig(v.tensor_mul(TA[:, c * T:(c + 1) * T],
                                     TA[:, c * T:(c + 1) * T], BB[:]), "s_vd")
            VE(apply01)

            att_cnt = {}

            def attn_w_load(sync):
                sync.wait_ge(s_su, startup_x_cnt)
                sync.wait_ge(s_mm, xlast["w1_user"])
                sig(sync.dma_start(out=W1B[:, 0:4096], in_=kth[0:128, :]), "s_w1", 16)
                sig(sync.dma_start(out=W1B[:, 4096:8192], in_=kth[128:256, :]), "s_w1", 16)
                att_cnt["kt"] = cnt["s_w1"]
                sync.wait_ge(s_mm, xlast["w2_user"])
                sync.wait_ge(s_w2, cnt["s_w2"])
                for j in range(NF):
                    sig(sync.dma_start(out=W2B[:, j * 256:(j + 1) * 256],
                                       in_=wvh[128 * j:128 * (j + 1), :]), "s_w2", 16)
                att_cnt["v"] = cnt["s_w2"]
            SY(attn_w_load)

            ps_sc = [psC0, psC1, psH0, psH1]
            for h in range(4):
                relu_cnt = []
                p_cnt = []
                out_cnt = []
                mm_cnt = []

                def scores_hg(tensor, h=h, hg=None, relu_cnt=relu_cnt):
                    if h == 0 and hg == 0:
                        waits(tensor, ["s_vd", "s_sc"])
                        tensor.wait_ge(s_w1, att_cnt["kt"])
                    for jj in range(4 * hg, 4 * hg + 4):
                        pc = ps_sc[jj % 4]
                        if jj >= 4:
                            tensor.wait_ge(s_sc, relu_cnt[jj - 4])
                        tensor.matmul(pc[:], W1B[:, jj * 128:(jj + 1) * 128],
                                      U[:, (2 * h) * T:(2 * h + 1) * T],
                                      start=True, stop=False, skip_group_check=True)
                        inst = tensor.matmul(pc[:], W1B[:, 4096 + jj * 128:4096 + (jj + 1) * 128],
                                             U[:, (2 * h + 1) * T:(2 * h + 2) * T],
                                             start=False, stop=True, skip_group_check=True)
                        sig(inst, "s_mm")
                        mm_cnt.append(cnt["s_mm"])
                        xlast["w1_user"] = cnt["s_mm"]

                def relu_hg(scalar, h=h, hg=None, mm_cnt=mm_cnt, p_cnt=p_cnt,
                            relu_cnt=relu_cnt):
                    if hg >= 2:
                        scalar.wait_ge(s_vd, p_cnt[hg - 2])
                    for jj in range(4 * hg, 4 * hg + 4):
                        pc = ps_sc[jj % 4]
                        scalar.wait_ge(s_mm, mm_cnt[jj])
                        col = (hg % 2) * 2048 + (jj % 4) * T
                        inst = scalar.activation(RB[:, col:col + T], pc[:],
                                                 AF.Relu, bias=0.0, scale=1.0)
                        sig(inst, "s_sc")
                        relu_cnt.append(cnt["s_sc"])

                def p_hg(vector, h=h, hg=None, relu_cnt=relu_cnt, out_cnt=out_cnt,
                         p_cnt=p_cnt):
                    vector.wait_ge(s_sc, relu_cnt[4 * hg + 3])
                    if hg >= 2:
                        vector.wait_ge(s_mm, out_cnt[hg - 2])
                    half = (hg % 2) * 2048
                    sig(vector.tensor_mul(PB[:, half:half + 2048],
                                          RB[:, half:half + 2048],
                                          RB[:, half:half + 2048]), "s_vd")
                    p_cnt.append(cnt["s_vd"])

                def out_hg(tensor, h=h, hg=None, p_cnt=p_cnt, out_cnt=out_cnt):
                    tensor.wait_ge(s_vd, p_cnt[hg])
                    if h == 0 and hg == 0:
                        tensor.wait_ge(s_w2, att_cnt["v"])
                    half = (hg % 2) * 2048
                    for k in range(4):
                        jj = 4 * hg + k
                        tensor.matmul(psO0[:], W2B[:, jj * 256:jj * 256 + 128],
                                      PB[:, half + k * T:half + (k + 1) * T],
                                      start=(jj == 0), stop=(jj == 31),
                                      skip_group_check=True)
                        inst = tensor.matmul(psO1[:], W2B[:, jj * 256 + 128:jj * 256 + 256],
                                             PB[:, half + k * T:half + (k + 1) * T],
                                             start=(jj == 0), stop=(jj == 31),
                                             skip_group_check=True)
                    sig(inst, "s_mm")
                    out_cnt.append(cnt["s_mm"])
                    xlast["w2_user"] = cnt["s_mm"]

                for hg in range(8):
                    TE(lambda t, f=scores_hg, hg=hg: f(t, hg=hg))
                    SC(lambda s, f=relu_hg, hg=hg: f(s, hg=hg))
                    VE(lambda v, f=p_hg, hg=hg: f(v, hg=hg))
                    if hg >= 1:
                        TE(lambda t, f=out_hg, hg=hg - 1: f(t, hg=hg))
                TE(lambda t, f=out_hg: f(t, hg=7))

                def a_add(vector, h=h):
                    waits(vector, ["s_mm"])
                    sig(vector.tensor_tensor(TA[:, (2 * h) * T:(2 * h + 1) * T],
                                             TA[:, (2 * h) * T:(2 * h + 1) * T],
                                             psO0[:], OP.add), "s_vd")
                    sig(vector.tensor_tensor(TA[:, (2 * h + 1) * T:(2 * h + 2) * T],
                                             TA[:, (2 * h + 1) * T:(2 * h + 2) * T],
                                             psO1[:], OP.add), "s_vd")
                VE(a_add)
            rb_free[i] = cnt["s_vd"]

            stats(TA, is_ln01=False)

            def apply2(vector, Xb=Xb):
                v = vector
                for c in range(NC):
                    sig(v.tensor_tensor(TA[:, c * T:(c + 1) * T],
                                        TA[:, c * T:(c + 1) * T], GS[:, 0:T], OP.subtract), "s_vd")
                v.wait_ge(s_vd, cnt["s_vd"])
                for c in range(NC):
                    sig(v.tensor_mul(TA[:, c * T:(c + 1) * T],
                                     TA[:, c * T:(c + 1) * T], BB[:]), "s_vd")
                v.wait_ge(s_vd, cnt["s_vd"])
                for c in range(NC):
                    sig(v.tensor_add(Xb[:, c * T:(c + 1) * T],
                                     Xb[:, c * T:(c + 1) * T], TA[:, c * T:(c + 1) * T]), "s_vd")
            VE(apply2)

            stats(Xb, is_ln01=False)

            def apply3(vector, Xb=Xb):
                v = vector
                for c in range(NC):
                    sig(v.tensor_tensor(TA[:, c * T:(c + 1) * T],
                                        Xb[:, c * T:(c + 1) * T], GS[:, 0:T], OP.subtract), "s_vd")
                v.wait_ge(s_vd, cnt["s_vd"])
                for c in range(NC):
                    sig(v.tensor_mul(U[:, c * T:(c + 1) * T],
                                     TA[:, c * T:(c + 1) * T], BB[:]), "s_vd")
            VE(apply3)

            h_mm_cnt = {}
            h_ev_cnt = {}
            gelu_end = {}
            w1_cnt = {}
            for q in range(4):
                def w1_load(sync, q=q):
                    sync.wait_ge(s_su, startup_x_cnt)
                    if q >= 1:
                        sync.wait_ge(s_mm, h_mm_cnt[(q - 1, 1, 3)])
                    else:
                        sync.wait_ge(s_mm, xlast["w1_user"])
                    for c in range(NC):
                        inst = sync.dma_start(
                            out=W1B[:, c * 1024:(c + 1) * 1024],
                            in_=w1h[128 * c:128 * (c + 1), 1024 * q:1024 * (q + 1)])
                        sig(inst, "s_w1", 16)
                    w1_cnt[q] = cnt["s_w1"]
                SY(w1_load)

                for half in range(2):
                    gb = 2 * q + half

                    for k in range(4):
                        def h_mm_k(tensor, q=q, half=half, k=k):
                            if q == 0 and half == 0 and k == 0:
                                waits(tensor, ["s_vd", "s_sc"])
                            tensor.wait_ge(s_w1, w1_cnt[q])
                            jf = 4 * half + k
                            ph = [psH0, psH1][k % 2]
                            prev = ((q, half, k - 2) if k >= 2 else
                                    ((q, half - 1, k + 2) if half else (q - 1, 1, k + 2)))
                            if prev in h_ev_cnt:
                                tensor.wait_ge(s_sc, h_ev_cnt[prev])
                            for c in range(NC):
                                inst = tensor.matmul(
                                    ph[:], W1B[:, c * 1024 + jf * 128:c * 1024 + (jf + 1) * 128],
                                    U[:, c * T:(c + 1) * T],
                                    start=(c == 0), stop=(c == NC - 1))
                            sig(inst, "s_mm")
                            h_mm_cnt[(q, half, k)] = cnt["s_mm"]
                            xlast["w1_user"] = cnt["s_mm"]
                        TE(h_mm_k)

                        def h_evac_k(scalar, q=q, half=half, gb=gb, k=k):
                            if k == 0 and gb >= 2:
                                scalar.wait_ge(s_vd, gelu_end[gb - 2])
                            Wb = [WQ, WQ2][gb % 2]
                            Zb = [GZ, GZ2][gb % 2]
                            jf = 4 * half + k
                            fg = 8 * q + jf
                            ph = [psH0, psH1][k % 2]
                            scalar.wait_ge(s_mm, h_mm_cnt[(q, half, k)])
                            inst = scalar.activation(Wb[:, k * T:(k + 1) * T], ph[:],
                                                     AF.Identity,
                                                     bias=B1Q[:, fg:fg + 1], scale=256.0)
                            sig(inst, "s_sc")
                            inst = scalar.activation(Zb[:, k * T:(k + 1) * T], ph[:],
                                                     AF.Identity,
                                                     bias=B1Q[:, fg:fg + 1], scale=256.0)
                            sig(inst, "s_sc")
                            h_ev_cnt[(q, half, k)] = cnt["s_sc"]
                        SC(h_evac_k)

                    def gelu(vector, q=q, half=half, gb=gb):
                        Wb = [WQ, WQ2][gb % 2]
                        Zb = [GZ, GZ2][gb % 2]
                        vector.wait_ge(s_sc, h_ev_cnt[(q, half, 3)])
                        fg0 = 8 * q + 4 * half
                        cols = slice(fg0 * T, (fg0 + 4) * T)
                        v = vector
                        vc(v, lambda: v.tensor_tensor(GD[:], Wb[:], Zb[:], OP.subtract))
                        vc(v, lambda: v.scalar_tensor_tensor(GA[:], GD[:], 0.5, Zb[:], OP.is_gt, OP.add))
                        vc(v, lambda: v.tensor_scalar(GS[:], GD[:], -0.5, None, OP.is_lt))
                        vc(v, lambda: v.tensor_tensor(GA[:], GA[:], GS[:], OP.subtract))
                        vc(v, lambda: v.tensor_scalar(GS[:], GA[:], BIAS, 1.0 / 64.0,
                                                      OP.subtract, OP.mult))
                        vc(v, lambda: v.scalar_tensor_tensor(GS[:], GS[:], -1.0, GS[:],
                                                             OP.mult, OP.max))
                        vc(v, lambda: v.tensor_scalar(GD[:], GA[:], BIAS, 1.0 / 512.0,
                                                      OP.subtract, OP.mult))
                        vc(v, lambda: v.scalar_tensor_tensor(GB[:], GS[:], float(LAM0), GD[:],
                                                             OP.mult, OP.add))
                        mbuf = [GD, Wb, GA]
                        ts_c = [None] * 8
                        m_c = [None] * 8
                        acc_c = [None] * 9
                        acc_c[0] = cnt["s_vd"]
                        base = cnt["s_vd"]

                        def e_ts(j):
                            mb = mbuf[j % 3]
                            dep = acc_c[j - 2] if j >= 3 else base
                            vc(v, lambda: v.tensor_scalar(
                                mb[:], GS[:], float(2 * j + 1), float(PAIR_RHO[j]),
                                OP.max, OP.mult), dep=dep)
                            ts_c[j] = cnt["s_vd"]

                        def e_m(j):
                            mb = mbuf[j % 3]
                            vc(v, lambda: v.scalar_tensor_tensor(
                                mb[:], GS[:], float(2 * j + 2), mb[:], OP.max, OP.add),
                                dep=ts_c[j])
                            m_c[j] = cnt["s_vd"]

                        def e_acc(j):
                            mb = mbuf[j % 3]
                            vc(v, lambda: v.scalar_tensor_tensor(
                                GB[:], mb[:], float(PAIR_EB[j]), GB[:], OP.mult, OP.add),
                                dep=max(m_c[j], acc_c[j]))
                            acc_c[j + 1] = cnt["s_vd"]

                        e_ts(0); e_ts(1); e_m(0); e_ts(2); e_m(1); e_acc(0)
                        for j in range(1, 6):
                            e_m(j + 1); e_acc(j); e_ts(j + 2)
                        e_m(7); e_acc(6); e_acc(7)
                        vc(v, lambda: v.tensor_scalar(Wb[:], GB[:], 256.0, float(W2OFF),
                                                      OP.mult, OP.add))
                        vc(v, lambda: v.tensor_copy(Zb[:], Wb[:]))
                        vc(v, lambda: v.tensor_tensor(GD[:], Wb[:], Zb[:], OP.subtract))
                        vc(v, lambda: v.scalar_tensor_tensor(GA[:], GD[:], 0.5, Zb[:], OP.is_gt, OP.add))
                        vc(v, lambda: v.scalar_tensor_tensor(GA[:], GD[:], -0.5, GA[:], OP.is_lt, OP.subtract))
                        vc(v, lambda: v.tensor_scalar(HB[:, cols], GA[:], BIAS, -1.0 / 256.0,
                                                      OP.add, OP.mult))
                        gelu_end[gb] = cnt["s_vd"]
                    VE(gelu)

            o_mm_cnt = {}
            o_ev_cnt = {}
            o_rnd_end = {}
            for Q in range(4):
                def w2_load(sync, Q=Q):
                    sync.wait_ge(s_su, startup_x_cnt)
                    WB = [W2B, W1B][Q % 2]
                    if Q >= 2:
                        sync.wait_ge(s_mm, o_mm_cnt[(Q - 2, 1)])
                    elif Q == 0:
                        sync.wait_ge(s_mm, xlast["w2_user"])
                    else:
                        sync.wait_ge(s_mm, xlast["w1_user"])
                    sync.wait_ge(s_w2, cnt["s_w2"])
                    for j in range(NF):
                        inst = sync.dma_start(
                            out=WB[:, j * 256:(j + 1) * 256],
                            in_=w2h[128 * j:128 * (j + 1), 256 * Q:256 * (Q + 1)])
                        sig(inst, "s_w2", 16)
                    o_mm_cnt[("w2x", Q)] = cnt["s_w2"]
                SY(w2_load)

                def o_mm(tensor, Q=Q):
                    if Q == 0:
                        waits(tensor, ["s_vd", "s_sc"])
                    WB = [W2B, W1B][Q % 2]
                    tensor.wait_ge(s_w2, o_mm_cnt[("w2x", Q)])
                    for dd in range(2):
                        ph = [psH0, psH1][dd]
                        if (Q - 1, dd) in o_ev_cnt:
                            tensor.wait_ge(s_sc, o_ev_cnt[(Q - 1, dd)])
                        for j in range(NF):
                            inst = tensor.matmul(
                                ph[:], WB[:, j * 256 + dd * 128:j * 256 + (dd + 1) * 128],
                                HB[:, j * T:(j + 1) * T],
                                start=(j == 0), stop=(j == NF - 1))
                        sig(inst, "s_mm")
                        o_mm_cnt[(Q, dd)] = cnt["s_mm"]
                        if Q % 2 == 0:
                            xlast["w2_user"] = cnt["s_mm"]
                        else:
                            xlast["w1_user"] = cnt["s_mm"]
                TE(o_mm)

                def o_evac(scalar, Q=Q):
                    if Q >= 2:
                        scalar.wait_ge(s_vd, o_rnd_end[Q - 2])
                    Wb = [WQ, WQ2][Q % 2]
                    for dd in range(2):
                        cq = 2 * Q + dd
                        ph = [psH0, psH1][dd]
                        scalar.wait_ge(s_mm, o_mm_cnt[(Q, dd)])
                        inst = scalar.activation(Wb[:, dd * T:(dd + 1) * T], ph[:],
                                                 AF.Identity,
                                                 bias=B2Q[:, cq:cq + 1], scale=256.0)
                        sig(inst, "s_sc")
                        o_ev_cnt[(Q, dd)] = cnt["s_sc"]
                SC(o_evac)

                def o_round(vector, Q=Q):
                    vector.wait_ge(s_sc, o_ev_cnt[(Q, 1)])
                    Wb = [WQ, WQ2][Q % 2]
                    v = vector
                    rhe(v, Wb, GZ, GD, GS, GA, GB, 2 * T)
                    vc(v, lambda: v.tensor_scalar(TA[:, 2 * Q * T:(2 * Q + 2) * T],
                                                  GA[:, 0:2 * T], BIAS, -1.0 / 256.0,
                                                  OP.add, OP.mult))
                    o_rnd_end[Q] = cnt["s_vd"]
                VE(o_round)

            if i + 1 < NT:
                load_x(i + 1, dep=dict(vd=ap4_end.get(i - 1, 0),
                                       y=store_end.get(i - 1, 0),
                                       sc=st01_cast.get(i - 1, 0)))
                xload_end[i + 1] = cnt["s_x"]

            ln4 = stats(TA, is_ln01=False)
            ln4_mm[i] = ln4["mm"]
            ln4_chain[i] = ln4["chain"]

            def apply4(vector, Xb=Xb):
                v = vector
                for c in range(NC):
                    sig(v.tensor_tensor(TA[:, c * T:(c + 1) * T],
                                        TA[:, c * T:(c + 1) * T], GS[:, 0:T], OP.subtract), "s_vd")
                v.wait_ge(s_vd, cnt["s_vd"])
                for c in range(NC):
                    sig(v.tensor_mul(TA[:, c * T:(c + 1) * T],
                                     TA[:, c * T:(c + 1) * T], BB[:]), "s_vd")
                v.wait_ge(s_vd, cnt["s_vd"])
                for c in range(NC):
                    sig(v.tensor_add(Xb[:, c * T:(c + 1) * T],
                                     Xb[:, c * T:(c + 1) * T], TA[:, c * T:(c + 1) * T]), "s_vd")
            VE(apply4)
            ap4_end[i] = cnt["s_vd"]

            def store(sync, i=i, Xb=Xb):
                waits(sync, ["s_vd"])
                for c in range(NC):
                    inst = sync.dma_start(
                        out=yout[128 * c:128 * (c + 1), T * i:T * (i + 1)],
                        in_=Xb[:, c * T:(c + 1) * T])
                    sig(inst, "s_y", 16)
            SY(store)
            store_end[i] = cnt["s_y"]

        def fin(sync):
            sync.wait_ge(s_y, cnt["s_y"])
        SY(fin)

    return nc



_NC_CACHE = {}


def kernel(L0_attn_residual, L0_attn_w_fc, L0_attn_w_proj, L0_attn_ln_g,
           L0_attn_ln_b, L0_ffn_w1, L0_ffn_b1, L0_ffn_w2, L0_ffn_b2):
    x = np.ascontiguousarray(np.asarray(L0_attn_residual, dtype=np.float32))
    w_fc = np.asarray(L0_attn_w_fc, dtype=np.float32)
    w_proj = np.asarray(L0_attn_w_proj, dtype=np.float32)
    ln_g = np.asarray(L0_attn_ln_g, dtype=np.float32)
    ln_b = np.asarray(L0_attn_ln_b, dtype=np.float32)
    w1 = np.ascontiguousarray(np.asarray(L0_ffn_w1, dtype=np.float32))
    b1 = np.asarray(L0_ffn_b1, dtype=np.float32)
    w2 = np.ascontiguousarray(np.asarray(L0_ffn_w2, dtype=np.float32))
    b2 = np.asarray(L0_ffn_b2, dtype=np.float32)

    wkt = np.ascontiguousarray(w_fc.reshape(FF, 256).T)
    wv = np.ascontiguousarray(np.ascontiguousarray(w_proj.T).reshape(FF, 256))
    lng = np.ascontiguousarray(ln_g.reshape(NC, 128).T)
    lnb = np.ascontiguousarray(ln_b.reshape(NC, 128).T)
    b1t = np.ascontiguousarray(b1.reshape(NF, 128).T)
    b2t = np.ascontiguousarray(b2.reshape(NC, 128).T)

    xf = x.reshape(B * TT, C)
    in_maps = []
    for core in range(N_CORES):
        xs = np.ascontiguousarray(xf[core * TOK:(core + 1) * TOK, :].T)
        in_maps.append(dict(x=xs, wkt=wkt, wv=wv, w1=w1, w2=w2,
                            lng=lng, lnb=lnb, b1t=b1t, b2t=b2t))

    if "nc" not in _NC_CACHE:
        _NC_CACHE["nc"] = build_nc()
    nc = _NC_CACHE["nc"]
    res = run_bass_kernel_spmd(nc, in_maps, core_ids=list(range(N_CORES)))

    out = np.empty((B * TT, C), dtype=np.float32)
    for core in range(N_CORES):
        out[core * TOK:(core + 1) * TOK, :] = res.results[core]["y"].T
    return out.reshape(B, TT, C)
